# version 01
# baseline (speedup 1.0000x reference)
import numpy as np
import ml_dtypes
import concourse.bass as bass
import concourse.mybir as mybir
from concourse.bass_utils import run_bass_kernel_spmd

F32 = mybir.dt.float32
BF16 = mybir.dt.bfloat16
I32 = mybir.dt.int32
AF = mybir.ActivationFunctionType
ALU = mybir.AluOpType
AX = mybir.AxisListType
BF = ml_dtypes.bfloat16

T = 2048
D = 1024
NTT = T // 128
NFT = D // 128
NEG = -1.0e30



def _t(a):
    return np.ascontiguousarray(np.asarray(a).T)


def _pcols(v):
    v = np.asarray(v).reshape(-1)
    n = v.shape[0]
    nt = (n + 127) // 128
    out = np.zeros((nt, 128), v.dtype)
    out.reshape(-1)[:n] = v
    return np.ascontiguousarray(out.T)


def host_consts():
    d = {}
    d["eye"] = np.eye(128, dtype=np.float32)
    d["ones_row"] = np.ones((1, 2048), dtype=np.float32)
    r = np.arange(128)[:, None]
    j = np.arange(128)[None, :]
    d["maskA"] = np.where(j <= (r % 64), 0.0, NEG).astype(np.float32)
    d["maskB"] = np.where(j <= 64 + (r % 64), 0.0, NEG).astype(np.float32)
    d["maskWA"] = np.where(j >= (r % 64), 0.0, NEG).astype(np.float32)
    d["maskWB"] = np.where(j >= 64 + (r % 64), 0.0, NEG).astype(np.float32)
    m32 = np.zeros((32, 64), np.float32)
    r32 = np.arange(32)[:, None]
    c32 = np.arange(32)[None, :]
    m32[:, 0:32] = (c32 < r32).astype(np.float32)
    m32[:, 32:64] = (c32 >= r32).astype(np.float32)
    d["masks32"] = m32
    d["maskD41"] = np.array([[0.0], [0.0], [0.0], [1.0 - 1.0 / 3000.0]],
                            np.float32)
    hs = np.zeros((128, 2), np.float32)
    hs[0:64, 0] = 1.0
    hs[64:128, 1] = 1.0
    d["hsel"] = hs
    d["hselT"] = np.ascontiguousarray(hs.T)
    d["onescol"] = np.ones((128, 1), np.float32)
    return d



class Sem:
    def __init__(self, h):
        self.h = h
        self.count = 0

    def inc(self, instr, n=1):
        instr.then_inc(self.h, n)
        self.count += n
        return self.count

    def wait(self, eng, at=None):
        eng.wait_ge(self.h, self.count if at is None else at)


class Ctx:
    def __init__(self, nc):
        self.nc = nc
        self._stack = []
        self.sems = {}
        self.psum = {}
        self.ar = {}
        self.dram = {}

    def sem(self, name):
        if name not in self.sems:
            h = self._enter(self.nc.semaphore(name))
            self.sems[name] = Sem(h)
        return self.sems[name]

    def _enter(self, cm):
        v = cm.__enter__()
        self._stack.append(cm)
        return v

    def sbuf(self, name, shape, dtype):
        t = self._enter(self.nc.sbuf_tensor("sb_" + name, shape, dtype))
        self.ar[name] = t
        return t

    def psum_t(self, name, shape, dtype):
        t = self._enter(self.nc.psum_tensor(name, shape, dtype))
        self.psum[name] = t
        return t

    def close(self):
        for cm in reversed(self._stack):
            cm.__exit__(None, None, None)

    def barrier(self):
        self.nc.all_engine_barrier()


def vv(ctx, instr):
    sv = ctx.sem("ve")
    sv.inc(instr)
    sv.wait(ctx.nc.vector)
    return instr


def sssync(ctx, instr):
    ss = ctx.sem("se")
    ss.inc(instr)
    ss.wait(ctx.nc.scalar)
    return instr


def vspacer(ctx):
    return ctx.nc.vector.memset(ctx.ar["JNK"][:, 0:1], 0.0)


def sspacer(ctx):
    return ctx.nc.scalar.activation(ctx.ar["JNK"][:, 2:3],
                                    ctx.ar["JNK"][:, 1:2], AF.Identity)


def mm_acc(ctx, out_ap, pairs, sem=None, extra=()):
    nc = ctx.nc
    allp = list(pairs) + list(extra)
    n = len(allp)
    last = None
    for i, p in enumerate(allp):
        tp = p[2] if len(p) > 2 else None
        last = nc.tensor.matmul(out_ap, p[0], p[1], start=(i == 0),
                                stop=(i == n - 1), tile_position=tp)
    if sem is not None:
        sem.inc(last)
    return last


class WRing:

    def __init__(self, ctx, arena="W", nslots=4, cols=256, kt=8):
        self.ctx = ctx
        self.nslots = nslots
        self.cols = cols
        self.kt = kt
        self.W = ctx.ar[arena]
        self.sem = ctx.sem("wld")
        self.free_toks = [None] * nslots
        self.i = 0

    def load(self, hbm_ap):
        ctx, nc = self.ctx, self.ctx.nc
        slot = self.i % self.nslots
        self.i += 1
        base = slot * self.kt * self.cols
        ncols = hbm_ap.shape[-1]
        kt = hbm_ap.shape[1]
        assert kt * ncols <= self.kt * self.cols
        dst = self.W[:, base:base + kt * ncols].rearrange("p (a c) -> p a c", a=kt)
        ft = self.free_toks[slot]
        if ft is not None:
            nc.gpsimd.wait_ge(ft[0].h, ft[1])
        tok = self.sem.inc(nc.gpsimd.dma_start(out=dst, in_=hbm_ap), 16)

        def view(k, c0=0, c1=None):
            c1 = ncols if c1 is None else c1
            return self.W[:, base + k * ncols + c0: base + k * ncols + c1]
        return view, tok, slot

    def mark_free(self, slot, sem, count):
        self.free_toks[slot] = (sem, count)


def hbm_slab(ctx, name, c0, c1, a0=0, a1=None):
    t = ctx.dram[name]
    r = t.rearrange("(a p) c -> p a c", p=128)
    if a1 is None:
        a1 = r.shape[1]
    return r[:, a0:a1, c0:c1]



def emit_ln(ctx, x_tensor, out_tensor, n_tiles=NTT, width=D, eps_col=240, c0=0):
    nc = ctx.nc
    sv, ss = ctx.sem("ve"), ctx.sem("se")
    nt = n_tiles
    sums = ctx.ar["lnsum"][:, c0:c0 + nt]
    ssqs = ctx.ar["lnssq"][:, c0:c0 + nt]
    scl = ctx.ar["lnscl"][:, c0:c0 + nt]
    bia = ctx.ar["lnbia"][:, c0:c0 + nt]
    tmpA = ctx.ar["lntmpA"][:, c0:c0 + nt]
    tmpB = ctx.ar["lntmpB"][:, c0:c0 + nt]
    lastq = lastr = None
    for t in range(nt):
        xt = x_tensor[:, t * width:(t + 1) * width]
        lastq = nc.scalar.activation(out_tensor[:, t * width:(t + 1) * width], xt,
                                     AF.Square, accum_out=ssqs[:, t:t + 1])
        lastr = nc.vector.tensor_reduce(sums[:, t:t + 1], xt, AX.X, ALU.add)
    sv.inc(lastr)
    ss.inc(lastq)
    sv.wait(nc.vector)
    ss.wait(nc.vector)
    vv(ctx, nc.vector.tensor_tensor(tmpA[:], sums[:], sums[:], ALU.mult))
    vv(ctx, nc.vector.tensor_scalar(ssqs[:], ssqs[:], 1.0 / width, None, ALU.mult))
    vv(ctx, nc.vector.tensor_scalar(tmpA[:], tmpA[:], 1.0 / (width * width),
                                    None, ALU.mult))
    vv(ctx, nc.vector.tensor_scalar(sums[:], sums[:], 1.0 / width, None, ALU.mult))
    sv.inc(nc.vector.tensor_tensor(tmpB[:], ssqs[:], tmpA[:], ALU.subtract))
    sv.wait(nc.vector)
    sv.wait(nc.scalar)
    sssync(ctx, nc.scalar.activation(tmpB[:], tmpB[:], AF.Ln,
                                     bias=ctx.ar["V32"][:, eps_col:eps_col + 1]))
    ss.inc(nc.scalar.activation(scl[:], tmpB[:], AF.Exp, scale=-0.5))
    ss.wait(nc.vector)
    sv.inc(nc.vector.scalar_tensor_tensor(bia[:], sums[:], -1.0, scl[:],
                                          ALU.mult, ALU.mult))
    sv.wait(nc.vector)
    sv.wait(nc.scalar)
    last = None
    for t in range(nt):
        xt = x_tensor[:, t * width:(t + 1) * width]
        last = nc.scalar.activation(out_tensor[:, t * width:(t + 1) * width], xt,
                                    AF.Identity, scale=scl[:, t:t + 1],
                                    bias=bia[:, t:t + 1])
    ss.inc(last)
    return scl, bia


def emit_transpose(ctx, src, dst, n_tok_tiles=NTT, n_feat_tiles=NFT,
                   src_width=None, dst_row=None, shift=None):
    nc = ctx.nc
    if src_width is None:
        src_width = n_feat_tiles * 128
    if dst_row is None:
        dst_row = n_tok_tiles * 128
    sm, sv = ctx.sem("mm"), ctx.sem("ve")
    eye = ctx.ar["eye_bf"]
    pts = [ctx.psum["ptA"], ctx.psum["ptB"]]
    ctx.sems["se"].wait(nc.tensor)
    ctx.sems["ve"].wait(nc.tensor)
    evac_tok = {}
    gi = 0
    for o in range(n_feat_tiles):
        for half in range(0, n_tok_tiles, 8):
            pt = pts[gi % 2]
            nh = min(8, n_tok_tiles - half)
            if gi - 2 in evac_tok:
                ctx.sems["ve"].wait(nc.tensor, at=evac_tok[gi - 2])
            lastt = None
            for j in range(nh):
                t = half + j
                lastt = nc.tensor.transpose(
                    pt[:, j * 128:(j + 1) * 128],
                    src[:, t * src_width + o * 128: t * src_width + o * 128 + 128],
                    eye[:])
            sm.inc(lastt)
            sm.wait(nc.vector)
            ev = nc.vector.tensor_copy(
                dst[:, o * dst_row + half * 128: o * dst_row + (half + nh) * 128],
                pt[:, 0:nh * 128])
            evac_tok[gi] = sv.inc(ev)
            gi += 1
    return sv.count



def emit_mha(ctx, cfg):
    nc = ctx.nc
    X, A, B, C, QS, PS = (ctx.ar[k] for k in ("X", "A", "B", "C", "QS", "PS"))
    S32, KB = ctx.ar["S32"], ctx.ar["KB"]
    sv, ss, sm, sw = ctx.sem("ve"), ctx.sem("se"), ctx.sem("mm"), ctx.sem("wld")
    causal = cfg["causal"]
    scale = cfg["scale"]
    has_bias = "bq" in cfg
    ones = ctx.ar["ones_bf"]
    eyebf = ctx.ar["eye_bf"]
    maskA, maskB = ctx.ar["maskA"], ctx.ar["maskB"]
    pk, pk2 = ctx.psum["pk"], ctx.psum["pk2"]
    ps_sc = [ctx.psum["ps0"], ctx.psum["ps1"], ctx.psum["ps2"], ctx.psum["ps3"]]

    stop_after = cfg.get("stop_after")
    emit_ln(ctx, X, A)
    if stop_after == "ln":
        return
    emit_transpose(ctx, A, B)
    tr_done = sv.count
    if stop_after == "tr":
        return

    ring = WRing(ctx, nslots=3, cols=256)

    def qkv_ap(which, c0, c1):
        if "wqkv" in cfg:
            return hbm_slab(ctx, cfg["wqkv"], which * D + c0, which * D + c1)
        return hbm_slab(ctx, cfg[("wq", "wk", "wv")[which]], c0, c1)

    if has_bias:
        for i, bn in enumerate(("bq", "bk", "bv", "bo")):
            sw.inc(nc.gpsimd.dma_start(
                out=KB[0:1, 512 + i * 1024:512 + (i + 1) * 1024],
                in_=ctx.dram[cfg[bn]][:]), 16)

    def brow(i, c0, c1):
        return KB[0:1, 512 + i * 1024 + c0:512 + i * 1024 + c1]

    evt = []

    def proj_guard(pid):
        for q, tk in evt[-2:]:
            if q == pid:
                ctx.sems["ve"].wait(nc.tensor, at=tk)

    ctx.sems["ve"].wait(nc.tensor, at=tr_done)
    it = 0
    import os as _os
    _kn = int(_os.environ.get("MHA_KN", "8"))
    for o in range(_kn):
        view, tok, slot = ring.load(qkv_ap(1, o * 128, (o + 1) * 128))
        for ts in range(4):
            pb, pid = (pk, 0) if it % 2 == 0 else (pk2, 1)
            it += 1
            nc.tensor.wait_ge(sw.h, tok)
            proj_guard(pid)
            extra = [(brow(1, o * 128, (o + 1) * 128), ones[0:1, 0:512])] \
                if has_bias else []
            mm_acc(ctx, pb[:, 0:512],
                   [(view(k, 0, 128), B[:, k * T + ts * 512:k * T + (ts + 1) * 512])
                    for k in range(8)], sem=sm, extra=extra)
            sm.wait(nc.vector)
            ev = nc.vector.tensor_copy(C[:, o * T + ts * 512:o * T + (ts + 1) * 512],
                                       pb[:, 0:512])
            evt.append((pid, sv.inc(ev)))
        ring.mark_free(slot, sv, sv.count)

    for qtr in range(4):
        view, tok, slot = ring.load(qkv_ap(2, qtr * 256, (qtr + 1) * 256))
        for t in range(NTT):
            pb, pid = (pk, 0) if it % 2 == 0 else (pk2, 1)
            it += 1
            nc.tensor.wait_ge(sw.h, tok)
            proj_guard(pid)
            extra = [(ones[0:1, 0:128], brow(2, qtr * 256, (qtr + 1) * 256))] \
                if has_bias else []
            mm_acc(ctx, pb[:, 0:256],
                   [(B[:, k * T + t * 128:k * T + (t + 1) * 128], view(k, 0, 256))
                    for k in range(8)], sem=sm, extra=extra)
            sm.wait(nc.vector)
            ev = nc.vector.tensor_copy(
                A[:, t * D + qtr * 256:t * D + (qtr + 1) * 256], pb[:, 0:256])
            evt.append((pid, sv.inc(ev)))
        ring.mark_free(slot, sv, sv.count)

    if stop_after == "v":
        return
    for g in range(4):
        qs_toks = []
        sv.wait(nc.tensor)
        for o in range(8):
            view, tok, slot = ring.load(qkv_ap(0, o * 128, (o + 1) * 128))
            pb, pid = (pk, 0) if it % 2 == 0 else (pk2, 1)
            it += 1
            nc.tensor.wait_ge(sw.h, tok)
            for q, tk in qs_toks[-2:]:
                if q == pid:
                    ctx.sems["se"].wait(nc.tensor, at=tk)
            extra = [(brow(0, o * 128, (o + 1) * 128), ones[0:1, 0:512])] \
                if has_bias else []
            mm_acc(ctx, pb[:, 0:512],
                   [(view(k, 0, 128), B[:, k * T + g * 512:k * T + (g + 1) * 512])
                    for k in range(8)], sem=sm, extra=extra)
            sm.wait(nc.scalar)
            ev = nc.scalar.activation(QS[:, o * 512:(o + 1) * 512], pb[:, 0:512],
                                      AF.Identity, scale=scale)
            qs_toks.append((pid, ss.inc(ev)))
            ring.mark_free(slot, ss, ss.count)
        ss.wait(nc.tensor)
        if stop_after == "q":
            continue

        for p in range(8):
            for c in range(8):
                qbase = g * 512 + c * 64
                kv_end = (qbase + 64) if causal else T
                nkv = (kv_end + 127) // 128
                nslab = (nkv + 3) // 4
                sv.wait(nc.tensor)
                lastmm = None
                for s in range(nslab):
                    ncols = min(512, nkv * 128 - s * 512)
                    qa = QS[0:64, p * 512 + c * 64:p * 512 + (c + 1) * 64]
                    qb = QS[64:128, p * 512 + c * 64:p * 512 + (c + 1) * 64]
                    ka = C[0:64, p * T + s * 512:p * T + s * 512 + ncols]
                    kb = C[64:128, p * T + s * 512:p * T + s * 512 + ncols]
                    nc.tensor.matmul(ps_sc[s][0:64, 0:ncols], qa, ka,
                                     start=True, stop=True, tile_position=(0, 0))
                    lastmm = nc.tensor.matmul(ps_sc[s][64:128, 0:ncols], qb, kb,
                                              start=True, stop=True,
                                              tile_position=(64, 64))
                sm.inc(lastmm)
                sm.wait(nc.vector)
                if causal:
                    dt = nkv - 1
                    sl, off = dt // 4, (dt % 4) * 128
                    mk = maskA if (qbase % 128) == 0 else maskB
                    nc.vector.tensor_tensor(ps_sc[sl][:, off:off + 128],
                                            ps_sc[sl][:, off:off + 128],
                                            mk[:], ALU.add)
                    vspacer(ctx)
                nm = S32[:, 0:4]
                lastv = None
                for s in range(nslab):
                    ncols = min(512, nkv * 128 - s * 512)
                    lastv = nc.vector.tensor_reduce(nm[:, s:s + 1],
                                                    ps_sc[s][:, 0:ncols],
                                                    AX.X, ALU.max, negate=True)
                sv.inc(lastv)
                sv.wait(nc.vector)
                lastv = nc.vector.tensor_reduce(S32[:, 8:9], nm[:, 0:nslab], AX.X,
                                                ALU.min)
                sv.inc(lastv)
                sv.wait(nc.scalar)
                sm.wait(nc.scalar)
                laste = None
                for s in range(nslab):
                    ncols = min(512, nkv * 128 - s * 512)
                    laste = nc.scalar.activation(
                        PS[:, s * 512:s * 512 + ncols], ps_sc[s][:, 0:ncols],
                        AF.Exp, bias=S32[:, 8:9], accum_out=S32[:, 12 + s:13 + s])
                ss.inc(laste)
                ss.wait(nc.vector)
                vv(ctx, nc.vector.tensor_reduce(S32[:, 9:10], S32[:, 12:12 + nslab],
                                                AX.X, ALU.add))
                vv(ctx, nc.vector.reciprocal(S32[:, 11:12], S32[:, 9:10]))
                lastd = nc.vector.tensor_scalar(PS[:, 2048:2176], eyebf[:],
                                                S32[:, 11:12], None, ALU.mult)
                sv.inc(lastd)
                ss.wait(nc.tensor)
                sv.wait(nc.tensor)
                lastt = None
                for kt in range(nkv):
                    lastt = nc.tensor.matmul(
                        ps_sc[kt // 4][:, (kt % 4) * 128:(kt % 4) * 128 + 128],
                        PS[:, kt * 128:(kt + 1) * 128], PS[:, 2048:2176],
                        start=True, stop=True)
                sm.inc(lastt)
                sm.wait(nc.vector)
                lastev = None
                for s in range(nslab):
                    ncols = min(512, nkv * 128 - s * 512)
                    lastev = nc.vector.tensor_copy(
                        PS[:, 2176 + s * 512:2176 + s * 512 + ncols],
                        ps_sc[s][:, 0:ncols])
                sv.inc(lastev)
                sv.wait(nc.tensor)
                lastpv = None
                for kt in range(nkv):
                    va = A[:, kt * D + p * 128:kt * D + p * 128 + 64]
                    vb = A[:, kt * D + p * 128 + 64:kt * D + p * 128 + 128]
                    nc.tensor.matmul(pk[0:64, 0:64], va,
                                     PS[:, 2176 + kt * 128:2176 + kt * 128 + 64],
                                     start=(kt == 0), stop=(kt == nkv - 1),
                                     tile_position=(0, 0), skip_group_check=True)
                    lastpv = nc.tensor.matmul(
                        pk[64:128, 0:64], vb,
                        PS[:, 2176 + kt * 128 + 64:2176 + kt * 128 + 128],
                        start=(kt == 0), stop=(kt == nkv - 1),
                        tile_position=(0, 64), skip_group_check=True)
                sm.inc(lastpv)
                sm.wait(nc.vector)
                ev = nc.vector.tensor_copy(B[:, p * T + qbase:p * T + qbase + 64],
                                           pk[:, 0:64])
                sv.inc(ev)

    if stop_after in ("q", "attn"):
        return
    evt = []
    it = 0
    for qtr in range(4):
        view, tok, slot = ring.load(hbm_slab(ctx, cfg["wo"], qtr * 256,
                                             (qtr + 1) * 256))
        for t in range(NTT):
            pb, pid = (pk2, 1) if it % 2 == 0 else (pk, 0)
            it += 1
            nc.tensor.wait_ge(sw.h, tok)
            if it == 1:
                sv.wait(nc.tensor)
            proj_guard(pid)
            extra = [(ones[0:1, 0:128], brow(3, qtr * 256, (qtr + 1) * 256))] \
                if has_bias else []
            mm_acc(ctx, pb[:, 0:256],
                   [(B[:, k * T + t * 128:k * T + (t + 1) * 128], view(k, 0, 256))
                    for k in range(8)], sem=sm, extra=extra)
            sm.wait(nc.vector)
            ev = nc.vector.tensor_tensor(
                X[:, t * D + qtr * 256:t * D + (qtr + 1) * 256],
                X[:, t * D + qtr * 256:t * D + (qtr + 1) * 256],
                pb[:, 0:256], ALU.add)
            evt.append((pid, sv.inc(ev)))
        ring.mark_free(slot, sv, sv.count)



def emit_l2f(ctx):
    nc = ctx.nc
    X, A, B, C, QS = (ctx.ar[k] for k in ("X", "A", "B", "C", "QS"))
    sv, ss, sm, sw = ctx.sem("ve"), ctx.sem("se"), ctx.sem("mm"), ctx.sem("wld")
    pk, pk2 = ctx.psum["pk"], ctx.psum["pk2"]
    S = 1.0 / 127.0

    emit_ln(ctx, X, A)
    emit_transpose(ctx, A, B)
    tr_done = sv.count
    ctx.sems["ve"].wait(nc.tensor, at=tr_done)

    ring = WRing(ctx, nslots=3, cols=256)
    for hc in range(3):
        last_stt = 0
        for ph in range(4):
            c0 = hc * 1024 + ph * 256
            vg, tg, sg_ = ring.load(hbm_slab(ctx, "l2f_wg", c0, c0 + 256))
            vu, tu, su_ = ring.load(hbm_slab(ctx, "l2f_wu", c0, c0 + 256))
            for so in range(2):
                ho = ph * 2 + so
                for ts in range(4):
                    nc.tensor.wait_ge(sw.h, tg)
                    ctx.sems["se"].wait(nc.tensor)
                    mm_acc(ctx, pk[:, 0:512],
                           [(vg(k, so * 128, so * 128 + 128),
                             B[:, k * T + ts * 512:k * T + (ts + 1) * 512])
                            for k in range(8)], sem=sm)
                    sm.wait(nc.scalar)
                    ctx.sems["ve"].wait(nc.scalar, at=last_stt)
                    ss.inc(nc.scalar.activation(QS[:, 0:512], pk[:, 0:512],
                                                AF.Silu, scale=S))
                    nc.tensor.wait_ge(sw.h, tu)
                    ctx.sems["ve"].wait(nc.tensor)
                    mm_acc(ctx, pk2[:, 0:512],
                           [(vu(k, so * 128, so * 128 + 128),
                             B[:, k * T + ts * 512:k * T + (ts + 1) * 512])
                            for k in range(8)], sem=sm)
                    sm.wait(nc.vector)
                    ss.wait(nc.vector)
                    ev = nc.vector.scalar_tensor_tensor(
                        C[:, ho * T + ts * 512:ho * T + (ts + 1) * 512],
                        pk2[:, 0:512], S, QS[:, 0:512], ALU.mult, ALU.mult)
                    last_stt = sv.inc(ev)
            ring.mark_free(sg_, sv, sv.count)
            ring.mark_free(su_, sv, sv.count)
        dn_toks = []
        it = 0
        for qtr in range(4):
            vd, td, sd_ = ring.load(hbm_slab(ctx, "l2f_wd", qtr * 256,
                                             (qtr + 1) * 256, a0=8 * hc,
                                             a1=8 * hc + 8))
            for t in range(NTT):
                pb, pid = (pk, 0) if it % 2 == 0 else (pk2, 1)
                it += 1
                nc.tensor.wait_ge(sw.h, td)
                if it <= 2:
                    sv.wait(nc.tensor)
                    ctx.sems["se"].wait(nc.tensor)
                for q, tk in dn_toks[-2:]:
                    if q == pid:
                        ctx.sems["ve"].wait(nc.tensor, at=tk)
                mm_acc(ctx, pb[:, 0:256],
                       [(C[:, k * T + t * 128:k * T + (t + 1) * 128],
                         vd(k, 0, 256)) for k in range(8)], sem=sm)
                sm.wait(nc.vector)
                ev = nc.vector.scalar_tensor_tensor(
                    X[:, t * D + qtr * 256:t * D + (qtr + 1) * 256],
                    pb[:, 0:256], S,
                    X[:, t * D + qtr * 256:t * D + (qtr + 1) * 256],
                    ALU.mult, ALU.add)
                dn_toks.append((pid, sv.inc(ev)))
            ring.mark_free(sd_, sv, sv.count)



def emit_l1f(ctx):
    nc = ctx.nc
    X, A, B, C, QS, K32 = (ctx.ar[k] for k in ("X", "A", "B", "C", "QS", "K32"))
    sv, ss, sm, sw = ctx.sem("ve"), ctx.sem("se"), ctx.sem("mm"), ctx.sem("wld")
    sp = ctx.sem("spill")
    pk, pk2 = ctx.psum["pk"], ctx.psum["pk2"]
    scr = ctx.dram["scr_h"]

    emit_ln(ctx, X, A)
    emit_transpose(ctx, A, B)
    ctx.sems["ve"].wait(nc.tensor)

    sld = ctx.sem("ldx")
    for i, nm in enumerate(("l1f_s1", "l1f_s2", "l1f_sh1", "l1f_sh2")):
        sld.inc(nc.sync.dma_start(out=K32[:, i * 32:(i + 1) * 32],
                                  in_=ctx.dram[nm][:]), 16)
    sld.inc(nc.sync.dma_start(out=ctx.ar["S32"][:, 0:128],
                              in_=ctx.dram["l1f_s3b"][:, 0:128]), 16)
    sld.wait(nc.vector)
    sld.wait(nc.scalar)

    ring = WRing(ctx, nslots=3, cols=256)
    sc = lambda i, o: K32[:, i * 32 + o:i * 32 + o + 1]

    last_stt = 0
    last_spill = [0, 0]
    for ob in range(16):
        v1, t1, s1_ = ring.load(hbm_slab(ctx, "l1f_w1", ob * 256, (ob + 1) * 256))
        v2, t2, s2_ = ring.load(hbm_slab(ctx, "l1f_w2", ob * 256, (ob + 1) * 256))
        buf = 2 + (ob % 2) * 2
        for so in range(2):
            go = ob * 2 + so
            for ts in range(4):
                nc.tensor.wait_ge(sw.h, t1)
                ctx.sems["se"].wait(nc.tensor)
                mm_acc(ctx, pk[:, 0:512],
                       [(v1(k, so * 128, so * 128 + 128),
                         B[:, k * T + ts * 512:k * T + (ts + 1) * 512])
                        for k in range(8)], sem=sm)
                sm.wait(nc.scalar)
                ctx.sems["ve"].wait(nc.scalar, at=last_stt)
                ss.inc(nc.scalar.activation(
                    C[:, so * T + ts * 512:so * T + (ts + 1) * 512],
                    pk[:, 0:512], AF.Silu, scale=sc(0, go)))
                nc.tensor.wait_ge(sw.h, t2)
                ctx.sems["ve"].wait(nc.tensor)
                mm_acc(ctx, pk2[:, 0:512],
                       [(v2(k, so * 128, so * 128 + 128),
                         B[:, k * T + ts * 512:k * T + (ts + 1) * 512])
                        for k in range(8)], sem=sm)
                sm.wait(nc.vector)
                ss.wait(nc.vector)
                ctx.sems["spill"].wait(nc.vector, at=last_spill[ob % 2])
                ev = nc.vector.scalar_tensor_tensor(
                    C[:, (buf + so) * T + ts * 512:(buf + so) * T + (ts + 1) * 512],
                    pk2[:, 0:512], sc(1, go),
                    C[:, so * T + ts * 512:so * T + (ts + 1) * 512],
                    ALU.mult, ALU.mult)
                last_stt = sv.inc(ev)
        ring.mark_free(s1_, sv, sv.count)
        ring.mark_free(s2_, sv, sv.count)
        sv.wait(nc.sync)
        last_spill[ob % 2] = sp.inc(nc.sync.dma_start(
            out=scr.rearrange("(a p) c -> p a c", p=128)[:, 2 * ob:2 * ob + 2, :],
            in_=C[:, buf * T:(buf + 2) * T].rearrange("p (a c) -> p a c", a=2)), 16)

    sldh = ctx.sem("ldh")
    for ch in range(4):
        c0 = ch * 512
        sp.wait(nc.sync)
        if ch == 0:
            ctx.sems["ve"].wait(nc.sync)
        ldtok = sldh.inc(nc.sync.dma_start(
            out=A[:].rearrange("p (a c) -> p a c", a=32),
            in_=scr.rearrange("(a p) c -> p a c", p=128)[:, :, c0:c0 + 512]), 16)

        def big_mm(dst, dst_w, src, src_w, wname, scol, func):
            toks = []
            for o in range(32):
                va, ta, sa_ = ring.load(hbm_slab(ctx, wname, o * 128,
                                                 (o + 1) * 128, a0=0, a1=16))
                vb, tb, sb_ = ring.load(hbm_slab(ctx, wname, o * 128,
                                                 (o + 1) * 128, a0=16, a1=32))
                pb, pid = (pk, 0) if o % 2 == 0 else (pk2, 1)
                nc.tensor.wait_ge(sw.h, tb)
                nc.tensor.wait_ge(sldh.h, ldtok)
                for q, tk in toks[-2:]:
                    if q == pid:
                        ctx.sems["se"].wait(nc.tensor, at=tk)
                pairs = [(va(k), src[:, k * src_w:k * src_w + 512]) for k in range(16)] +                         [(vb(k), src[:, (16 + k) * src_w:(16 + k) * src_w + 512]) for k in range(16)]
                mm_acc(ctx, pb[:, 0:512], pairs, sem=sm)
                sm.wait(nc.scalar)
                ev = nc.scalar.activation(dst[:, o * dst_w:o * dst_w + 512],
                                          pb[:, 0:512], func, scale=scol(o))
                toks.append((pid, ss.inc(ev)))
                ring.mark_free(sa_, ss, ss.count)
                ring.mark_free(sb_, ss, ss.count)

        big_mm(C, 512, A, 512, "l1f_h1", lambda o: sc(2, o), AF.Gelu)
        ss.wait(nc.tensor)
        big_mm(A, 512, C, 512, "l1f_h2", lambda o: sc(3, o), AF.Gelu)
        ss.wait(nc.tensor)
        ytoks = []
        it = 0
        for oq in range(8):
            vw, tw, sw_ = ring.load(hbm_slab(ctx, "l1f_w3", oq * 128,
                                             (oq + 1) * 128, a0=0, a1=16))
            vw2, tw2, sw2_ = ring.load(hbm_slab(ctx, "l1f_w3", oq * 128,
                                                (oq + 1) * 128, a0=16, a1=32))
            if len(ytoks) >= 8:
                nc.sync.wait_ge(sv.h, ytoks[-8][1] if False else ytoks[-5][1])
            s3tok = sld.inc(nc.sync.dma_start(
                out=ctx.ar["S32"][:, (oq % 2) * 128:(oq % 2) * 128 + 128],
                in_=ctx.dram["l1f_s3b"][:, oq * 128:(oq + 1) * 128]), 16)
            for tt in range(4):
                pb, pid = (pk, 0) if it % 2 == 0 else (pk2, 1)
                it += 1
                nc.tensor.wait_ge(sw.h, tw2)
                for q, tk in ytoks[-2:]:
                    if q == pid:
                        ctx.sems["ve"].wait(nc.tensor, at=tk)
                pairs = [(A[:, k * 512 + tt * 128:k * 512 + tt * 128 + 128],
                          vw(k)) for k in range(16)] +                         [(A[:, (16 + k) * 512 + tt * 128:(16 + k) * 512 + tt * 128 + 128],
                          vw2(k)) for k in range(16)]
                mm_acc(ctx, pb[:, 0:128], pairs, sem=sm)
                sm.wait(nc.vector)
                sld.wait(nc.vector, at=s3tok)
                gt = (c0 + tt * 128)
                xsl = X[:, (gt // 128) * D + oq * 128:(gt // 128) * D + (oq + 1) * 128]
                nc.vector.tensor_tensor(ctx.ar["F128"][:],
                                        pb[:, 0:128],
                                        ctx.ar["S32"][:, (oq % 2) * 128:(oq % 2) * 128 + 128],
                                        ALU.mult)
                ev = nc.vector.tensor_tensor(xsl, xsl, ctx.ar["F128"][:], ALU.add)
                ytoks.append((pid, sv.inc(ev)))
            ring.mark_free(sw_, sv, sv.count)
            ring.mark_free(sw2_, sv, sv.count)
        sv.wait(nc.sync)
        sv.wait(nc.tensor)
        ss.wait(nc.tensor)



def emit_l3f(ctx):
    nc = ctx.nc
    X, A, B, C, QS, S32, KB, F128 = (ctx.ar[k] for k in
                                     ("X", "A", "B", "C", "QS", "S32", "KB", "F128"))
    V32 = ctx.ar["V32"]
    sv, ss, sm, sw = ctx.sem("ve"), ctx.sem("se"), ctx.sem("mm"), ctx.sem("wld")
    sld = ctx.sem("ldx")
    pk, pk2 = ctx.psum["pk"], ctx.psum["pk2"]
    ptA = ctx.psum["ptA"]
    eyebf = ctx.ar["eye_bf"]
    ones = ctx.ar["ones_bf"]

    emit_ln(ctx, X, A)
    emit_transpose(ctx, A, B)
    ctx.sems["ve"].wait(nc.tensor)

    sw.inc(nc.gpsimd.dma_start(
        out=KB[:, 0:48].rearrange("p (a c) -> p a c", a=8),
        in_=hbm_slab(ctx, "l3f_proj", 0, 6)), 16)
    K32 = ctx.ar["K32"]
    sld.inc(nc.sync.dma_start(out=K32[0:6, 128:134], in_=ctx.dram["l3f_anchT"][:]), 16)
    sld.inc(nc.sync.dma_start(out=K32[:, 256:384], in_=ctx.dram["eye"][:]), 16)
    sld.inc(nc.sync.dma_start(out=V32[:, 245:246], in_=ctx.dram["l3f_temp"][:]), 16)
    sld.wait(nc.vector)
    vv(ctx, nc.vector.tensor_scalar(V32[:, 246:247], V32[:, 245:246], 0.1, None,
                                    ALU.max))
    vv(ctx, nc.vector.reciprocal(V32[:, 247:248], V32[:, 246:247]))
    sv.inc(nc.vector.tensor_scalar(F128[:, 10:11], V32[:, 247:248], 0.5, None,
                                   ALU.mult))
    sv.wait(nc.vector)
    sv.wait(nc.tensor)
    sw.wait(nc.tensor)

    ss.wait(nc.tensor)
    for t in range(NTT):
        ss.wait(nc.tensor)
        import os as _os2
        _dbg3 = _os2.environ.get("L3F_DEBUG") == "3"
        if _dbg3:
            sv.wait(nc.tensor)
        mm_acc(ctx, pk[:, 0:6],
               [(B[:, k * T + t * 128:k * T + (t + 1) * 128],
                 KB[:, k * 6:(k + 1) * 6]) for k in range(8)], sem=sm)
        if _dbg3:
            sm.wait(nc.vector)
            sv.inc(nc.vector.tensor_copy(K32[:, 640 + t * 6:646 + t * 6],
                                         pk[:, 0:6]))
        sm.wait(nc.scalar)
        ss.inc(nc.scalar.activation(F128[:, 64:70], pk[:, 0:6], AF.Tanh,
                                    scale=1.0 / 0.3))
        ss.wait(nc.vector)
        nc.vector.memset(F128[:, 70:72], 0.0)
        sv.inc(vspacer(ctx))
        ss.wait(nc.tensor)
        sv.wait(nc.tensor)
        sm.inc(nc.tensor.transpose(ctx.psum["ps0"][0:8, 0:128], F128[:, 64:72],
                                   K32[:, 256:384]))
        sm.wait(nc.vector)
        sv.inc(nc.vector.tensor_copy(K32[0:8, 0:128], ctx.psum["ps0"][0:8, 0:128]))
        sv.wait(nc.tensor)
        mm_acc(ctx, pk2[:, 0:6],
               [(K32[0:6, 0:128], K32[0:6, 128:134])], sem=sm)
        sm.wait(nc.vector)
        if _dbg3:
            nc.vector.tensor_copy(S32[:, 96 + t * 6:102 + t * 6], pk2[:, 0:6])
            sv.inc(nc.vector.tensor_copy(K32[:, 512 + t * 6:518 + t * 6],
                                         F128[:, 64:70]))
        d6 = pk2[:, 0:6]
        m1 = F128[:, 8:9]
        vv(ctx, nc.vector.tensor_reduce(m1, d6, AX.X, ALU.max))
        sv.inc(nc.vector.scalar_tensor_tensor(F128[:, 11:12], m1, -1.0,
                                              F128[:, 10:11], ALU.mult, ALU.mult))
        sv.wait(nc.vector)
        sv.wait(nc.scalar)
        ss.inc(nc.scalar.activation(F128[:, 40:46], d6, AF.Exp,
                                    scale=F128[:, 10:11], bias=F128[:, 11:12]))
        ss.wait(nc.vector)
        E = F128[:, 40:46]
        vv(ctx, nc.vector.tensor_reduce(F128[:, 15:16], E, AX.X, ALU.max))
        vv(ctx, nc.vector.tensor_scalar(F128[:, 16:22], E, F128[:, 15:16], None,
                                        ALU.is_equal))
        vv(ctx, nc.vector.scalar_tensor_tensor(F128[:, 24:30], F128[:, 16:22],
                                               NEG, E, ALU.mult, ALU.add))
        vv(ctx, nc.vector.tensor_reduce(F128[:, 9:10], F128[:, 24:30], AX.X,
                                        ALU.max))
        vv(ctx, nc.vector.tensor_scalar(F128[:, 13:14], F128[:, 9:10], 1.0,
                                        None, ALU.add))
        vv(ctx, nc.vector.reciprocal(F128[:, 14:15], F128[:, 13:14]))
        vv(ctx, nc.vector.tensor_scalar(F128[:, 48:54], E, F128[:, 9:10], None,
                                        ALU.is_ge))
        lw = nc.vector.scalar_tensor_tensor(S32[:, t * 6:t * 6 + 6], E,
                                            F128[:, 14:15], F128[:, 48:54],
                                            ALU.mult, ALU.mult)
        sv.inc(lw)
        sv.wait(nc.vector)
        sv.wait(nc.tensor)
    import os as _os
    if _os.environ.get("L3F_DEBUG"):
        dbg = ctx.dram["l3f_dbg"]
        sdb = ctx.sem("st")
        sv.wait(nc.sync)
        sdb.inc(nc.sync.dma_start(out=dbg[:, 0:96], in_=S32[:, 0:96]), 16)
        sdb.inc(nc.sync.dma_start(out=dbg[:, 96:224], in_=F128[:, 0:128]), 16)
        sdb.wait(nc.sync)

    import os as _os
    _ne = int(_os.environ.get("L3F_NE", "6"))
    ring = WRing(ctx, nslots=3, cols=256)
    for e in range(_ne):
        etoks = []
        for o in range(4):
            vw, tw, sl_ = ring.load(hbm_slab3(ctx, "l3f_w1", e, o * 128,
                                              (o + 1) * 128))
            for ts in range(4):
                nc.tensor.wait_ge(sw.h, tw)
                for q, tk in etoks[-2:]:
                    if q == 0:
                        ctx.sems["se"].wait(nc.tensor, at=tk)
                mm_acc(ctx, pk[:, 0:512],
                       [(vw(k, 0, 128), B[:, k * T + ts * 512:k * T + (ts + 1) * 512])
                        for k in range(8)], sem=sm)
                sm.wait(nc.scalar)
                ev = nc.scalar.activation(C[:, o * T + ts * 512:o * T + (ts + 1) * 512],
                                          pk[:, 0:512], AF.Gelu_apprx_tanh)
                etoks.append((0, ss.inc(ev)))
            ring.mark_free(sl_, ss, ss.count)
        import os as _os
        if _os.environ.get("L3F_PART") == "h":
            continue
        if _os.environ.get("L3F_DEBUG") == "3" and e == _ne - 1:
            sdb = ctx.sem("st")
            sv.wait(nc.sync)
            sdb.inc(nc.sync.dma_start(out=ctx.dram["l3f_dbg3"][:, 0:1024],
                                      in_=K32[:, 0:1024]), 16)
            sdb.inc(nc.sync.dma_start(out=ctx.dram["l3f_dbg3"][:, 1024:1280],
                                      in_=S32[:, 0:256]), 16)
            sdb.wait(nc.sync)
        if _os.environ.get("L3F_DEBUG") == "2" and e == 0:
            sdb = ctx.sem("st")
            ss.wait(nc.sync)
            sdb.inc(nc.sync.dma_start(out=ctx.dram["l3f_dbg2"][:],
                                      in_=C[:, 0:2048]), 16)
            sdb.wait(nc.sync)
        ss.wait(nc.tensor)
        ytoks = []
        it = 0
        for oq in range(4):
            vw, tw, sl_ = ring.load(hbm_slab3(ctx, "l3f_w2", e, oq * 256,
                                              (oq + 1) * 256, kt=4))
            for t in range(NTT):
                pb, pid = (pk2, 1) if it % 2 == 0 else (pk, 0)
                it += 1
                nc.tensor.wait_ge(sw.h, tw)
                for q, tk in ytoks[-2:]:
                    if q == pid:
                        ctx.sems["ve"].wait(nc.tensor, at=tk)
                mm_acc(ctx, pb[:, 0:256],
                       [(C[:, k * T + t * 128:k * T + (t + 1) * 128],
                         vw(k, 0, 256)) for k in range(4)], sem=sm)
                sm.wait(nc.vector)
                xsl = X[:, t * D + oq * 256:t * D + (oq + 1) * 256]
                ev = nc.vector.scalar_tensor_tensor(
                    xsl, pb[:, 0:256], S32[:, t * 6 + e:t * 6 + e + 1], xsl,
                    ALU.mult, ALU.add)
                ytoks.append((pid, sv.inc(ev)))
            ring.mark_free(sl_, sv, sv.count)
        sv.wait(nc.tensor)


def hbm_slab3(ctx, name, e, c0, c1, kt=8, a0=0, a1=None):
    t = ctx.dram[name]
    r = t.rearrange("e (a p) c -> e p a c", p=128)
    if a1 is None:
        a1 = r.shape[2]
    return r[e][:, a0:a1, c0:c1]



def emit_l0f(ctx):
    nc = ctx.nc
    X, A, B, C, QS, PS, S32, KB, K32, F128 = (ctx.ar[k] for k in
        ("X", "A", "B", "C", "QS", "PS", "S32", "KB", "K32", "F128"))
    V32 = ctx.ar["V32"]
    sv, ss, sm, sw = ctx.sem("ve"), ctx.sem("se"), ctx.sem("mm"), ctx.sem("wld")
    sp, sld = ctx.sem("spill"), ctx.sem("ldx")
    pk, pk2 = ctx.psum["pk"], ctx.psum["pk2"]
    ps_sc = [ctx.psum["ps0"], ctx.psum["ps1"], ctx.psum["ps2"], ctx.psum["ps3"]]
    eyebf = ctx.ar["eye_bf"]
    ones = ctx.ar["ones_bf"]
    maskA, maskB = ctx.ar["maskA"], ctx.ar["maskB"]
    maskWA, maskWB = ctx.ar["maskWA"], ctx.ar["maskWB"]
    scr = ctx.dram["scr_x"]

    sv.wait(nc.sync)
    ctx.sems["se"].wait(nc.sync)
    sp.inc(nc.sync.dma_start(out=scr.rearrange("(n p) d -> p n d", p=128),
                             in_=X[:].rearrange("p (n d) -> p n d", n=NTT)), 16)
    sp.wait(nc.scalar)
    sp.wait(nc.vector)
    scl, bia = emit_ln(ctx, X, A)
    lastf = None
    for t in range(NTT):
        lastf = nc.scalar.activation(X[:, t * D:(t + 1) * D], X[:, t * D:(t + 1) * D],
                                     AF.Identity, scale=scl[:, t:t + 1],
                                     bias=bia[:, t:t + 1])
    ss.inc(lastf)

    sw.inc(nc.gpsimd.dma_start(out=KB[:, 56:57], in_=ctx.dram["onescol"][:]), 16)
    sw.inc(nc.gpsimd.dma_start(out=KB[:, 58:60], in_=ctx.dram["hsel"][:]), 16)
    sw.inc(nc.gpsimd.dma_start(out=KB[0:2, 64:192], in_=ctx.dram["hselT"][:]), 16)
    sld.inc(nc.sync.dma_start(out=S32[:, 216:220], in_=ctx.dram["l0f_qks"][:]), 16)
    sld.inc(nc.sync.dma_start(out=S32[:, 184:216], in_=ctx.dram["l0f_lgb"][:]), 16)
    sld.inc(nc.sync.dma_start(out=S32[:, 96:140], in_=ctx.dram["l0f_b1a"][:]), 16)
    sld.inc(nc.sync.dma_start(out=S32[:, 140:184], in_=ctx.dram["l0f_b1g"][:]), 16)
    sw.inc(nc.gpsimd.dma_start(out=KB[0:1, 512:2560], in_=ctx.dram["l0f_b2"][:]), 16)
    sld.wait(nc.vector)
    sv.inc(nc.vector.tensor_scalar(S32[:, 220:222], S32[:, 218:220], 8.0, None,
                                   ALU.mult))

    ring = WRing(ctx, nslots=3, cols=256)

    import os as _os
    _stop = _os.environ.get("L0F_STOP", "full")
    _nl = 0 if _stop == "ln" else (1 if _stop in ("attn0", "layer0") else 2)
    for l in range(_nl):
        if l == 1:
            sv.wait(nc.scalar)
            ctx.sems["wld"].wait(nc.scalar)
            lastc = None
            for t in range(NTT):
                lastc = nc.scalar.activation(A[:, t * D:(t + 1) * D],
                                             X[:, t * D:(t + 1) * D], AF.Identity)
            ss.inc(lastc)
        emit_transpose(ctx, A, B)
        sv.wait(nc.vector)
        lastsh = None
        for ft in range(8):
            if ft < 4:
                lastsh = nc.vector.tensor_copy(C[:, ft * T:(ft + 1) * T],
                                               B[:, ft * T:(ft + 1) * T])
            else:
                nc.vector.memset(C[:, ft * T:ft * T + 1], 0.0)
                lastsh = nc.vector.tensor_copy(C[:, ft * T + 1:(ft + 1) * T],
                                               B[:, ft * T:(ft + 1) * T - 1])
        sv.inc(lastsh)
        sv.wait(nc.tensor)

        evt = []
        for which in range(2):
            for half in range(2):
                c0 = which * 512 + half * 256
                vw, tw, sl_ = ring.load(hbm_slab3(ctx, "l0f_wqkv", l, c0, c0 + 256))
                for so in range(2):
                    o = half * 2 + so
                    dst = (which * 4 + o) * T
                    for ts in range(4):
                        pb, pid = (pk, 0) if (len(evt)) % 2 == 0 else (pk2, 1)
                        nc.tensor.wait_ge(sw.h, tw)
                        for q, tk in evt[-2:]:
                            if q == pid:
                                ctx.sems["ve"].wait(nc.tensor, at=tk)
                        mm_acc(ctx, pb[:, 0:512],
                               [(vw(k, so * 128, so * 128 + 128),
                                 C[:, k * T + ts * 512:k * T + (ts + 1) * 512])
                                for k in range(8)], sem=sm)
                        sm.wait(nc.vector)
                        ev = nc.vector.tensor_copy(
                            B[:, dst + ts * 512:dst + (ts + 1) * 512], pb[:, 0:512])
                        evt.append((pid, sv.inc(ev)))
                ring.mark_free(sl_, sv, sv.count)
        for half in range(2):
            c0 = 1024 + half * 256
            vw, tw, sl_ = ring.load(hbm_slab3(ctx, "l0f_wqkv", l, c0, c0 + 256))
            for t in range(NTT):
                pb, pid = (pk, 0) if (len(evt)) % 2 == 0 else (pk2, 1)
                nc.tensor.wait_ge(sw.h, tw)
                for q, tk in evt[-2:]:
                    if q == pid:
                        ctx.sems["ve"].wait(nc.tensor, at=tk)
                mm_acc(ctx, pb[:, 0:256],
                       [(C[:, k * T + t * 128:k * T + (t + 1) * 128],
                         vw(k, 0, 256)) for k in range(8)], sem=sm)
                sm.wait(nc.vector)
                ev = nc.vector.tensor_copy(
                    A[:, t * 512 + half * 256:t * 512 + (half + 1) * 256],
                    pb[:, 0:256])
                evt.append((pid, sv.inc(ev)))
            ring.mark_free(sl_, sv, sv.count)
        sv.wait(nc.tensor)
        sv.wait(nc.scalar)

        for which in range(2):
            spc = S32[:, 220 + l:221 + l] if which == 0 else S32[:, 216 + l:217 + l]
            for p in range(4):
                bt = (which * 4 + p) * T
                for th in range(2):
                    for sl2 in range(2):
                        cols = th * 1024 + sl2 * 512
                        sm.wait(nc.scalar)
                        ss.inc(nc.scalar.activation(
                            QS[:, sl2 * 512:(sl2 + 1) * 512],
                            B[:, bt + cols:bt + cols + 512], AF.Square))
                        ss.wait(nc.tensor)
                        sv.wait(nc.tensor)
                        mm_acc(ctx, ps_sc[sl2][0:2, 0:512],
                               [(KB[:, 58:60], QS[:, sl2 * 512:(sl2 + 1) * 512])],
                               sem=sm)
                    sm.wait(nc.vector)
                    for sl2 in range(2):
                        vv(ctx, nc.vector.tensor_copy(
                            K32[0:2, sl2 * 512:(sl2 + 1) * 512],
                            ps_sc[sl2][0:2, 0:512]))
                    sv.wait(nc.scalar)
                    sssync(ctx, nc.scalar.activation(K32[32:34, 0:1024],
                                                     K32[0:2, 0:1024], AF.Sqrt))
                    ss.wait(nc.vector)
                    vv(ctx, nc.vector.tensor_scalar(K32[64:66, 0:1024],
                                                    K32[32:34, 0:1024], 1e-12,
                                                    None, ALU.max))
                    vv(ctx, nc.vector.reciprocal(K32[96:98, 0:1024],
                                                 K32[64:66, 0:1024]))
                    sv.inc(nc.vector.tensor_copy(QS[0:2, 1024:2048],
                                                 K32[96:98, 0:1024]))
                    sv.wait(nc.vector)
                    sv.wait(nc.tensor)
                    for sl2 in range(2):
                        mm_acc(ctx, ps_sc[2 + sl2][:, 0:512],
                               [(KB[0:2, 64:192],
                                 QS[0:2, 1024 + sl2 * 512:1024 + (sl2 + 1) * 512])],
                               sem=sm)
                    sm.wait(nc.vector)
                    for sl2 in range(2):
                        cols = th * 1024 + sl2 * 512
                        lastn = nc.vector.scalar_tensor_tensor(
                            B[:, bt + cols:bt + cols + 512],
                            B[:, bt + cols:bt + cols + 512], spc,
                            ps_sc[2 + sl2][:, 0:512], ALU.mult, ALU.mult)
                        sv.inc(lastn)
                    sv.wait(nc.tensor)

        for which in range(2):
            nm_c = "rotqc" if which == 0 else "rotkc"
            nm_s = "rotqs" if which == 0 else "rotks"
            vc, tc, slc = ring.load(ctx.dram[nm_c].rearrange("p (a c) -> p a c", a=1))
            vs_, ts_, sls = ring.load(ctx.dram[nm_s].rearrange("p (a c) -> p a c", a=1))
            nc.vector.wait_ge(sw.h, ts_)
            for p in range(4):
                bt = (which * 4 + p) * T
                nc.vector.tensor_scalar(QS[:, 2048:4096].rearrange(
                    "p (h c) -> p h c", h=2)[0:0 + 32, :, :] if False else
                    QS[0:32, 2048:4096], B[32:64, bt:bt + T], -1.0, None, ALU.mult)
                nc.vector.tensor_copy(QS[32:64, 2048:4096], B[0:32, bt:bt + T])
                nc.vector.tensor_scalar(QS[64:96, 2048:4096], B[96:128, bt:bt + T],
                                        -1.0, None, ALU.mult)
                nc.vector.tensor_copy(QS[96:128, 2048:4096], B[64:96, bt:bt + T])
                nc.vector.tensor_tensor(QS[:, 0:2048], B[:, bt:bt + T],
                                        vc(0, 0, 2048), ALU.mult)
                nc.vector.tensor_tensor(QS[:, 2048:4096], QS[:, 2048:4096],
                                        vs_(0, 0, 2048), ALU.mult)
                lastr = nc.vector.tensor_tensor(B[:, bt:bt + T], QS[:, 0:2048],
                                                QS[:, 2048:4096], ALU.add)
                sv.inc(lastr)
            ring.mark_free(slc, sv, sv.count)
            ring.mark_free(sls, sv, sv.count)
        sv.wait(nc.tensor)

        for p in range(4):
            for c in range(32):
                qbase = c * 64
                kv0 = max(0, (qbase - 512) // 128)
                kvL = (qbase + 63) // 128
                nkv = kvL - kv0 + 1
                sv.wait(nc.tensor)
                lastmm = None
                for s in range((nkv + 3) // 4):
                    t0 = kv0 + s * 4
                    ncols = min(4, kvL + 1 - t0) * 128
                    qa = B[0:64, p * T + qbase:p * T + qbase + 64]
                    qb = B[64:128, p * T + qbase:p * T + qbase + 64]
                    ka = B[0:64, (4 + p) * T + t0 * 128:(4 + p) * T + t0 * 128 + ncols]
                    kb = B[64:128, (4 + p) * T + t0 * 128:(4 + p) * T + t0 * 128 + ncols]
                    nc.tensor.matmul(ps_sc[s][0:64, 0:ncols], qa, ka,
                                     start=True, stop=True, tile_position=(0, 0))
                    lastmm = nc.tensor.matmul(ps_sc[s][64:128, 0:ncols], qb, kb,
                                              start=True, stop=True,
                                              tile_position=(64, 64))
                sm.inc(lastmm)
                sm.wait(nc.vector)
                dloc = kvL - kv0
                mk = maskA if (qbase % 128) == 0 else maskB
                nc.vector.tensor_tensor(
                    ps_sc[dloc // 4][:, (dloc % 4) * 128:(dloc % 4) * 128 + 128],
                    ps_sc[dloc // 4][:, (dloc % 4) * 128:(dloc % 4) * 128 + 128],
                    mk[:], ALU.add)
                if qbase >= 512:
                    mw = maskWA if (qbase % 128) == 0 else maskWB
                    nc.vector.tensor_tensor(ps_sc[0][:, 0:128], ps_sc[0][:, 0:128],
                                            mw[:], ALU.add)
                sv.inc(vspacer(ctx))
                sv.wait(nc.vector)
                nm = S32[:, 0:4]
                lastv = None
                for s in range((nkv + 3) // 4):
                    ncols = min(4, kvL + 1 - (kv0 + s * 4)) * 128
                    lastv = nc.vector.tensor_reduce(nm[:, s:s + 1],
                                                    ps_sc[s][:, 0:ncols],
                                                    AX.X, ALU.max, negate=True)
                sv.inc(lastv)
                sv.wait(nc.vector)
                lastv = nc.vector.tensor_reduce(S32[:, 8:9], nm[:, 0:(nkv + 3) // 4],
                                                AX.X, ALU.min)
                sv.inc(lastv)
                sv.wait(nc.vector)
                sv.wait(nc.scalar)
                sm.wait(nc.scalar)
                laste = None
                for s in range((nkv + 3) // 4):
                    ncols = min(4, kvL + 1 - (kv0 + s * 4)) * 128
                    laste = nc.scalar.activation(
                        PS[:, s * 512:s * 512 + ncols], ps_sc[s][:, 0:ncols],
                        AF.Exp, bias=S32[:, 8:9], accum_out=S32[:, 12 + s:13 + s])
                ss.inc(laste)
                ss.wait(nc.vector)
                vv(ctx, nc.vector.tensor_reduce(S32[:, 9:10],
                                                S32[:, 12:12 + (nkv + 3) // 4],
                                                AX.X, ALU.add))
                vv(ctx, nc.vector.reciprocal(S32[:, 11:12], S32[:, 9:10]))
                lastd = nc.vector.tensor_scalar(PS[:, 2048:2176], eyebf[:],
                                                S32[:, 11:12], None, ALU.mult)
                sv.inc(lastd)
                ss.wait(nc.tensor)
                sv.wait(nc.tensor)
                lastt = None
                for kt in range(nkv):
                    lastt = nc.tensor.matmul(
                        ps_sc[kt // 4][:, (kt % 4) * 128:(kt % 4) * 128 + 128],
                        PS[:, kt * 128:(kt + 1) * 128], PS[:, 2048:2176],
                        start=True, stop=True)
                sm.inc(lastt)
                sm.wait(nc.vector)
                lastev = None
                for s in range((nkv + 3) // 4):
                    ncols = min(4, kvL + 1 - (kv0 + s * 4)) * 128
                    lastev = nc.vector.tensor_copy(
                        PS[:, 2176 + s * 512:2176 + s * 512 + ncols],
                        ps_sc[s][:, 0:ncols])
                sv.inc(lastev)
                sv.wait(nc.tensor)
                lastpv = None
                for kt in range(nkv):
                    gt = kv0 + kt
                    va = A[:, gt * 512 + p * 128:gt * 512 + p * 128 + 64]
                    vb = A[:, gt * 512 + p * 128 + 64:gt * 512 + p * 128 + 128]
                    nc.tensor.matmul(pk[0:64, 0:64], va,
                                     PS[:, 2176 + kt * 128:2176 + kt * 128 + 64],
                                     start=(kt == 0), stop=(kt == nkv - 1),
                                     tile_position=(0, 0), skip_group_check=True)
                    lastpv = nc.tensor.matmul(
                        pk[64:128, 0:64], vb,
                        PS[:, 2176 + kt * 128 + 64:2176 + kt * 128 + 128],
                        start=(kt == 0), stop=(kt == nkv - 1),
                        tile_position=(0, 64), skip_group_check=True)
                sm.inc(lastpv)
                sm.wait(nc.vector)
                ev = nc.vector.tensor_copy(
                    A[:, 8192 + p * T + qbase:8192 + p * T + qbase + 64],
                    pk[:, 0:64])
                sv.inc(ev)

        evt = []
        it = 0
        for qtr in range(4):
            vw, tw, sl_ = ring.load(hbm_slab3(ctx, "l0f_wo", l, qtr * 256,
                                              (qtr + 1) * 256, kt=4))
            for t in range(NTT):
                pb, pid = (pk2, 1) if it % 2 == 0 else (pk, 0)
                it += 1
                nc.tensor.wait_ge(sw.h, tw)
                if it == 1:
                    sv.wait(nc.tensor)
                for q, tk in evt[-2:]:
                    if q == pid:
                        ctx.sems["ve"].wait(nc.tensor, at=tk)
                mm_acc(ctx, pb[:, 0:256],
                       [(A[:, 8192 + k * T + t * 128:8192 + k * T + (t + 1) * 128],
                         vw(k, 0, 256)) for k in range(4)], sem=sm)
                sm.wait(nc.vector)
                xsl = X[:, t * D + qtr * 256:t * D + (qtr + 1) * 256]
                ev = nc.vector.tensor_tensor(xsl, xsl, pb[:, 0:256], ALU.add)
                evt.append((pid, sv.inc(ev)))
            ring.mark_free(sl_, sv, sv.count)
        sv.wait(nc.scalar)
        if _stop == "attn0":
            break

        lastc = None
        for t in range(NTT):
            lastc = nc.scalar.activation(A[:, t * D:(t + 1) * D],
                                         X[:, t * D:(t + 1) * D], AF.Identity)
        ss.inc(lastc)
        emit_transpose(ctx, A, B)
        sv.wait(nc.vector)
        lastsh = None
        for ft in range(8):
            if ft < 4:
                lastsh = nc.vector.tensor_copy(C[:, ft * T:(ft + 1) * T],
                                               B[:, ft * T:(ft + 1) * T])
            else:
                nc.vector.memset(C[:, ft * T:ft * T + 1], 0.0)
                lastsh = nc.vector.tensor_copy(C[:, ft * T + 1:(ft + 1) * T],
                                               B[:, ft * T:(ft + 1) * T - 1])
        sv.inc(lastsh)
        sv.wait(nc.tensor)
        sv.wait(nc.scalar)
        for ts in range(4):
            sqtoks = []
            ctx.sems["ve"].wait(nc.scalar)
            for ft in range(8):
                qh = (ft % 2) * 512
                if len(sqtoks) >= 2:
                    ctx.sems["mm"].wait(nc.scalar, at=sqtoks[-2])
                ss.inc(nc.scalar.activation(
                    QS[:, qh:qh + 512],
                    C[:, ft * T + ts * 512:ft * T + (ts + 1) * 512], AF.Square))
                ss.wait(nc.tensor)
                if ft == 0:
                    sv.wait(nc.tensor)
                nc.tensor.matmul(ps_sc[0][0:1, 0:512], KB[:, 56:57],
                                 QS[:, qh:qh + 512], start=(ft == 0),
                                 stop=(ft == 7), skip_group_check=True)
                tok = sm.inc(nc.tensor.matmul(
                    ps_sc[1][0:1, 0:512], KB[:, 56:57],
                    C[:, ft * T + ts * 512:ft * T + (ts + 1) * 512],
                    start=(ft == 0), stop=(ft == 7), skip_group_check=True))
                sqtoks.append(tok)
            sm.wait(nc.vector)
            for hh in range(2):
                sa = K32[0:1, 0:256]
                sq = K32[0:1, 256:512]
                si = K32[0:1, 512:768]
                sb = K32[0:1, 768:1024]
                c0h = hh * 256
                vv(ctx, nc.vector.tensor_copy(sa, ps_sc[1][0:1, c0h:c0h + 256]))
                vv(ctx, nc.vector.tensor_copy(sq, ps_sc[0][0:1, c0h:c0h + 256]))
                vv(ctx, nc.vector.tensor_scalar(sa, sa, 1.0 / D, None, ALU.mult))
                vv(ctx, nc.vector.tensor_scalar(sq, sq, 1.0 / D, None, ALU.mult))
                vv(ctx, nc.vector.tensor_tensor(si, sa, sa, ALU.mult))
                sv.inc(nc.vector.tensor_tensor(si, sq, si, ALU.subtract))
                sv.wait(nc.vector)
                sv.wait(nc.scalar)
                sssync(ctx, nc.scalar.activation(si, si, AF.Ln,
                                                 bias=V32[0:1, 240:241]))
                sssync(ctx, nc.scalar.activation(si, si, AF.Exp, scale=-0.5))
                ss.wait(nc.vector)
                vv(ctx, nc.vector.scalar_tensor_tensor(sb, sa, -1.0, si,
                                                       ALU.mult, ALU.mult))
                vv(ctx, nc.vector.tensor_copy(QS[0:1, 1024 + c0h:1280 + c0h], si))
                sv.inc(nc.vector.tensor_copy(QS[0:1, 1536 + c0h:1792 + c0h], sb))
                sv.wait(nc.vector)
                sv.wait(nc.tensor)
                mm_acc(ctx, ps_sc[2][:, c0h:c0h + 256],
                       [(ones[0:1, 0:128], QS[0:1, 1024 + c0h:1280 + c0h])],
                       sem=None)
                mm_acc(ctx, ps_sc[3][:, c0h:c0h + 256],
                       [(ones[0:1, 0:128], QS[0:1, 1536 + c0h:1792 + c0h])],
                       sem=sm)
            sm.wait(nc.vector)
            for ft in range(8):
                lg = S32[:, 184 + l * 8 + ft:185 + l * 8 + ft]
                lb = S32[:, 200 + l * 8 + ft:201 + l * 8 + ft]
                nc.vector.tensor_tensor(QS[:, 0:512],
                                        C[:, ft * T + ts * 512:ft * T + (ts + 1) * 512],
                                        ps_sc[2][:, 0:512], ALU.mult)
                nc.vector.tensor_tensor(QS[:, 512:1024], QS[:, 0:512],
                                        ps_sc[3][:, 0:512], ALU.add)
                lastl = nc.vector.tensor_scalar(
                    B[:, ft * T + ts * 512:ft * T + (ts + 1) * 512],
                    QS[:, 512:1024], lg, lb, ALU.mult, ALU.add)
                sv.inc(lastl)
            sv.wait(nc.tensor)

        for ts in range(4):
            htoks = []
            for i in range(22):
                vg, tg, slg = ring.load(hbm_slab3(ctx, "l0f_w1g", l, i * 128,
                                                  (i + 1) * 128))
                va, ta, sla = ring.load(hbm_slab3(ctx, "l0f_w1a", l, i * 128,
                                                  (i + 1) * 128))
                nc.tensor.wait_ge(sw.h, tg)
                ctx.sems["se"].wait(nc.tensor)
                mm_acc(ctx, pk[:, 0:512],
                       [(vg(k, 0, 128), B[:, k * T + ts * 512:k * T + (ts + 1) * 512])
                        for k in range(8)], sem=sm)
                sm.wait(nc.scalar)
                if htoks:
                    ctx.sems["ve"].wait(nc.scalar, at=htoks[-1])
                ss.inc(nc.scalar.activation(QS[:, 0:512], pk[:, 0:512], AF.Gelu,
                                            bias=S32[:, 140 + 22 * l + i:
                                                     141 + 22 * l + i]))
                nc.tensor.wait_ge(sw.h, ta)
                ctx.sems["ve"].wait(nc.tensor)
                mm_acc(ctx, pk2[:, 0:512],
                       [(va(k, 0, 128), B[:, k * T + ts * 512:k * T + (ts + 1) * 512])
                        for k in range(8)], sem=sm)
                sm.wait(nc.vector)
                ss.wait(nc.vector)
                ev = nc.vector.scalar_tensor_tensor(
                    A[:, i * 512:(i + 1) * 512], pk2[:, 0:512],
                    S32[:, 96 + 22 * l + i:97 + 22 * l + i], QS[:, 0:512],
                    ALU.add, ALU.mult)
                htoks.append(sv.inc(ev))
                ring.mark_free(slg, sv, sv.count)
                ring.mark_free(sla, sv, sv.count)
            sv.wait(nc.tensor)
            ytoks = []
            it = 0
            for oq in range(4):
                loads = []
                for kk in range(3):
                    a0, a1 = kk * 8, min(22, (kk + 1) * 8)
                    loads.append(ring.load(hbm_slab3(ctx, "l0f_w2", l, oq * 256,
                                                     (oq + 1) * 256, a0=a0, a1=a1)))
                for tt in range(4):
                    pb, pid = (pk, 0) if it % 2 == 0 else (pk2, 1)
                    it += 1
                    nc.tensor.wait_ge(sw.h, loads[-1][1])
                    for q, tk in ytoks[-2:]:
                        if q == pid:
                            ctx.sems["ve"].wait(nc.tensor, at=tk)
                    pairs = []
                    for kk in range(3):
                        a0, a1 = kk * 8, min(22, (kk + 1) * 8)
                        for k in range(a1 - a0):
                            pairs.append((A[:, (a0 + k) * 512 + tt * 128:
                                            (a0 + k) * 512 + tt * 128 + 128],
                                          loads[kk][0](k, 0, 256)))
                    extra = [(ones[0:1, 0:128],
                              KB[0:1, 512 + l * 1024 + oq * 256:
                                 512 + l * 1024 + (oq + 1) * 256])]
                    mm_acc(ctx, pb[:, 0:256], pairs, sem=sm, extra=extra)
                    sm.wait(nc.vector)
                    gt = ts * 4 + tt
                    xsl = X[:, gt * D + oq * 256:gt * D + (oq + 1) * 256]
                    ev = nc.vector.tensor_tensor(xsl, xsl, pb[:, 0:256], ALU.add)
                    ytoks.append((pid, sv.inc(ev)))
                for (vw_, tw_, sl2_) in loads:
                    ring.mark_free(sl2_, sv, sv.count)
            sv.wait(nc.tensor)
            ss.wait(nc.tensor)

    if _stop == "full":
        sv.wait(nc.gpsimd)
        ctx.sems["se"].wait(nc.gpsimd)
        sp2 = ctx.sem("wld")
        sp2.inc(nc.gpsimd.dma_start(out=X[:].rearrange("p (n d) -> p n d", n=NTT),
                                    in_=scr.rearrange("(n p) d -> p n d", p=128),
                                    accum_op=ALU.add), 16)
        for eng in (nc.tensor, nc.vector, nc.scalar, nc.sync, nc.gpsimd):
            sp2.wait(eng)



def emit_l2a(ctx):
    nc = ctx.nc
    X, A, B, C, QS, PS, S32, KB, K32, F128 = (ctx.ar[k] for k in
        ("X", "A", "B", "C", "QS", "PS", "S32", "KB", "K32", "F128"))
    V32 = ctx.ar["V32"]
    sv, ss, sm, sw = ctx.sem("ve"), ctx.sem("se"), ctx.sem("mm"), ctx.sem("wld")
    sld, sp = ctx.sem("ldx"), ctx.sem("spill")
    pk, pk2 = ctx.psum["pk"], ctx.psum["pk2"]
    ps_sc = [ctx.psum["ps0"], ctx.psum["ps1"], ctx.psum["ps2"], ctx.psum["ps3"]]
    ptA, ptB = ctx.psum["ptA"], ctx.psum["ptB"]
    eyebf = ctx.ar["eye_bf"]
    ones = ctx.ar["ones_bf"]
    NH, DK, DV, CH = 4, 256, 256, 32
    NCK = T // CH
    DECAY = 1.0 - 1.0 / 3000.0
    EPSF = 0.08 * DECAY

    emit_ln(ctx, X, A)
    emit_transpose(ctx, A, B)
    ctx.sems["ve"].wait(nc.tensor)

    ring = WRing(ctx, nslots=3, cols=256)

    vb, tb, sb_ = ring.load(hbm_slab(ctx, "l2a_wb", 0, 4))
    for sl in range(4):
        pb = pk if sl % 2 == 0 else pk2
        nc.tensor.wait_ge(sw.h, tb)
        if sl >= 2:
            ctx.sems["se"].wait(nc.tensor)
        mm_acc(ctx, pb[0:4, 0:512],
               [(vb(k, 0, 4), B[:, k * T + sl * 512:k * T + (sl + 1) * 512])
                for k in range(8)], sem=sm)
        sm.wait(nc.scalar)
        ss.inc(nc.scalar.activation(QS[0:4, sl * 512:(sl + 1) * 512],
                                    pb[0:4, 0:512], AF.Sigmoid))
    ring.mark_free(sb_, ss, ss.count)
    ss.wait(nc.tensor)
    for tt in range(NTT):
        sm.inc(nc.tensor.transpose(ptA[:, 0:4],
                                   QS[0:4, tt * 128:(tt + 1) * 128], eyebf[0:4, 0:4]))
        sm.wait(nc.vector)
        sv.inc(nc.vector.tensor_copy(V32[:, tt * 4:(tt + 1) * 4], ptA[:, 0:4]))
        sv.wait(nc.tensor)

    def proj_raw(wname):
        evt = []
        for oc in range(4):
            vw, tw, so_ = ring.load(hbm_slab(ctx, wname, oc * 256, (oc + 1) * 256))
            for so in range(2):
                o = oc * 2 + so
                for sl in range(4):
                    pb, pid = (pk, 0) if (len(evt)) % 2 == 0 else (pk2, 1)
                    nc.tensor.wait_ge(sw.h, tw)
                    for q, tk in evt[-2:]:
                        if q == pid:
                            ctx.sems["ve"].wait(nc.tensor, at=tk)
                    mm_acc(ctx, pb[:, 0:512],
                           [(vw(k, so * 128, so * 128 + 128),
                             B[:, k * T + sl * 512:k * T + (sl + 1) * 512])
                            for k in range(8)], sem=sm)
                    sm.wait(nc.vector)
                    ev = nc.vector.tensor_copy(
                        C[:, o * T + sl * 512:o * T + (sl + 1) * 512],
                        pb[:, 0:512])
                    evt.append((pid, sv.inc(ev)))
            ring.mark_free(so_, sv, sv.count)
        sv.wait(nc.tensor)

    def conv_silu(wname, dst):
        K32_ = K32
        tok = sld.inc(nc.sync.dma_start(out=K32_[:, 0:32],
                                        in_=ctx.dram[wname][:]), 16)
        sld.wait(nc.vector, at=tok)
        for ct in range(8):
            sm.wait(nc.vector)
            lastd = None
            for j in range(4):
                lastd = nc.vector.tensor_scalar(
                    QS[:, 2048 + j * 128:2048 + (j + 1) * 128], eyebf[:],
                    K32_[:, ct * 4 + j:ct * 4 + j + 1], None, ALU.mult)
            sv.inc(lastd)
            sv.wait(nc.tensor)
            for sl in range(4):
                pb = pk if sl % 2 == 0 else pk2
                c0 = sl * 512
                ctx.sems["se"].wait(nc.tensor)
                last = None
                order = [3, 0, 1, 2]
                for oi, j in enumerate(order):
                    sh = 3 - j
                    lo = max(0, c0 - sh)
                    outoff = lo - (c0 - sh)
                    ncols = 512 - outoff
                    last = nc.tensor.matmul(
                        pb[:, outoff:512],
                        QS[:, 2048 + j * 128:2048 + (j + 1) * 128],
                        C[:, ct * T + lo:ct * T + lo + ncols],
                        start=(oi == 0), stop=(oi == 3), skip_group_check=True)
                sm.inc(last)
                sm.wait(nc.scalar)
                ss.inc(nc.scalar.activation(dst(ct, sl), pb[:, 0:512], AF.Silu))
            ss.wait(nc.tensor)

    for which, wname, cname in (("q", "l2a_wq", "l2a_cq"),
                                ("k", "l2a_wk", "l2a_ck"),
                                ("v", "l2a_wv", "l2a_cv")):
        proj_raw(wname)
        if which == "v":
            conv_silu(cname, lambda ct, sl: A[:, ct * T + sl * 512:
                                              ct * T + (sl + 1) * 512])
            sv.wait(nc.scalar)
        else:
            scr = ctx.dram["scr_qT" if which == "q" else "scr_kT"]
            conv_silu(cname, lambda ct, sl: A[:, ct * T + sl * 512:
                                              ct * T + (sl + 1) * 512])
            ss.wait(nc.sync)
            sv.wait(nc.sync)
            sp.inc(nc.sync.dma_start(
                out=scr.rearrange("(a p) c -> p a c", p=128),
                in_=A[:].rearrange("p (a c) -> p a c", a=8)), 16)
            sp.wait(nc.tensor)
            sp.wait(nc.scalar)

    sv.wait(nc.tensor)
    gi = 0
    sptok = {}
    for tt in range(NTT):
        for g2 in range(2):
            pt = (ptA, ptB)[gi % 2]
            if gi - 2 in sptok:
                ctx.sems["ve"].wait(nc.tensor, at=sptok[gi - 2][0])
                nc.vector.wait_ge(ctx.sems["spill"].h, sptok[gi - 2][1])
            lastt = None
            for j in range(4):
                ft = g2 * 4 + j
                lastt = nc.tensor.transpose(
                    pt[:, j * 128:(j + 1) * 128],
                    A[:, ft * T + tt * 128:ft * T + (tt + 1) * 128], eyebf[:])
            sm.inc(lastt)
            sm.wait(nc.vector)
            ev = nc.vector.tensor_copy(
                QS[:, (gi % 2) * 512:(gi % 2) * 512 + 512], pt[:, 0:512])
            evt_ = sv.inc(ev)
            sv.wait(nc.sync)
            spt = sp.inc(nc.sync.dma_start(
                out=ctx.dram["scr_vtok"].rearrange("(a p) c -> p a c", p=128)
                [:, tt:tt + 1, g2 * 512:(g2 + 1) * 512],
                in_=QS[:, (gi % 2) * 512:(gi % 2) * 512 + 512]
                .rearrange("p (a c) -> p a c", a=1)), 16)
            sptok[gi] = (evt_, spt)
            gi += 1
    sv.wait(nc.tensor)
    for wname, taps, scr_nm in (("l2a_firs", 5, "scr_fs"),
                                ("l2a_firl", 64, "scr_fl")):
        scr = ctx.dram[scr_nm]
        tokd = sld.inc(nc.sync.dma_start(out=K32[:, 0:8 * taps],
                                         in_=ctx.dram[wname][:]), 16)
        sld.wait(nc.vector, at=tokd)
        for ct in range(8):
            tap_order = [taps - 1] + list(range(taps - 1))
            for wi in range(0, taps, 16):
                wtaps = tap_order[wi:wi + 16]
                sm.wait(nc.vector)
                lastd = None
                for jj, tp in enumerate(wtaps):
                    lastd = nc.vector.tensor_scalar(
                        QS[:, 2048 + jj * 128:2048 + (jj + 1) * 128], eyebf[:],
                        K32[:, ct * taps + tp:ct * taps + tp + 1],
                        None, ALU.mult)
                sv.inc(lastd)
                sv.wait(nc.tensor)
                for sl in range(4):
                    pb = [pk, pk2, ps_sc[0], ps_sc[1]][sl]
                    c0 = sl * 512
                    last = None
                    for jj, tp in enumerate(wtaps):
                        sh = (taps - 1) - tp
                        lo = max(0, c0 - sh)
                        outoff = lo - (c0 - sh)
                        ncols = 512 - outoff
                        if ncols <= 0:
                            continue
                        last = nc.tensor.matmul(
                            pb[:, outoff:512],
                            QS[:, 2048 + jj * 128:2048 + (jj + 1) * 128],
                            A[:, ct * T + lo:ct * T + lo + ncols],
                            start=(wi == 0 and jj == 0),
                            stop=(wi + 16 >= taps and jj == len(wtaps) - 1),
                            skip_group_check=True)
                    if last is not None:
                        sm.inc(last)
            for sl in range(4):
                pb = [pk, pk2, ps_sc[0], ps_sc[1]][sl]
                sm.wait(nc.scalar)
                sp.wait(nc.scalar)
                ss.inc(nc.scalar.activation(QS[:, (sl % 2) * 512:(sl % 2) * 512 + 512],
                                            pb[:, 0:512], AF.Identity))
                ss.wait(nc.tensor)
                lastt = None
                for j in range(4):
                    lastt = nc.tensor.transpose(
                        (ptA, ptB)[sl % 2][:, j * 128:(j + 1) * 128],
                        QS[:, (sl % 2) * 512 + j * 128:(sl % 2) * 512 + (j + 1) * 128],
                        eyebf[:])
                sm.inc(lastt)
                sm.wait(nc.vector)
                sv.inc(nc.vector.tensor_copy(
                    QS[:, 1024 + (sl % 2) * 512:1024 + (sl % 2) * 512 + 512],
                    (ptA, ptB)[sl % 2][:, 0:512]))
                sv.wait(nc.sync)
                sp.inc(nc.sync.dma_start(
                    out=scr.rearrange("(a p) c -> p a c", p=128)
                    [:, sl * 4:(sl + 1) * 4, ct * 128:(ct + 1) * 128],
                    in_=QS[:, 1024 + (sl % 2) * 512:1024 + (sl % 2) * 512 + 512]
                    .rearrange("p (a c) -> p a c", a=4)), 16)
            sp.wait(nc.tensor)

    sv.wait(nc.vector)
    lastz = None
    for h in range(NH):
        lastz = nc.vector.memset(QS[:, 512 * h:512 * (h + 1)], 0.0)
    sv.inc(lastz)
    eye32 = eyebf[0:32, 0:32]
    mskSL = KB[0:32, 3616:3648]
    mskU = KB[0:32, 3648:3680]
    sw.inc(nc.gpsimd.dma_start(out=KB[0:32, 3616:3680],
                               in_=ctx.dram["masks32"][:]), 16)
    sw.wait(nc.tensor)
    TBC = 3072
    tcols = {"A": 0, "B": 32, "A2": 64, "B2": 96, "A4": 128, "B4": 160,
             "A8": 192, "B8": 224, "B16": 256, "C1": 288, "D1": 320,
             "C2": 352, "D2": 384, "C3": 416, "D3": 448, "TT": 480}

    def tb(nm):
        return QS[0:32, TBC + tcols[nm]:TBC + tcols[nm] + 32]

    for h in range(NH):
        if h == 0:
            sp.wait(nc.sync)
        ctx.sems["mm"].wait(nc.sync)
        ctx.sems["ve"].wait(nc.sync)
        sld.inc(nc.sync.dma_start(
            out=A[:, 0:4096].rearrange("p (a c) -> p a c", a=2),
            in_=ctx.dram["scr_qT"].rearrange("(a p) c -> p a c", p=128)
            [:, 2 * h:2 * h + 2, :]), 16)
        ldt = sld.inc(nc.sync.dma_start(
            out=A[:, 4096:8192].rearrange("p (a c) -> p a c", a=2),
            in_=ctx.dram["scr_kT"].rearrange("(a p) c -> p a c", p=128)
            [:, 2 * h:2 * h + 2, :]), 16)
        sld.wait(nc.tensor, at=ldt)
        sv.wait(nc.tensor)
        ctx.sems["mm"].wait(nc.vector)
        gi = 0
        evtok = {}
        for tt in range(NTT):
            pt = (ptA, ptB)[gi % 2]
            if gi - 2 in evtok:
                ctx.sems["ve"].wait(nc.tensor, at=evtok[gi - 2])
            nc.tensor.transpose(
                pt[:, 0:128],
                A[:, 4096 + tt * 128:4096 + (tt + 1) * 128], eyebf[:])
            lastt = nc.tensor.transpose(
                pt[:, 128:256],
                A[:, 4096 + T + tt * 128:4096 + T + (tt + 1) * 128], eyebf[:])
            sm.inc(lastt)
            sm.wait(nc.vector)
            ev = nc.vector.tensor_copy(
                A[:, 8192 + tt * 256:8192 + (tt + 1) * 256], pt[:, 0:256])
            evtok[gi] = sv.inc(ev)
            gi += 1
        import os as _osA
        if _osA.environ.get("L2A_SPILL") == "ktokraw" and h == 0:
            ctx.barrier()
            sp.inc(nc.sync.dma_start(
                out=ctx.dram["scr_od"].rearrange("(a p) c -> p a c", p=128)
                [:, 0:4, :],
                in_=A[:, 8192:12288].rearrange("p (a c) -> p a c", a=4)), 16)
            sp.wait(nc.sync)
            ctx.barrier()
        import os as _osB
        _nonorm = _osB.environ.get("L2A_NONORM")
        sv.wait(nc.tensor)
        gi = 0
        evtok2 = {}
        for tt in range(NTT):
            pt = (ptA, ptB)[gi % 2]
            if gi - 2 in evtok2:
                ctx.sems["ve"].wait(nc.tensor, at=evtok2[gi - 2])
            nc.tensor.transpose(pt[:, 0:128],
                                A[:, tt * 128:(tt + 1) * 128], eyebf[:])
            lastt = nc.tensor.transpose(
                pt[:, 128:256], A[:, T + tt * 128:T + (tt + 1) * 128], eyebf[:])
            sm.inc(lastt)
            sm.wait(nc.vector)
            ev = nc.vector.tensor_copy(PS[:, tt * 256:(tt + 1) * 256],
                                       pt[:, 0:256])
            evtok2[gi] = sv.inc(ev)
            gi += 1
        sv.wait(nc.scalar)
        lastq1 = lastq2 = None
        for tt in range(NTT):
            lastq1 = nc.scalar.activation(QS[:, 2048 + (tt % 4) * 256:
                                             2048 + (tt % 4) * 256 + 256],
                                          PS[:, tt * 256:(tt + 1) * 256],
                                          AF.Square, accum_out=F128[:, tt:tt + 1])
        for tt in range(NTT):
            lastq2 = nc.scalar.activation(QS[:, 2048 + (tt % 4) * 256:
                                             2048 + (tt % 4) * 256 + 256],
                                          A[:, 8192 + tt * 256:8192 + (tt + 1) * 256],
                                          AF.Square, accum_out=F128[:, 32 + tt:33 + tt])
        ss.inc(lastq2)
        ss.wait(nc.scalar)
        ss.wait(nc.vector)
        sssync(ctx, nc.scalar.activation(F128[:, 16:32], F128[:, 0:16], AF.Ln,
                                         bias=V32[:, 241:242]))
        sssync(ctx, nc.scalar.activation(F128[:, 16:32], F128[:, 16:32], AF.Exp,
                                         scale=-0.5))
        sssync(ctx, nc.scalar.activation(F128[:, 48:64], F128[:, 32:48], AF.Ln,
                                         bias=V32[:, 241:242]))
        sssync(ctx, nc.scalar.activation(F128[:, 48:64], F128[:, 48:64], AF.Exp,
                                         scale=-0.5))
        ss.wait(nc.vector)
        if _osB.environ.get("L2A_RK1"):
            vv(ctx, nc.vector.memset(F128[:, 48:64], 1.0))
            vv(ctx, nc.vector.memset(F128[:, 16:32], 1.0))
        lastkn = None
        for tt in range(NTT):
            lastkn = nc.vector.tensor_scalar(
                A[:, 8192 + tt * 256:8192 + (tt + 1) * 256],
                A[:, 8192 + tt * 256:8192 + (tt + 1) * 256],
                F128[:, 48 + tt:49 + tt], None, ALU.mult)
        sv.inc(lastkn)
        import os as _os9
        if _os9.environ.get("L2A_SPILL") == "ktok" and h == 0:
            sv.wait(nc.sync)
            spk = sp.inc(nc.sync.dma_start(
                out=ctx.dram["scr_od"].rearrange("(a p) c -> p a c", p=128)
                [:, 0:4, :],
                in_=A[:, 8192:12288].rearrange("p (a c) -> p a c", a=4)), 16)
            sp.wait(nc.sync)
        for rid, fcol, base in ((0, 16, 0), (1, 48, 4096)):
            vv(ctx, nc.vector.tensor_copy(QS[0:128, 2048:2064],
                                          F128[:, fcol:fcol + 16]))
            sv.wait(nc.tensor)
            sm.inc(nc.tensor.transpose(ptA[0:16, 0:128], QS[:, 2048:2064], eyebf[:]))
            sm.wait(nc.vector)
            sv.inc(nc.vector.tensor_copy(QS[0:16, 2064:2192], ptA[0:16, 0:128]))
            sv.wait(nc.sync)
            spr = sp.inc(nc.sync.dma_start(out=ctx.dram["scr_row"][:],
                                           in_=QS[0:16, 2064:2192]), 16)
            nc.sync.wait_ge(sp.h, spr)
            sp.inc(nc.sync.dma_start(
                out=QS[0:1, 2192:4240] if False else QS[0:1, 2048:4096],
                in_=ctx.dram["scr_row"].rearrange("(o a) c -> o (a c)", o=1)), 16)
            sp.wait(nc.tensor)
            sp.wait(nc.vector)
            for sl in range(4):
                mm_acc(ctx, pk[:, 0:512],
                       [(ones[0:1, 0:128],
                         QS[0:1, 2048 + sl * 512:2048 + (sl + 1) * 512])], sem=sm)
                sm.wait(nc.vector)
                lastn2 = None
                for dt_ in range(2):
                    lastn2 = nc.vector.tensor_tensor(
                        A[:, base + dt_ * T + sl * 512:base + dt_ * T + (sl + 1) * 512],
                        A[:, base + dt_ * T + sl * 512:base + dt_ * T + (sl + 1) * 512],
                        pk[:, 0:512], ALU.mult)
                sv.inc(lastn2)
                sv.wait(nc.tensor)

        Spsum = ps_sc[h]
        last_osp = 0
        import os as _os7
        _nck = int(_os7.environ.get("L2A_NCK", str(NCK)))
        _c0 = int(_os7.environ.get("L2A_C0", "0"))
        for c in range(_c0, _nck):
            tt, pb32 = c // 4, (c % 4) * 32
            qoff = c * 32
            if c % 4 == 0:
                vbase = 1312 + ((c // 4) % 2) * 256
                sv.wait(nc.sync)
                vt_tok = sld.inc(nc.sync.dma_start(
                    out=PS[:, vbase:vbase + 256].rearrange("p (a c) -> p a c", a=1),
                    in_=ctx.dram["scr_vtok"].rearrange("(a p) c -> p a c", p=128)
                    [:, tt:tt + 1, 256 * h:256 * (h + 1)]), 16)
                sld.wait(nc.vector, at=vt_tok)
            ctx.sems["mm"].wait(nc.vector)
            nc.vector.tensor_copy(PS[0:32, 0:256],
                                  A[pb32:pb32 + 32,
                                    8192 + tt * 256:8192 + (tt + 1) * 256])
            vv(ctx, nc.vector.tensor_copy(
                F128[0:32, 121:122],
                V32[pb32:pb32 + 32, tt * 4 + h:tt * 4 + h + 1]))
            nc.vector.tensor_scalar(PS[0:32, 256:512], PS[0:32, 0:256],
                                    F128[0:32, 121:122], None, ALU.mult)
            nc.vector.tensor_scalar(
                PS[0:32, 512:768],
                PS[pb32:pb32 + 32,
                   1312 + ((c // 4) % 2) * 256:1568 + ((c // 4) % 2) * 256],
                F128[0:32, 121:122], None, ALU.mult)
            lastcp = nc.vector.tensor_scalar(
                F128[0:32, 120:121], F128[0:32, 121:122], -1.0, None, ALU.mult)
            sv.inc(lastcp)
            sv.wait(nc.tensor)
            import os as _os5
            if _os5.environ.get("L2A_DBG_CHUNK") == f"{h}:{c}":
                print(f"[DBG] emitting chunk dump at h={h} c={c}")
                ctx.barrier()
                sdb = ctx.sem("st")
                sdb.inc(nc.sync.dma_start(out=ctx.dram["l2a_dbgc"][0:32, 0:1312],
                                          in_=PS[0:32, 0:1312]), 16)
                sdb.inc(nc.sync.dma_start(out=ctx.dram["l2a_dbgc"][0:32, 1312:1824],
                                          in_=QS[0:32, 512 * h:512 * h + 512]), 16)
                sdb.wait(nc.sync)
                ctx.barrier()
            kTc = [A[:, 4096 + j * T + qoff:4096 + j * T + qoff + 32]
                   for j in range(2)]
            qTc = [A[:, j * T + qoff:j * T + qoff + 32] for j in range(2)]
            mm_acc(ctx, pk[0:32, 0:32], [(kTc[0], kTc[0]), (kTc[1], kTc[1])],
                   sem=sm)
            sm.wait(nc.vector)
            vv(ctx, nc.vector.tensor_scalar(PS[0:32, 1024:1056], pk[0:32, 0:32],
                                            F128[0:32, 120:121], None, ALU.mult))
            sv.inc(nc.vector.tensor_tensor(tb("A"), PS[0:32, 1024:1056],
                                           mskSL, ALU.mult))
            sv.wait(nc.tensor)
            sm.inc(nc.tensor.transpose(ptA[0:32, 0:32], tb("A"), eye32))
            sm.wait(nc.vector)
            sv.inc(nc.vector.tensor_copy(tb("B"), ptA[0:32, 0:32]))
            sv.wait(nc.tensor)

            def mm32(dst, lhs_nm, rhs_nm, adds=()):
                pairs = [(tb(lhs_nm), tb(rhs_nm))]
                for en in adds:
                    pairs.append((eye32, eye32 if en == "I" else tb(en)))
                mm_acc(ctx, pk[0:32, 32:64], pairs, sem=sm)
                sm.wait(nc.vector)
                sv.inc(nc.vector.tensor_copy(tb(dst), pk[0:32, 32:64]))
                sv.wait(nc.tensor)

            mm32("A2", "B", "A")
            mm32("B2", "A", "B")
            mm32("A4", "B2", "A2")
            mm32("B4", "A2", "B2")
            mm32("A8", "B4", "A4")
            mm32("B8", "A4", "B4")
            mm32("B16", "A8", "B8")
            mm32("C1", "A", "B2", ["B", "B2", "I"])
            mm32("D1", "B2", "A", ["A", "A2", "I"])
            mm32("C2", "D1", "B4", ["C1"])
            mm32("D2", "B4", "D1", ["D1"])
            mm32("C3", "D2", "B8", ["C2"])
            mm32("D3", "B8", "D2", ["D2"])
            mm32("TT", "D3", "B16", ["C3"])

            mm_acc(ctx, pk[0:32, 64:96], [(kTc[0], qTc[0]), (kTc[1], qTc[1])],
                   sem=sm)
            sm.wait(nc.vector)
            sv.inc(nc.vector.tensor_tensor(PS[0:32, 3072:3104], pk[0:32, 64:96],
                                           mskU, ALU.mult))
            sv.wait(nc.tensor)
            lastw = None
            for dh in range(2):
                lastw = nc.tensor.matmul(
                    pk2[:, 256 + 32 * dh:256 + 32 * (dh + 1)],
                    PS[0:32, 256 + 128 * dh:256 + 128 * (dh + 1)],
                    tb("TT"), start=True, stop=True)
            sm.inc(lastw)
            sm.wait(nc.scalar)
            nc.scalar.activation(PS[:, 3584:3616], pk2[:, 256:288],
                                 AF.Identity, scale=-1.0)
            ss.inc(nc.scalar.activation(PS[:, 3616:3648], pk2[:, 288:320],
                                        AF.Identity, scale=-1.0))
            ss.wait(nc.tensor)
            import os as _os6
            _nos = _os6.environ.get("L2A_NOS")
            pairs = [(tb("TT"), PS[0:32, 512:768])]
            if not _nos:
                pairs += [(PS[:, 3584:3616], QS[:, 512 * h:512 * h + 256]),
                          (PS[:, 3616:3648], QS[:, 512 * h + 256:512 * h + 512])]
            mm_acc(ctx, pk[0:32, 128:384], pairs, sem=sm)
            sm.wait(nc.vector)
            sv.inc(nc.vector.tensor_copy(PS[0:32, 3328:3584], pk[0:32, 128:384]))
            sv.wait(nc.tensor)
            pairs = [(PS[0:32, 3072:3104], PS[0:32, 3328:3584])]
            if not _nos:
                pairs = [(qTc[0], QS[:, 512 * h:512 * h + 256]),
                         (qTc[1], QS[:, 512 * h + 256:512 * h + 512])] + pairs
            mm_acc(ctx, pk2[0:32, 0:256], pairs, sem=sm)
            lastS = None
            for dh in range(2):
                lastS = nc.tensor.matmul(
                    Spsum[:, 256 * dh:256 * (dh + 1)],
                    PS[0:32, 128 * dh:128 * (dh + 1)],
                    PS[0:32, 3328:3584], start=True, stop=True,
                    skip_group_check=True)
            sm.inc(lastS)
            sm.wait(nc.vector)
            import os as _os8
            nc.vector.wait_ge(sp.h, last_osp)
            _src_sp = {"u": PS[0:32, 3328:3584], "k": PS[0:32, 0:256],
                       "kb": PS[0:32, 256:512], "vb": PS[0:32, 512:768],
                       "att": None}.get(_os8.environ.get("L2A_SPILL", "o"))
            if _src_sp is None and _os8.environ.get("L2A_SPILL") == "att":
                nc.vector.tensor_copy(PS[0:32, 1056:1312][:, 0:32],
                                      PS[0:32, 3072:3104])
                nc.vector.tensor_copy(PS[0:32, 1088:1312],
                                      pk2[0:32, 32:256])
            elif _src_sp is not None and _os8.environ.get("L2A_SPILL") != "o":
                nc.vector.tensor_copy(PS[0:32, 1056:1312], _src_sp)
            else:
                nc.vector.tensor_copy(PS[0:32, 1056:1312], pk2[0:32, 0:256])
            if c == 0:
                nc.vector.tensor_copy(K32[:, 0:512], Spsum[:, 0:512])
            else:
                nc.vector.tensor_tensor(K32[:, 0:512], K32[:, 0:512],
                                        Spsum[:, 0:512], ALU.add)
            vspacer(ctx)
            lasts = nc.vector.tensor_copy(QS[:, 512 * h:512 * (h + 1)],
                                          K32[:, 0:512])
            sv.inc(lasts)
            sv.wait(nc.sync)
            last_osp = sp.inc(nc.sync.dma_start(
                out=ctx.dram["scr_od"].rearrange("(a p) c -> p a c", p=128)
                [pb32:pb32 + 32, tt:tt + 1, 256 * h:256 * (h + 1)],
                in_=PS[0:32, 1056:1312].rearrange("p (a c) -> p a c", a=1)), 16)
            sv.wait(nc.tensor)

    sv.wait(nc.tensor)
    ss.wait(nc.tensor)
    ctx.sems["ve"].wait(nc.gpsimd)
    ctx.sems["mm"].wait(nc.gpsimd)
    sw.inc(nc.gpsimd.dma_start(
        out=KB[:, 3584:3616].rearrange("p (a c) -> p a c", a=8),
        in_=hbm_slab(ctx, "l2a_w2t", 0, 4)), 16)
    sw.inc(nc.gpsimd.dma_start(
        out=KB[0:16, 2560:3584].rearrange("p (a c) -> p a c", a=1),
        in_=ctx.dram["l2a_w1st"].rearrange("p (a c) -> p a c", a=1)), 16)
    sldc = ctx.sem("ldx")
    sldc.inc(nc.sync.dma_start(out=K32[:, 960:968], in_=ctx.dram["l2a_b1pc"][:]), 16)
    sldc.inc(nc.sync.dma_start(out=F128[0:4, 64:68], in_=ctx.dram["l2a_ltemp4"][:]), 16)
    sldc.inc(nc.sync.dma_start(out=F128[0:4, 68:72], in_=ctx.dram["l2a_cb4"][:]), 16)
    sldc.inc(nc.sync.dma_start(out=F128[0:4, 72:73], in_=ctx.dram["l2a_b24"][:]), 16)
    sldc.inc(nc.sync.dma_start(out=F128[0:4, 73:74], in_=ctx.dram["maskD41"][:]), 16)
    sw.inc(nc.gpsimd.dma_start(out=QS[:, 3072:4096], in_=ctx.dram["l2a_onw"][:]), 16)
    sldc.wait(nc.vector)
    sldc.wait(nc.scalar)
    sw.wait(nc.vector)
    sw.wait(nc.scalar)
    sw.wait(nc.tensor)

    sv.wait(nc.vector)
    sv.inc(nc.vector.memset(K32[:, 64:192], 0.0))
    for tt in range(NTT):
        sp.wait(nc.sync)
        ldt = None
        for si, nm in enumerate(("scr_fs", "scr_fl", "scr_od", "scr_vtok")):
            ldt = sld.inc(nc.sync.dma_start(
                out=PS[:, si * 1024:(si + 1) * 1024].rearrange(
                    "p (a c) -> p a c", a=1),
                in_=ctx.dram[nm].rearrange("(a p) c -> p a c", p=128)
                [:, tt:tt + 1, :]), 16)
        sld.wait(nc.vector, at=ldt)
        sld.wait(nc.scalar, at=ldt)
        for si in range(4):
            s3 = PS[:, si * 1024:(si + 1) * 1024].rearrange(
                "p (h d) -> p h d", h=4)
            vv(ctx, nc.vector.tensor_reduce(K32[:, si * 4:si * 4 + 4], s3,
                                            AX.X, ALU.add))
            vv(ctx, nc.vector.tensor_reduce(K32[:, 16 + si * 4:16 + si * 4 + 4],
                                            s3, AX.X, ALU.add,
                                            apply_absolute_value=True))
            sssync(ctx, nc.scalar.activation(QS[:, 0:1024],
                                             PS[:, si * 1024:(si + 1) * 1024],
                                             AF.Square))
            ss.wait(nc.vector)
            vv(ctx, nc.vector.tensor_reduce(
                K32[:, 32 + si * 4:32 + si * 4 + 4],
                QS[:, 0:1024].rearrange("p (h d) -> p h d", h=4), AX.X, ALU.add))
        for si in range(4):
            for h in range(NH):
                s_ = K32[:, si * 4 + h:si * 4 + h + 1]
                a_ = K32[:, 16 + si * 4 + h:16 + si * 4 + h + 1]
                q_ = K32[:, 32 + si * 4 + h:32 + si * 4 + h + 1]
                base = 64 + h * 32 + si * 4
                nc.vector.tensor_scalar(K32[:, base:base + 1], s_, 1.0 / 256,
                                        None, ALU.mult)
                nc.vector.tensor_scalar(K32[:, base + 2:base + 3], a_, 1.0 / 256,
                                        None, ALU.mult)
        sv.inc(vspacer(ctx))
        sv.wait(nc.vector)
        for si in range(4):
            for h in range(NH):
                q_ = K32[:, 32 + si * 4 + h:32 + si * 4 + h + 1]
                base = 64 + h * 32 + si * 4
                nc.vector.tensor_tensor(K32[:, base + 1:base + 2],
                                        K32[:, base:base + 1],
                                        K32[:, base:base + 1], ALU.mult)
        sv.inc(vspacer(ctx))
        sv.wait(nc.vector)
        for si in range(4):
            for h in range(NH):
                q_ = K32[:, 32 + si * 4 + h:32 + si * 4 + h + 1]
                base = 64 + h * 32 + si * 4
                nc.vector.scalar_tensor_tensor(
                    K32[:, base + 1:base + 2], q_, 1.0 / 256,
                    K32[:, base + 1:base + 2], ALU.mult, ALU.subtract)
        sv.inc(vspacer(ctx))
        sv.wait(nc.scalar)
        for si in range(4):
            for h in range(NH):
                q_ = K32[:, 32 + si * 4 + h:32 + si * 4 + h + 1]
                base = 64 + h * 32 + si * 4
                nc.scalar.activation(K32[:, base + 3:base + 4], q_, AF.Sqrt)
        ss.inc(sspacer(ctx))
        ss.wait(nc.vector)
        sv.inc(nc.vector.tensor_copy(QS[0:128, 1024:1152], K32[:, 64:192]))
        sv.wait(nc.tensor)
        sm.inc(nc.tensor.transpose(ptA[:, 0:128], QS[:, 1024:1152], eyebf[:]))
        sm.wait(nc.vector)
        lastst = None
        for h in range(NH):
            lastst = nc.vector.tensor_copy(
                C[0:16, h * 2048 + tt * 128:h * 2048 + (tt + 1) * 128],
                ptA[32 * h:32 * h + 16, 0:128])
        sv.inc(lastst)
        sv.wait(nc.tensor)
        sv.wait(nc.sync)

    for h in range(NH):
        sssync(ctx, nc.scalar.activation(F128[0:4, 74:75], F128[0:4, 64 + h:65 + h],
                                         AF.Exp))
        ss.wait(nc.vector)
        vv(ctx, nc.vector.reciprocal(F128[0:4, 75:76], F128[0:4, 74:75]))
        vv(ctx, nc.vector.scalar_tensor_tensor(
            F128[0:4, 76:77], F128[0:4, 73:74], F128[0:4, 68 + h:69 + h],
            F128[0:4, 72:73], ALU.mult, ALU.add))
        sv.inc(nc.vector.tensor_tensor(F128[0:4, 77:78], F128[0:4, 76:77],
                                       F128[0:4, 75:76], ALU.mult))
        sv.wait(nc.scalar)
        sv.wait(nc.tensor)
        htoks = []
        for o in range(8):
            vw, tw, sw_ = ring.load(hbm_slab(ctx, "l2a_w1hs", o * 128,
                                             (o + 1) * 128))
            for sl in range(4):
                pb, pid = (pk, 0) if len(htoks) % 2 == 0 else (pk2, 1)
                nc.tensor.wait_ge(sw.h, tw)
                for q, tk in htoks[-2:]:
                    if q == pid:
                        ctx.sems["se"].wait(nc.tensor, at=tk)
                pairs = [(vw(k, 0, 128),
                          B[:, k * T + sl * 512:k * T + (sl + 1) * 512])
                         for k in range(8)]
                pairs.append((KB[0:16, 2560 + o * 128:2560 + (o + 1) * 128],
                              C[0:16, h * 2048 + sl * 512:h * 2048 + (sl + 1) * 512]))
                mm_acc(ctx, pb[:, 0:512], pairs, sem=sm)
                sm.wait(nc.scalar)
                ev = nc.scalar.activation(
                    A[:, o * T + sl * 512:o * T + (sl + 1) * 512], pb[:, 0:512],
                    AF.Gelu_apprx_tanh, bias=K32[:, 960 + o:961 + o])
                htoks.append((pid, ss.inc(ev)))
            ring.mark_free(sw_, ss, ss.count)
        ss.wait(nc.tensor)
        for sl in range(4):
            pb = pk if sl % 2 == 0 else pk2
            ctx.sems["se"].wait(nc.tensor)
            mm_acc(ctx, pb[0:4, 0:512],
                   [(KB[:, 3584 + k * 4:3584 + (k + 1) * 4],
                     A[:, k * T + sl * 512:k * T + (sl + 1) * 512])
                    for k in range(8)], sem=sm)
            sm.wait(nc.scalar)
            ss.inc(nc.scalar.activation(QS[0:4, sl * 512:(sl + 1) * 512],
                                        pb[0:4, 0:512], AF.Identity,
                                        scale=F128[0:4, 75:76],
                                        bias=F128[0:4, 77:78]))
        ss.wait(nc.tensor)
        for tt in range(NTT):
            sm.inc(nc.tensor.transpose(ptB[:, 0:4],
                                       QS[0:4, tt * 128:(tt + 1) * 128],
                                       eyebf[0:4, 0:4]))
            sm.wait(nc.vector)
            vv(ctx, nc.vector.tensor_copy(F128[:, 80:84], ptB[:, 0:4]))
            vv(ctx, nc.vector.tensor_reduce(F128[:, 84:85], F128[:, 80:84],
                                            AX.X, ALU.max, negate=True))
            sv.wait(nc.scalar)
            ss.inc(nc.scalar.activation(F128[:, 85:89], F128[:, 80:84], AF.Exp,
                                        bias=F128[:, 84:85],
                                        accum_out=F128[:, 89:90]))
            ss.wait(nc.vector)
            vv(ctx, nc.vector.reciprocal(F128[:, 90:91], F128[:, 89:90]))
            vv(ctx, nc.vector.tensor_scalar(F128[:, 91:95], F128[:, 85:89],
                                            F128[:, 90:91], None, ALU.mult))
            base = 256 + tt * 16 + h * 4
            lastw = nc.vector.tensor_scalar(K32[:, base:base + 4],
                                            F128[:, 91:95], 1.0 - 4.0 * EPSF,
                                            EPSF, ALU.mult, ALU.add)
            sv.inc(lastw)
            sv.wait(nc.tensor)

    for tt in range(NTT):
        sp.wait(nc.sync)
        ldt = None
        for si, nm in enumerate(("scr_fs", "scr_fl", "scr_od", "scr_vtok")):
            ldt = sld.inc(nc.sync.dma_start(
                out=PS[:, si * 1024:(si + 1) * 1024].rearrange(
                    "p (a c) -> p a c", a=1),
                in_=ctx.dram[nm].rearrange("(a p) c -> p a c", p=128)
                [:, tt:tt + 1, :]), 16)
        sld.wait(nc.vector, at=ldt)
        for h in range(NH):
            wb_ = 256 + tt * 16 + h * 4
            acc = QS[:, 2048 + h * 256:2048 + (h + 1) * 256]
            nc.vector.tensor_scalar(acc, PS[:, h * 256:(h + 1) * 256],
                                    K32[:, wb_:wb_ + 1], None, ALU.mult)
            nc.vector.scalar_tensor_tensor(
                acc, PS[:, 1024 + h * 256:1024 + (h + 1) * 256],
                K32[:, wb_ + 1:wb_ + 2], acc, ALU.mult, ALU.add)
            nc.vector.scalar_tensor_tensor(
                acc, PS[:, 2048 + h * 256:2048 + (h + 1) * 256],
                K32[:, wb_ + 2:wb_ + 3], acc, ALU.mult, ALU.add)
            lastm = nc.vector.scalar_tensor_tensor(
                acc, PS[:, 3072 + h * 256:3072 + (h + 1) * 256],
                K32[:, wb_ + 3:wb_ + 4], acc, ALU.mult, ALU.add)
            sv.inc(lastm)
        sv.wait(nc.scalar)
        sssync(ctx, nc.scalar.activation(QS[:, 0:1024], QS[:, 2048:3072],
                                         AF.Square))
        ss.wait(nc.vector)
        vv(ctx, nc.vector.tensor_reduce(F128[:, 96:100],
                                        QS[:, 0:1024].rearrange(
                                            "p (h d) -> p h d", h=4),
                                        AX.X, ALU.add))
        vv(ctx, nc.vector.tensor_scalar(F128[:, 100:104], F128[:, 96:100],
                                        1.0 / 256, None, ALU.mult))
        sv.wait(nc.scalar)
        sssync(ctx, nc.scalar.activation(F128[:, 104:108], F128[:, 100:104],
                                         AF.Ln, bias=V32[:, 240:241]))
        sssync(ctx, nc.scalar.activation(F128[:, 108:112], F128[:, 104:108],
                                         AF.Exp, scale=-0.5))
        ss.wait(nc.vector)
        for h in range(NH):
            lastn = nc.vector.scalar_tensor_tensor(
                QS[:, 2048 + h * 256:2048 + (h + 1) * 256],
                QS[:, 2048 + h * 256:2048 + (h + 1) * 256],
                F128[:, 108 + h:109 + h],
                QS[:, 3072 + h * 256:3072 + (h + 1) * 256], ALU.mult, ALU.mult)
            sv.inc(lastn)
        sv.wait(nc.tensor)
        for g2 in range(2):
            pt = (ptA, ptB)[g2]
            lastt = None
            for j in range(4):
                lastt = nc.tensor.transpose(
                    pt[:, j * 128:(j + 1) * 128],
                    QS[:, 2048 + g2 * 512 + j * 128:2048 + g2 * 512 + (j + 1) * 128],
                    eyebf[:])
            sm.inc(lastt)
            sm.wait(nc.vector)
            lastc = None
            for j in range(4):
                ft = g2 * 4 + j
                lastc = nc.vector.tensor_copy(
                    C[:, ft * T + tt * 128:ft * T + (tt + 1) * 128],
                    pt[:, j * 128:(j + 1) * 128])
            sv.inc(lastc)
        sv.wait(nc.sync)

    evt2 = []
    it = 0
    for qtr in range(4):
        vw, tw, sw_ = ring.load(hbm_slab(ctx, "l2a_wo", qtr * 256, (qtr + 1) * 256))
        for t in range(NTT):
            pb, pid = (pk2, 1) if it % 2 == 0 else (pk, 0)
            it += 1
            nc.tensor.wait_ge(sw.h, tw)
            if it == 1:
                sv.wait(nc.tensor)
                ctx.sems["se"].wait(nc.tensor)
            for q, tk in evt2[-2:]:
                if q == pid:
                    ctx.sems["ve"].wait(nc.tensor, at=tk)
            mm_acc(ctx, pb[:, 0:256],
                   [(C[:, k * T + t * 128:k * T + (t + 1) * 128], vw(k, 0, 256))
                    for k in range(8)], sem=sm)
            sm.wait(nc.vector)
            xsl = X[:, t * D + qtr * 256:t * D + (qtr + 1) * 256]
            ev = nc.vector.tensor_tensor(xsl, xsl, pb[:, 0:256], ALU.add)
            evt2.append((pid, sv.inc(ev)))
        ring.mark_free(sw_, sv, sv.count)
    sv.wait(nc.tensor)



BLOCK_WEIGHTS = {
    "l1a": {"wq": "l1_wq", "wk": "l1_wk", "wv": "l1_wv", "wo": "l1_wo"},
}


def declare_common(ctx, consts=True):
    nc = ctx.nc
    ctx.sbuf("X", [128, NTT * D], F32)
    ctx.sbuf("A", [128, NTT * D], BF16)
    ctx.sbuf("B", [128, NFT * T], BF16)
    ctx.sbuf("C", [128, NFT * T], BF16)
    ctx.sbuf("QS", [128, 4096], BF16)
    ctx.sbuf("PS", [128, 4224], BF16)
    ctx.sbuf("W", [128, 6144], BF16)
    ctx.sbuf("S32", [128, 256], F32)
    ctx.sbuf("V32", [128, 256], F32)
    for nm in ("lnsum", "lnssq", "lnscl", "lnbia", "lntmpA", "lntmpB"):
        ctx.sbuf(nm, [128, 16], F32)
    ctx.sbuf("JNK", [128, 4], F32)
    ctx.sbuf("F128", [128, 128], F32)
    ctx.sbuf("K32", [128, 1024], F32)
    ctx.sbuf("KB", [128, 4608], BF16)
    ctx.sbuf("eye_bf", [128, 128], BF16)
    ctx.sbuf("ones_bf", [1, 512], BF16)
    ctx.sbuf("maskA", [128, 128], F32)
    ctx.sbuf("maskB", [128, 128], F32)
    ctx.sbuf("maskWA", [128, 128], F32)
    ctx.sbuf("maskWB", [128, 128], F32)
    ctx.psum_t("pk", [128, 512], F32)
    ctx.psum_t("pk2", [128, 512], F32)
    ctx.psum_t("ps0", [128, 512], F32)
    ctx.psum_t("ps1", [128, 512], F32)
    ctx.psum_t("ps2", [128, 512], F32)
    ctx.psum_t("ps3", [128, 512], F32)
    ctx.psum_t("ptA", [128, 1024], BF16)
    ctx.psum_t("ptB", [128, 1024], BF16)


def emit_prologue(ctx):
    nc = ctx.nc
    s = ctx.sem("ldx")
    s.inc(nc.sync.dma_start(
        out=ctx.ar["X"][:].rearrange("p (n d) -> p n d", n=NTT),
        in_=ctx.dram["x"].rearrange("(n p) d -> p n d", p=128)), 16)
    for nm in ("maskA", "maskB", "maskWA", "maskWB"):
        s.inc(nc.sync.dma_start(out=ctx.ar[nm][:], in_=ctx.dram[nm][:]), 16)

    sg = ctx.sem("wld")
    sg.inc(nc.gpsimd.dma_start(out=ctx.ar["eye_bf"][:], in_=ctx.dram["eye"][:]), 16)
    sg.inc(nc.gpsimd.dma_start(out=ctx.ar["ones_bf"][:],
                               in_=ctx.dram["ones_row"][:, 0:512]), 16)
    nc.vector.memset(ctx.ar["V32"][:, 240:241], 1e-5)
    nc.vector.memset(ctx.ar["V32"][:, 241:242], 1e-6)
    nc.vector.memset(ctx.ar["JNK"][:], 0.0)
    for eng in (nc.tensor, nc.vector, nc.scalar, nc.gpsimd, nc.sync):
        s.wait(eng)
        sg.wait(eng)


def emit_epilogue(ctx):
    nc = ctx.nc
    st = ctx.sem("st")
    sv = ctx.sem("ve")
    ctx.barrier()
    st.inc(nc.sync.dma_start(
        out=ctx.dram["y"].rearrange("(n p) d -> p n d", p=128),
        in_=ctx.ar["X"][:].rearrange("p (n d) -> p n d", n=NTT)), 16)
    st.wait(nc.sync)


def build_program(block_list, dram_specs):
    nc = bass.Bass("TRN2", target_bir_lowering=False,
                   detect_race_conditions=False)
    ctx = Ctx(nc)
    for name, shape, dt, kind in dram_specs:
        ctx.dram[name] = nc.dram_tensor(name, list(shape), dt, kind=kind).ap()
    if "l1f" in block_list:
        ctx.dram["scr_h"] = nc.dram_tensor("scr_h", [4096, T], BF16).ap()
    if "l0f" in block_list:
        ctx.dram["scr_x"] = nc.dram_tensor("scr_x", [T, D], F32).ap()
    if "l2a" in block_list:
        for nm in ("scr_qT", "scr_kT"):
            ctx.dram[nm] = nc.dram_tensor(nm, [1024, T], BF16).ap()
        for nm in ("scr_vtok", "scr_fs", "scr_fl", "scr_od"):
            ctx.dram[nm] = nc.dram_tensor(nm, [T, D], BF16).ap()
        ctx.dram["scr_row"] = nc.dram_tensor("scr_row", [16, 128], BF16).ap()
        import os as _os4
        if _os4.environ.get("L2A_DBG_CHUNK"):
            ctx.dram["l2a_dbgc"] = nc.dram_tensor("l2a_dbgc", [32, 1824], BF16,
                                                  kind="ExternalOutput").ap()
            ctx.dram["l2a_dbgc2"] = nc.dram_tensor("l2a_dbgc2", [128, 512], BF16,
                                                   kind="ExternalOutput").ap()
    import os as _os
    if "l3f" in block_list and _os.environ.get("L3F_DEBUG"):
        ctx.dram["l3f_dbg"] = nc.dram_tensor("l3f_dbg", [128, 224], F32,
                                             kind="ExternalOutput").ap()
    if "l3f" in block_list and _os.environ.get("L3F_DEBUG") == "3":
        ctx.dram["l3f_dbg3"] = nc.dram_tensor("l3f_dbg3", [128, 1280], F32,
                                              kind="ExternalOutput").ap()
    if "l3f" in block_list and _os.environ.get("L3F_DEBUG") == "2":
        ctx.dram["l3f_dbg2"] = nc.dram_tensor("l3f_dbg2", [128, 2048], BF16,
                                              kind="ExternalOutput").ap()
    declare_common(ctx)
    emit_prologue(ctx)
    for b in block_list:
        import os as _os
        _sa = _os.environ.get("MHA_STOP") or None
        if b == "l0a":
            emit_mha(ctx, {"wqkv": "l0a_wqkv", "wo": "l0a_wo",
                           "causal": True, "scale": 0.125})
        elif b == "l1a":
            emit_mha(ctx, {"wq": "l1_wq", "wk": "l1_wk", "wv": "l1_wv",
                           "wo": "l1_wo", "causal": True, "scale": 0.125,
                           "stop_after": _sa})
        elif b == "l3a":
            emit_mha(ctx, {"wqkv": "l3_wqkv", "wo": "l3_wo", "causal": False,
                           "scale": 0.125, "bq": "l3_bq", "bk": "l3_bk",
                           "bv": "l3_bv", "bo": "l3_bo"})
        elif b == "l2f":
            emit_l2f(ctx)
        elif b == "l1f":
            emit_l1f(ctx)
        elif b == "l3f":
            emit_l3f(ctx)
        elif b == "l0f":
            emit_l0f(ctx)
        elif b == "l2a":
            emit_l2a(ctx)
        else:
            raise NotImplementedError(b)
        ctx.barrier()
    emit_epilogue(ctx)
    return nc, ctx



def prep_weights(inp):
    d = {}
    wqkv = np.asarray(inp["L0_attn_w_qkv"])
    d["l0a_wqkv"] = np.ascontiguousarray(
        wqkv.reshape(1024, 16, 3, 64).transpose(0, 2, 1, 3).reshape(1024, 3072))
    d["l0a_wo"] = np.ascontiguousarray(
        np.asarray(inp["L0_attn_w_out"]).reshape(1024, 1024))
    d["l1_wq"] = _t(inp["L1_attn_q_heads"])
    d["l1_wk"] = _t(inp["L1_attn_k_heads"])
    d["l1_wv"] = _t(inp["L1_attn_v_heads"])
    d["l1_wo"] = _t(inp["L1_attn_output_proj"])
    wq0 = np.asarray(inp["L0_ffn_wqkv"], dtype=np.float32)
    d["l0f_wqkv"] = np.ascontiguousarray(wq0)
    d["l0f_wo"] = np.ascontiguousarray(np.asarray(inp["L0_ffn_wo"], np.float32))
    w1 = np.asarray(inp["L0_ffn_w1"], np.float32)
    w1a = np.zeros((2, 1024, 2816), np.float32)
    w1g = np.zeros((2, 1024, 2816), np.float32)
    w1a[:, :, :2730] = w1[:, :, :2730]
    w1g[:, :, :2730] = w1[:, :, 2730:]
    d["l0f_w1a"] = w1a
    d["l0f_w1g"] = w1g
    w2 = np.asarray(inp["L0_ffn_w2"], np.float32)
    w2p = np.zeros((2, 2816, 1024), np.float32)
    w2p[:, :2730, :] = w2
    d["l0f_w2"] = w2p
    b1 = np.asarray(inp["L0_ffn_b1"], np.float32)
    b1a = np.zeros((2, 2816), np.float32); b1a[:, :2730] = b1[:, :2730]
    b1g = np.zeros((2, 2816), np.float32); b1g[:, :2730] = b1[:, 2730:]
    d["l0f_b1a"] = np.concatenate(
        [_pcols(b1a[0]), _pcols(b1a[1])], axis=1)
    d["l0f_b1g"] = np.concatenate(
        [_pcols(b1g[0]), _pcols(b1g[1])], axis=1)
    d["l0f_b2"] = np.asarray(inp["L0_ffn_b2"], np.float32).reshape(1, 2048)
    lg = np.asarray(inp["L0_ffn_ln_g"], np.float32)
    lb = np.asarray(inp["L0_ffn_ln_b"], np.float32)
    d["l0f_lgb"] = np.concatenate(
        [_pcols(lg[0]), _pcols(lg[1]), _pcols(lb[0]), _pcols(lb[1])], axis=1)
    qs = np.asarray(inp["L0_ffn_q_scale"], np.float32)
    ks = np.asarray(inp["L0_ffn_k_scale"], np.float32)
    d["l0f_qks"] = np.ascontiguousarray(np.stack(
        [np.tile(ks[0], 2), np.tile(ks[1], 2),
         np.tile(qs[0], 2), np.tile(qs[1], 2)], axis=1))
    DH, WIN, NSEQ = 64, 512, 2048
    half = np.arange(0, DH, 2, dtype=np.float64)
    inv_freq = 1.0 / 10000.0 ** (half / DH)
    tpos = np.arange(NSEQ, dtype=np.float64)
    fr = tpos[:, None] * inv_freq[None, :]
    fr = np.concatenate([fr, fr], axis=-1)
    base = (half + 0.4 * DH) / (1.4 * DH)
    power = (tpos - NSEQ // 2) / float(WIN // 2)
    scale = base[None, :] ** power[:, None]
    scale = np.concatenate([scale, scale], axis=-1)
    cos, sin = np.cos(fr), np.sin(fr)
    qc = (cos * scale).astype(np.float32).T
    qsn = (sin * scale).astype(np.float32).T
    kc = (cos / scale).astype(np.float32).T
    ksn = (sin / scale).astype(np.float32).T
    d["rotqc"] = np.ascontiguousarray(np.concatenate([qc, qc], axis=0))
    d["rotqs"] = np.ascontiguousarray(np.concatenate([qsn, qsn], axis=0))
    d["rotkc"] = np.ascontiguousarray(np.concatenate([kc, kc], axis=0))
    d["rotks"] = np.ascontiguousarray(np.concatenate([ksn, ksn], axis=0))
    def _chpc(w, taps):
        w = np.asarray(w, np.float32).reshape(1024, taps)
        return np.ascontiguousarray(
            w.reshape(8, 128, taps).transpose(1, 0, 2).reshape(128, 8 * taps))
    d["l2a_wq"] = _t(inp["L2_attn_Wq"])
    d["l2a_wk"] = _t(inp["L2_attn_Wk"])
    d["l2a_wv"] = _t(inp["L2_attn_Wv"])
    d["l2a_wb"] = _t(inp["L2_attn_Wb"])
    d["l2a_cq"] = _chpc(inp["L2_attn_conv_q_w"], 4)
    d["l2a_ck"] = _chpc(inp["L2_attn_conv_k_w"], 4)
    d["l2a_cv"] = _chpc(inp["L2_attn_conv_v_w"], 4)
    d["l2a_firs"] = _chpc(inp["L2_attn_fir_short_filt"], 5)
    d["l2a_firl"] = _chpc(inp["L2_attn_fir_long_filt"], 64)
    w1t = _t(inp["L2_attn_gate_W1"])
    d["l2a_w1hs"] = np.ascontiguousarray(w1t[0:1024])
    d["l2a_w1st"] = np.ascontiguousarray(w1t[1024:1040])
    d["l2a_w2t"] = _t(inp["L2_attn_gate_W2"])
    d["l2a_b1pc"] = _pcols(inp["L2_attn_gate_b1"])
    d["l2a_ltemp4"] = np.ascontiguousarray(np.broadcast_to(
        np.asarray(inp["L2_attn_gate_log_temp"], np.float32)[None, :], (4, 4)))
    d["l2a_cb4"] = np.ascontiguousarray(np.broadcast_to(
        np.asarray(inp["L2_attn_gate_copy_bias"], np.float32)[None, :], (4, 4)))
    d["l2a_b24"] = np.asarray(inp["L2_attn_gate_b2"], np.float32).reshape(4, 1)
    d["l2a_onw"] = np.ascontiguousarray(np.broadcast_to(
        np.tile(np.asarray(inp["L2_attn_o_norm_w"], np.float32), 4)[None, :],
        (128, 1024)))
    d["l2a_wo"] = _t(inp["L2_attn_Wo"])
    d["l1f_w1"] = _t(inp["L1_ffn_w1_w"])
    d["l1f_w2"] = _t(inp["L1_ffn_w2_w"])
    d["l1f_h1"] = _t(inp["L1_ffn_h1_w"])
    d["l1f_h2"] = _t(inp["L1_ffn_h2_w"])
    d["l1f_w3"] = _t(inp["L1_ffn_w3_w"])
    d["l1f_s1"] = _pcols(inp["L1_ffn_w1_s"])
    d["l1f_s2"] = _pcols(inp["L1_ffn_w2_s"])
    d["l1f_sh1"] = _pcols(inp["L1_ffn_h1_s"])
    d["l1f_sh2"] = _pcols(inp["L1_ffn_h2_s"])
    d["l1f_s3b"] = np.ascontiguousarray(np.broadcast_to(
        np.asarray(inp["L1_ffn_w3_s"], dtype=np.float32)[None, :], (128, 1024)))
    d["l3f_proj"] = _t(inp["L3_ffn_proj_w"])
    d["l3f_anchT"] = _t(inp["L3_ffn_anchors"])
    d["l3f_temp"] = np.ascontiguousarray(np.broadcast_to(
        np.asarray(inp["L3_ffn_routing_temp"], dtype=np.float32).reshape(1, 1),
        (128, 1)))
    d["l3f_w1"] = np.ascontiguousarray(np.asarray(inp["L3_ffn_w1"]))
    d["l3f_w2"] = np.ascontiguousarray(np.asarray(inp["L3_ffn_w2"]))
    d["l2f_wg"] = _t(inp["L2_ffn_w_gate"])
    d["l2f_wu"] = _t(inp["L2_ffn_w_up"])
    d["l2f_wd"] = _t(inp["L2_ffn_w_down"])
    d["l3_wqkv"] = np.ascontiguousarray(np.concatenate(
        [np.asarray(inp["L3_attn_Wq"]), np.asarray(inp["L3_attn_Wk"]),
         np.asarray(inp["L3_attn_Wv"])], axis=1))
    d["l3_wo"] = np.ascontiguousarray(np.asarray(inp["L3_attn_Wo"]))
    for bn, k in (("l3_bq", "L3_attn_bq"), ("l3_bk", "L3_attn_bk"),
                  ("l3_bv", "L3_attn_bv"), ("l3_bo", "L3_attn_bo")):
        d[bn] = np.asarray(inp[k]).reshape(1, 1024)
    return d


def run_blocks(block_list, x, weights, consts):
    dram_specs = [("x", (T, D), F32, "ExternalInput"),
                  ("y", (T, D), F32, "ExternalOutput")]
    need = set()
    for b in block_list:
        import os as _os
        _sa = _os.environ.get("MHA_STOP") or None
        if b == "l0a":
            need |= {"l0a_wqkv", "l0a_wo"}
        elif b == "l1a":
            need |= {"l1_wq", "l1_wk", "l1_wv", "l1_wo"}
        elif b == "l3a":
            need |= {"l3_wqkv", "l3_wo", "l3_bq", "l3_bk", "l3_bv", "l3_bo"}
        elif b == "l2f":
            need |= {"l2f_wg", "l2f_wu", "l2f_wd"}
        elif b == "l1f":
            need |= {"l1f_w1", "l1f_w2", "l1f_h1", "l1f_h2", "l1f_w3",
                     "l1f_s1", "l1f_s2", "l1f_sh1", "l1f_sh2", "l1f_s3b"}
        elif b == "l3f":
            need |= {"l3f_proj", "l3f_anchT", "l3f_temp", "l3f_w1", "l3f_w2"}
        elif b == "l0f":
            need |= {"l0f_wqkv", "l0f_wo", "l0f_w1a", "l0f_w1g", "l0f_w2",
                     "l0f_b1a", "l0f_b1g", "l0f_b2", "l0f_lgb", "l0f_qks",
                     "rotqc", "rotqs", "rotkc", "rotks"}
        elif b == "l2a":
            need |= {"l2a_wq", "l2a_wk", "l2a_wv", "l2a_wb", "l2a_cq", "l2a_ck",
                     "l2a_cv", "l2a_firs", "l2a_firl", "l2a_w1hs", "l2a_w1st",
                     "l2a_w2t", "l2a_b1pc", "l2a_ltemp4", "l2a_cb4", "l2a_b24",
                     "l2a_onw", "l2a_wo"}
    for nm in sorted(need):
        dt = I32 if weights[nm].dtype == np.int32 else F32
        dram_specs.append((nm, weights[nm].shape, dt, "ExternalInput"))
    for nm in ("eye", "ones_row", "maskA", "maskB", "maskWA", "maskWB",
               "hsel", "hselT", "onescol", "masks32", "maskD41"):
        dram_specs.append((nm, consts[nm].shape, F32, "ExternalInput"))
    nc, ctx = build_program(block_list, dram_specs)
    in_maps = []
    for b in range(x.shape[0]):
        m = {"x": np.ascontiguousarray(x[b])}
        for nm in sorted(need):
            m[nm] = weights[nm]
        for nm in ("eye", "ones_row", "maskA", "maskB", "maskWA", "maskWB",
                   "hsel", "hselT", "onescol", "masks32", "maskD41"):
            m[nm] = consts[nm]
        in_maps.append(m)
    res = run_bass_kernel_spmd(nc, in_maps, core_ids=list(range(x.shape[0])))
    ctx.close()
    import os as _os
    if _os.environ.get("L3F_DEBUG") and "l3f_dbg" in res.results[0]:
        np.save("/tmp/l3f_dbg.npy", res.results[0]["l3f_dbg"])
    if "l2a_dbgc" in res.results[0]:
        np.save("/tmp/l2a_dbgc.npy", res.results[0]["l2a_dbgc"].astype(np.float32))
    if "l2a_dbgc2" in res.results[0]:
        np.save("/tmp/l2a_dbgc2.npy", res.results[0]["l2a_dbgc2"].astype(np.float32))
    if "l3f_dbg3" in res.results[0]:
        np.save("/tmp/l3f_dbg3.npy", res.results[0]["l3f_dbg3"])
    if "l3f_dbg2" in res.results[0]:
        np.save("/tmp/l3f_dbg2.npy", res.results[0]["l3f_dbg2"].astype(np.float32))
    return np.stack([r["y"] for r in res.results], axis=0)


ALL_BLOCKS = ["l0a", "l0f", "l1a", "l1f", "l2a", "l2f", "l3a", "l3f"]


def kernel(**inputs):
    x = np.ascontiguousarray(np.asarray(inputs["L0_attn_x"], dtype=np.float32))
    weights = prep_weights(inputs)
    consts = host_consts()
    groups = [["l0a"], ["l0f"], ["l1a"], ["l1f"], ["l2a"], ["l2f"],
              ["l3a"], ["l3f"]]
    for group in groups:
        x = run_blocks(group, x, weights, consts)
    return x


# version 02
# speedup vs baseline: 1.0147x; 1.0147x over previous
import numpy as np
import ml_dtypes
import concourse.bass as bass
import concourse.mybir as mybir
from concourse.bass_utils import run_bass_kernel_spmd

F32 = mybir.dt.float32
BF16 = mybir.dt.bfloat16
I32 = mybir.dt.int32
AF = mybir.ActivationFunctionType
ALU = mybir.AluOpType
AX = mybir.AxisListType
BF = ml_dtypes.bfloat16

T = 2048
D = 1024
NTT = T // 128
NFT = D // 128
NEG = -1.0e30



def _t(a):
    return np.ascontiguousarray(np.asarray(a).T)


def _pcols(v):
    v = np.asarray(v).reshape(-1)
    n = v.shape[0]
    nt = (n + 127) // 128
    out = np.zeros((nt, 128), v.dtype)
    out.reshape(-1)[:n] = v
    return np.ascontiguousarray(out.T)


def host_consts():
    d = {}
    d["eye"] = np.eye(128, dtype=np.float32)
    d["ones_row"] = np.ones((1, 2048), dtype=np.float32)
    r = np.arange(128)[:, None]
    j = np.arange(128)[None, :]
    d["maskA"] = np.where(j <= (r % 64), 0.0, NEG).astype(np.float32)
    d["maskB"] = np.where(j <= 64 + (r % 64), 0.0, NEG).astype(np.float32)
    d["maskWA"] = np.where(j >= (r % 64), 0.0, NEG).astype(np.float32)
    d["maskWB"] = np.where(j >= 64 + (r % 64), 0.0, NEG).astype(np.float32)
    m32 = np.zeros((32, 64), np.float32)
    r32 = np.arange(32)[:, None]
    c32 = np.arange(32)[None, :]
    m32[:, 0:32] = (c32 < r32).astype(np.float32)
    m32[:, 32:64] = (c32 >= r32).astype(np.float32)
    d["masks32"] = m32
    d["maskD41"] = np.array([[0.0], [0.0], [0.0], [1.0 - 1.0 / 3000.0]],
                            np.float32)
    hs = np.zeros((128, 2), np.float32)
    hs[0:64, 0] = 1.0
    hs[64:128, 1] = 1.0
    d["hsel"] = hs
    d["hselT"] = np.ascontiguousarray(hs.T)
    d["onescol"] = np.ones((128, 1), np.float32)
    return d



class Sem:
    def __init__(self, h):
        self.h = h
        self.count = 0

    def inc(self, instr, n=1):
        instr.then_inc(self.h, n)
        self.count += n
        return self.count

    def wait(self, eng, at=None):
        eng.wait_ge(self.h, self.count if at is None else at)


class Ctx:
    def __init__(self, nc):
        self.nc = nc
        self._stack = []
        self.sems = {}
        self.psum = {}
        self.ar = {}
        self.dram = {}

    def sem(self, name):
        if name not in self.sems:
            h = self._enter(self.nc.semaphore(name))
            self.sems[name] = Sem(h)
        return self.sems[name]

    def _enter(self, cm):
        v = cm.__enter__()
        self._stack.append(cm)
        return v

    def sbuf(self, name, shape, dtype):
        t = self._enter(self.nc.sbuf_tensor("sb_" + name, shape, dtype))
        self.ar[name] = t
        return t

    def psum_t(self, name, shape, dtype):
        t = self._enter(self.nc.psum_tensor(name, shape, dtype))
        self.psum[name] = t
        return t

    def close(self):
        for cm in reversed(self._stack):
            cm.__exit__(None, None, None)

    def barrier(self):
        self.nc.all_engine_barrier()


def vv(ctx, instr):
    sv = ctx.sem("ve")
    sv.inc(instr)
    sv.wait(ctx.nc.vector)
    return instr


def sssync(ctx, instr):
    ss = ctx.sem("se")
    ss.inc(instr)
    ss.wait(ctx.nc.scalar)
    return instr


def vspacer(ctx):
    return ctx.nc.vector.memset(ctx.ar["JNK"][:, 0:1], 0.0)


def sspacer(ctx):
    return ctx.nc.scalar.activation(ctx.ar["JNK"][:, 2:3],
                                    ctx.ar["JNK"][:, 1:2], AF.Identity)


def mm_acc(ctx, out_ap, pairs, sem=None, extra=()):
    nc = ctx.nc
    allp = list(pairs) + list(extra)
    n = len(allp)
    last = None
    for i, p in enumerate(allp):
        tp = p[2] if len(p) > 2 else None
        last = nc.tensor.matmul(out_ap, p[0], p[1], start=(i == 0),
                                stop=(i == n - 1), tile_position=tp)
    if sem is not None:
        sem.inc(last)
    return last


class WRing:

    def __init__(self, ctx, arena="W", nslots=4, cols=256, kt=8):
        self.ctx = ctx
        self.nslots = nslots
        self.cols = cols
        self.kt = kt
        self.W = ctx.ar[arena]
        self.sem = ctx.sem("wld")
        self.free_toks = [None] * nslots
        self.i = 0

    def load(self, hbm_ap):
        ctx, nc = self.ctx, self.ctx.nc
        slot = self.i % self.nslots
        self.i += 1
        base = slot * self.kt * self.cols
        ncols = hbm_ap.shape[-1]
        kt = hbm_ap.shape[1]
        assert kt * ncols <= self.kt * self.cols
        dst = self.W[:, base:base + kt * ncols].rearrange("p (a c) -> p a c", a=kt)
        ft = self.free_toks[slot]
        if ft is not None:
            nc.gpsimd.wait_ge(ft[0].h, ft[1])
        tok = self.sem.inc(nc.gpsimd.dma_start(out=dst, in_=hbm_ap), 16)

        def view(k, c0=0, c1=None):
            c1 = ncols if c1 is None else c1
            return self.W[:, base + k * ncols + c0: base + k * ncols + c1]
        return view, tok, slot

    def mark_free(self, slot, sem, count):
        self.free_toks[slot] = (sem, count)


def hbm_slab(ctx, name, c0, c1, a0=0, a1=None):
    t = ctx.dram[name]
    r = t.rearrange("(a p) c -> p a c", p=128)
    if a1 is None:
        a1 = r.shape[1]
    return r[:, a0:a1, c0:c1]



def emit_ln(ctx, x_tensor, out_tensor, n_tiles=NTT, width=D, eps_col=240, c0=0):
    nc = ctx.nc
    sv, ss = ctx.sem("ve"), ctx.sem("se")
    nt = n_tiles
    sums = ctx.ar["lnsum"][:, c0:c0 + nt]
    ssqs = ctx.ar["lnssq"][:, c0:c0 + nt]
    scl = ctx.ar["lnscl"][:, c0:c0 + nt]
    bia = ctx.ar["lnbia"][:, c0:c0 + nt]
    tmpA = ctx.ar["lntmpA"][:, c0:c0 + nt]
    tmpB = ctx.ar["lntmpB"][:, c0:c0 + nt]
    lastq = lastr = None
    for t in range(nt):
        xt = x_tensor[:, t * width:(t + 1) * width]
        lastq = nc.scalar.activation(out_tensor[:, t * width:(t + 1) * width], xt,
                                     AF.Square, accum_out=ssqs[:, t:t + 1])
        lastr = nc.vector.tensor_reduce(sums[:, t:t + 1], xt, AX.X, ALU.add)
    sv.inc(lastr)
    ss.inc(lastq)
    sv.wait(nc.vector)
    ss.wait(nc.vector)
    vv(ctx, nc.vector.tensor_tensor(tmpA[:], sums[:], sums[:], ALU.mult))
    vv(ctx, nc.vector.tensor_scalar(ssqs[:], ssqs[:], 1.0 / width, None, ALU.mult))
    vv(ctx, nc.vector.tensor_scalar(tmpA[:], tmpA[:], 1.0 / (width * width),
                                    None, ALU.mult))
    vv(ctx, nc.vector.tensor_scalar(sums[:], sums[:], 1.0 / width, None, ALU.mult))
    sv.inc(nc.vector.tensor_tensor(tmpB[:], ssqs[:], tmpA[:], ALU.subtract))
    sv.wait(nc.vector)
    sv.wait(nc.scalar)
    sssync(ctx, nc.scalar.activation(tmpB[:], tmpB[:], AF.Ln,
                                     bias=ctx.ar["V32"][:, eps_col:eps_col + 1]))
    ss.inc(nc.scalar.activation(scl[:], tmpB[:], AF.Exp, scale=-0.5))
    ss.wait(nc.vector)
    sv.inc(nc.vector.scalar_tensor_tensor(bia[:], sums[:], -1.0, scl[:],
                                          ALU.mult, ALU.mult))
    sv.wait(nc.vector)
    sv.wait(nc.scalar)
    last = None
    for t in range(nt):
        xt = x_tensor[:, t * width:(t + 1) * width]
        last = nc.scalar.activation(out_tensor[:, t * width:(t + 1) * width], xt,
                                    AF.Identity, scale=scl[:, t:t + 1],
                                    bias=bia[:, t:t + 1])
    ss.inc(last)
    return scl, bia


def emit_transpose(ctx, src, dst, n_tok_tiles=NTT, n_feat_tiles=NFT,
                   src_width=None, dst_row=None, shift=None):
    nc = ctx.nc
    if src_width is None:
        src_width = n_feat_tiles * 128
    if dst_row is None:
        dst_row = n_tok_tiles * 128
    sm, sv = ctx.sem("mm"), ctx.sem("ve")
    eye = ctx.ar["eye_bf"]
    pts = [ctx.psum["ptA"], ctx.psum["ptB"]]
    ctx.sems["se"].wait(nc.tensor)
    ctx.sems["ve"].wait(nc.tensor)
    evac_tok = {}
    gi = 0
    for o in range(n_feat_tiles):
        for half in range(0, n_tok_tiles, 8):
            pt = pts[gi % 2]
            nh = min(8, n_tok_tiles - half)
            if gi - 2 in evac_tok:
                ctx.sems["ve"].wait(nc.tensor, at=evac_tok[gi - 2])
            lastt = None
            for j in range(nh):
                t = half + j
                lastt = nc.tensor.transpose(
                    pt[:, j * 128:(j + 1) * 128],
                    src[:, t * src_width + o * 128: t * src_width + o * 128 + 128],
                    eye[:])
            sm.inc(lastt)
            sm.wait(nc.vector)
            ev = nc.vector.tensor_copy(
                dst[:, o * dst_row + half * 128: o * dst_row + (half + nh) * 128],
                pt[:, 0:nh * 128])
            evac_tok[gi] = sv.inc(ev)
            gi += 1
    return sv.count



def emit_mha(ctx, cfg):
    nc = ctx.nc
    X, A, B, C, QS, PS = (ctx.ar[k] for k in ("X", "A", "B", "C", "QS", "PS"))
    S32, KB = ctx.ar["S32"], ctx.ar["KB"]
    sv, ss, sm, sw = ctx.sem("ve"), ctx.sem("se"), ctx.sem("mm"), ctx.sem("wld")
    causal = cfg["causal"]
    scale = cfg["scale"]
    has_bias = "bq" in cfg
    ones = ctx.ar["ones_bf"]
    eyebf = ctx.ar["eye_bf"]
    maskA, maskB = ctx.ar["maskA"], ctx.ar["maskB"]
    pk, pk2 = ctx.psum["pk"], ctx.psum["pk2"]
    ps_sc = [ctx.psum["ps0"], ctx.psum["ps1"], ctx.psum["ps2"], ctx.psum["ps3"]]

    stop_after = cfg.get("stop_after")
    emit_ln(ctx, X, A)
    if stop_after == "ln":
        return
    emit_transpose(ctx, A, B)
    tr_done = sv.count
    if stop_after == "tr":
        return

    ring = WRing(ctx, nslots=3, cols=256)

    def qkv_ap(which, c0, c1):
        if "wqkv" in cfg:
            return hbm_slab(ctx, cfg["wqkv"], which * D + c0, which * D + c1)
        return hbm_slab(ctx, cfg[("wq", "wk", "wv")[which]], c0, c1)

    if has_bias:
        for i, bn in enumerate(("bq", "bk", "bv", "bo")):
            sw.inc(nc.gpsimd.dma_start(
                out=KB[0:1, 512 + i * 1024:512 + (i + 1) * 1024],
                in_=ctx.dram[cfg[bn]][:]), 16)

    def brow(i, c0, c1):
        return KB[0:1, 512 + i * 1024 + c0:512 + i * 1024 + c1]

    evt = []

    def proj_guard(pid):
        for q, tk in evt[-2:]:
            if q == pid:
                ctx.sems["ve"].wait(nc.tensor, at=tk)

    ctx.sems["ve"].wait(nc.tensor, at=tr_done)
    it = 0
    import os as _os
    _kn = int(_os.environ.get("MHA_KN", "8"))
    for o in range(_kn):
        view, tok, slot = ring.load(qkv_ap(1, o * 128, (o + 1) * 128))
        for ts in range(4):
            pb, pid = (pk, 0) if it % 2 == 0 else (pk2, 1)
            it += 1
            nc.tensor.wait_ge(sw.h, tok)
            proj_guard(pid)
            extra = [(brow(1, o * 128, (o + 1) * 128), ones[0:1, 0:512])] \
                if has_bias else []
            mm_acc(ctx, pb[:, 0:512],
                   [(view(k, 0, 128), B[:, k * T + ts * 512:k * T + (ts + 1) * 512])
                    for k in range(8)], sem=sm, extra=extra)
            sm.wait(nc.vector)
            ev = nc.vector.tensor_copy(C[:, o * T + ts * 512:o * T + (ts + 1) * 512],
                                       pb[:, 0:512])
            evt.append((pid, sv.inc(ev)))
        ring.mark_free(slot, sv, sv.count)

    for qtr in range(4):
        view, tok, slot = ring.load(qkv_ap(2, qtr * 256, (qtr + 1) * 256))
        for t in range(NTT):
            pb, pid = (pk, 0) if it % 2 == 0 else (pk2, 1)
            it += 1
            nc.tensor.wait_ge(sw.h, tok)
            proj_guard(pid)
            extra = [(ones[0:1, 0:128], brow(2, qtr * 256, (qtr + 1) * 256))] \
                if has_bias else []
            mm_acc(ctx, pb[:, 0:256],
                   [(B[:, k * T + t * 128:k * T + (t + 1) * 128], view(k, 0, 256))
                    for k in range(8)], sem=sm, extra=extra)
            sm.wait(nc.vector)
            ev = nc.vector.tensor_copy(
                A[:, t * D + qtr * 256:t * D + (qtr + 1) * 256], pb[:, 0:256])
            evt.append((pid, sv.inc(ev)))
        ring.mark_free(slot, sv, sv.count)

    if stop_after == "v":
        return
    for g in range(4):
        qs_toks = []
        sv.wait(nc.tensor)
        for o in range(8):
            view, tok, slot = ring.load(qkv_ap(0, o * 128, (o + 1) * 128))
            pb, pid = (pk, 0) if it % 2 == 0 else (pk2, 1)
            it += 1
            nc.tensor.wait_ge(sw.h, tok)
            for q, tk in qs_toks[-2:]:
                if q == pid:
                    ctx.sems["se"].wait(nc.tensor, at=tk)
            extra = [(brow(0, o * 128, (o + 1) * 128), ones[0:1, 0:512])] \
                if has_bias else []
            mm_acc(ctx, pb[:, 0:512],
                   [(view(k, 0, 128), B[:, k * T + g * 512:k * T + (g + 1) * 512])
                    for k in range(8)], sem=sm, extra=extra)
            sm.wait(nc.scalar)
            ev = nc.scalar.activation(QS[:, o * 512:(o + 1) * 512], pb[:, 0:512],
                                      AF.Identity, scale=scale)
            qs_toks.append((pid, ss.inc(ev)))
            ring.mark_free(slot, ss, ss.count)
        ss.wait(nc.tensor)
        if stop_after == "q":
            continue

        for p in range(8):
            for c in range(8):
                qbase = g * 512 + c * 64
                kv_end = (qbase + 64) if causal else T
                nkv = (kv_end + 127) // 128
                nslab = (nkv + 3) // 4
                sv.wait(nc.tensor)
                lastmm = None
                for s in range(nslab):
                    ncols = min(512, nkv * 128 - s * 512)
                    qa = QS[0:64, p * 512 + c * 64:p * 512 + (c + 1) * 64]
                    qb = QS[64:128, p * 512 + c * 64:p * 512 + (c + 1) * 64]
                    ka = C[0:64, p * T + s * 512:p * T + s * 512 + ncols]
                    kb = C[64:128, p * T + s * 512:p * T + s * 512 + ncols]
                    nc.tensor.matmul(ps_sc[s][0:64, 0:ncols], qa, ka,
                                     start=True, stop=True, tile_position=(0, 0))
                    lastmm = nc.tensor.matmul(ps_sc[s][64:128, 0:ncols], qb, kb,
                                              start=True, stop=True,
                                              tile_position=(64, 64))
                sm.inc(lastmm)
                sm.wait(nc.vector)
                if causal:
                    dt = nkv - 1
                    sl, off = dt // 4, (dt % 4) * 128
                    mk = maskA if (qbase % 128) == 0 else maskB
                    nc.vector.tensor_tensor(ps_sc[sl][:, off:off + 128],
                                            ps_sc[sl][:, off:off + 128],
                                            mk[:], ALU.add)
                    vspacer(ctx)
                nm = S32[:, 0:4]
                lastv = None
                for s in range(nslab):
                    ncols = min(512, nkv * 128 - s * 512)
                    lastv = nc.vector.tensor_reduce(nm[:, s:s + 1],
                                                    ps_sc[s][:, 0:ncols],
                                                    AX.X, ALU.max, negate=True)
                sv.inc(lastv)
                sv.wait(nc.vector)
                lastv = nc.vector.tensor_reduce(S32[:, 8:9], nm[:, 0:nslab], AX.X,
                                                ALU.min)
                sv.inc(lastv)
                sv.wait(nc.scalar)
                sm.wait(nc.scalar)
                laste = None
                for s in range(nslab):
                    ncols = min(512, nkv * 128 - s * 512)
                    laste = nc.scalar.activation(
                        PS[:, s * 512:s * 512 + ncols], ps_sc[s][:, 0:ncols],
                        AF.Exp, bias=S32[:, 8:9], accum_out=S32[:, 12 + s:13 + s])
                ss.inc(laste)
                ss.wait(nc.vector)
                vv(ctx, nc.vector.tensor_reduce(S32[:, 9:10], S32[:, 12:12 + nslab],
                                                AX.X, ALU.add))
                vv(ctx, nc.vector.reciprocal(S32[:, 11:12], S32[:, 9:10]))
                lastd = nc.vector.tensor_scalar(PS[:, 2048:2176], eyebf[:],
                                                S32[:, 11:12], None, ALU.mult)
                sv.inc(lastd)
                ss.wait(nc.tensor)
                sv.wait(nc.tensor)
                lastt = None
                for kt in range(nkv):
                    lastt = nc.tensor.matmul(
                        ps_sc[kt // 4][:, (kt % 4) * 128:(kt % 4) * 128 + 128],
                        PS[:, kt * 128:(kt + 1) * 128], PS[:, 2048:2176],
                        start=True, stop=True)
                sm.inc(lastt)
                sm.wait(nc.vector)
                lastev = None
                for s in range(nslab):
                    ncols = min(512, nkv * 128 - s * 512)
                    lastev = nc.vector.tensor_copy(
                        PS[:, 2176 + s * 512:2176 + s * 512 + ncols],
                        ps_sc[s][:, 0:ncols])
                sv.inc(lastev)
                sv.wait(nc.tensor)
                lastpv = None
                for kt in range(nkv):
                    va = A[:, kt * D + p * 128:kt * D + p * 128 + 64]
                    vb = A[:, kt * D + p * 128 + 64:kt * D + p * 128 + 128]
                    nc.tensor.matmul(pk[0:64, 0:64], va,
                                     PS[:, 2176 + kt * 128:2176 + kt * 128 + 64],
                                     start=(kt == 0), stop=(kt == nkv - 1),
                                     tile_position=(0, 0), skip_group_check=True)
                    lastpv = nc.tensor.matmul(
                        pk[64:128, 0:64], vb,
                        PS[:, 2176 + kt * 128 + 64:2176 + kt * 128 + 128],
                        start=(kt == 0), stop=(kt == nkv - 1),
                        tile_position=(0, 64), skip_group_check=True)
                sm.inc(lastpv)
                sm.wait(nc.vector)
                ev = nc.vector.tensor_copy(B[:, p * T + qbase:p * T + qbase + 64],
                                           pk[:, 0:64])
                sv.inc(ev)

    if stop_after in ("q", "attn"):
        return
    evt = []
    it = 0
    for qtr in range(4):
        view, tok, slot = ring.load(hbm_slab(ctx, cfg["wo"], qtr * 256,
                                             (qtr + 1) * 256))
        for t in range(NTT):
            pb, pid = (pk2, 1) if it % 2 == 0 else (pk, 0)
            it += 1
            nc.tensor.wait_ge(sw.h, tok)
            if it == 1:
                sv.wait(nc.tensor)
            proj_guard(pid)
            extra = [(ones[0:1, 0:128], brow(3, qtr * 256, (qtr + 1) * 256))] \
                if has_bias else []
            mm_acc(ctx, pb[:, 0:256],
                   [(B[:, k * T + t * 128:k * T + (t + 1) * 128], view(k, 0, 256))
                    for k in range(8)], sem=sm, extra=extra)
            sm.wait(nc.vector)
            ev = nc.vector.tensor_tensor(
                X[:, t * D + qtr * 256:t * D + (qtr + 1) * 256],
                X[:, t * D + qtr * 256:t * D + (qtr + 1) * 256],
                pb[:, 0:256], ALU.add)
            evt.append((pid, sv.inc(ev)))
        ring.mark_free(slot, sv, sv.count)



def emit_l2f(ctx):
    nc = ctx.nc
    X, A, B, C, QS = (ctx.ar[k] for k in ("X", "A", "B", "C", "QS"))
    sv, ss, sm, sw = ctx.sem("ve"), ctx.sem("se"), ctx.sem("mm"), ctx.sem("wld")
    pk, pk2 = ctx.psum["pk"], ctx.psum["pk2"]
    S = 1.0 / 127.0

    emit_ln(ctx, X, A)
    emit_transpose(ctx, A, B)
    tr_done = sv.count
    ctx.sems["ve"].wait(nc.tensor, at=tr_done)

    ring = WRing(ctx, nslots=3, cols=256)
    for hc in range(3):
        last_stt = 0
        for ph in range(4):
            c0 = hc * 1024 + ph * 256
            vg, tg, sg_ = ring.load(hbm_slab(ctx, "l2f_wg", c0, c0 + 256))
            vu, tu, su_ = ring.load(hbm_slab(ctx, "l2f_wu", c0, c0 + 256))
            for so in range(2):
                ho = ph * 2 + so
                for ts in range(4):
                    nc.tensor.wait_ge(sw.h, tg)
                    ctx.sems["se"].wait(nc.tensor)
                    mm_acc(ctx, pk[:, 0:512],
                           [(vg(k, so * 128, so * 128 + 128),
                             B[:, k * T + ts * 512:k * T + (ts + 1) * 512])
                            for k in range(8)], sem=sm)
                    sm.wait(nc.scalar)
                    ctx.sems["ve"].wait(nc.scalar, at=last_stt)
                    ss.inc(nc.scalar.activation(QS[:, 0:512], pk[:, 0:512],
                                                AF.Silu, scale=S))
                    nc.tensor.wait_ge(sw.h, tu)
                    ctx.sems["ve"].wait(nc.tensor)
                    mm_acc(ctx, pk2[:, 0:512],
                           [(vu(k, so * 128, so * 128 + 128),
                             B[:, k * T + ts * 512:k * T + (ts + 1) * 512])
                            for k in range(8)], sem=sm)
                    sm.wait(nc.vector)
                    ss.wait(nc.vector)
                    ev = nc.vector.scalar_tensor_tensor(
                        C[:, ho * T + ts * 512:ho * T + (ts + 1) * 512],
                        pk2[:, 0:512], S, QS[:, 0:512], ALU.mult, ALU.mult)
                    last_stt = sv.inc(ev)
            ring.mark_free(sg_, sv, sv.count)
            ring.mark_free(su_, sv, sv.count)
        dn_toks = []
        it = 0
        for qtr in range(4):
            vd, td, sd_ = ring.load(hbm_slab(ctx, "l2f_wd", qtr * 256,
                                             (qtr + 1) * 256, a0=8 * hc,
                                             a1=8 * hc + 8))
            for t in range(NTT):
                pb, pid = (pk, 0) if it % 2 == 0 else (pk2, 1)
                it += 1
                nc.tensor.wait_ge(sw.h, td)
                if it <= 2:
                    sv.wait(nc.tensor)
                    ctx.sems["se"].wait(nc.tensor)
                for q, tk in dn_toks[-2:]:
                    if q == pid:
                        ctx.sems["ve"].wait(nc.tensor, at=tk)
                mm_acc(ctx, pb[:, 0:256],
                       [(C[:, k * T + t * 128:k * T + (t + 1) * 128],
                         vd(k, 0, 256)) for k in range(8)], sem=sm)
                sm.wait(nc.vector)
                ev = nc.vector.scalar_tensor_tensor(
                    X[:, t * D + qtr * 256:t * D + (qtr + 1) * 256],
                    pb[:, 0:256], S,
                    X[:, t * D + qtr * 256:t * D + (qtr + 1) * 256],
                    ALU.mult, ALU.add)
                dn_toks.append((pid, sv.inc(ev)))
            ring.mark_free(sd_, sv, sv.count)



def emit_l1f(ctx):
    nc = ctx.nc
    X, A, B, C, QS, K32 = (ctx.ar[k] for k in ("X", "A", "B", "C", "QS", "K32"))
    sv, ss, sm, sw = ctx.sem("ve"), ctx.sem("se"), ctx.sem("mm"), ctx.sem("wld")
    sp = ctx.sem("spill")
    pk, pk2 = ctx.psum["pk"], ctx.psum["pk2"]
    scr = ctx.dram["scr_h"]

    emit_ln(ctx, X, A)
    emit_transpose(ctx, A, B)
    ctx.sems["ve"].wait(nc.tensor)

    sld = ctx.sem("ldx")
    for i, nm in enumerate(("l1f_s1", "l1f_s2", "l1f_sh1", "l1f_sh2")):
        sld.inc(nc.sync.dma_start(out=K32[:, i * 32:(i + 1) * 32],
                                  in_=ctx.dram[nm][:]), 16)
    sld.inc(nc.sync.dma_start(out=ctx.ar["S32"][:, 0:128],
                              in_=ctx.dram["l1f_s3b"][:, 0:128]), 16)
    sld.wait(nc.vector)
    sld.wait(nc.scalar)

    ring = WRing(ctx, nslots=3, cols=256)
    sc = lambda i, o: K32[:, i * 32 + o:i * 32 + o + 1]

    last_stt = 0
    last_spill = [0, 0]
    for ob in range(16):
        v1, t1, s1_ = ring.load(hbm_slab(ctx, "l1f_w1", ob * 256, (ob + 1) * 256))
        v2, t2, s2_ = ring.load(hbm_slab(ctx, "l1f_w2", ob * 256, (ob + 1) * 256))
        buf = 2 + (ob % 2) * 2
        for so in range(2):
            go = ob * 2 + so
            for ts in range(4):
                nc.tensor.wait_ge(sw.h, t1)
                ctx.sems["se"].wait(nc.tensor)
                mm_acc(ctx, pk[:, 0:512],
                       [(v1(k, so * 128, so * 128 + 128),
                         B[:, k * T + ts * 512:k * T + (ts + 1) * 512])
                        for k in range(8)], sem=sm)
                sm.wait(nc.scalar)
                ctx.sems["ve"].wait(nc.scalar, at=last_stt)
                ss.inc(nc.scalar.activation(
                    C[:, so * T + ts * 512:so * T + (ts + 1) * 512],
                    pk[:, 0:512], AF.Silu, scale=sc(0, go)))
                nc.tensor.wait_ge(sw.h, t2)
                ctx.sems["ve"].wait(nc.tensor)
                mm_acc(ctx, pk2[:, 0:512],
                       [(v2(k, so * 128, so * 128 + 128),
                         B[:, k * T + ts * 512:k * T + (ts + 1) * 512])
                        for k in range(8)], sem=sm)
                sm.wait(nc.vector)
                ss.wait(nc.vector)
                ctx.sems["spill"].wait(nc.vector, at=last_spill[ob % 2])
                ev = nc.vector.scalar_tensor_tensor(
                    C[:, (buf + so) * T + ts * 512:(buf + so) * T + (ts + 1) * 512],
                    pk2[:, 0:512], sc(1, go),
                    C[:, so * T + ts * 512:so * T + (ts + 1) * 512],
                    ALU.mult, ALU.mult)
                last_stt = sv.inc(ev)
        ring.mark_free(s1_, sv, sv.count)
        ring.mark_free(s2_, sv, sv.count)
        sv.wait(nc.sync)
        last_spill[ob % 2] = sp.inc(nc.sync.dma_start(
            out=scr.rearrange("(a p) c -> p a c", p=128)[:, 2 * ob:2 * ob + 2, :],
            in_=C[:, buf * T:(buf + 2) * T].rearrange("p (a c) -> p a c", a=2)), 16)

    sldh = ctx.sem("ldh")
    for ch in range(4):
        c0 = ch * 512
        sp.wait(nc.sync)
        if ch == 0:
            ctx.sems["ve"].wait(nc.sync)
        ldtok = sldh.inc(nc.sync.dma_start(
            out=A[:].rearrange("p (a c) -> p a c", a=32),
            in_=scr.rearrange("(a p) c -> p a c", p=128)[:, :, c0:c0 + 512]), 16)

        def big_mm(dst, dst_w, src, src_w, wname, scol, func):
            toks = []
            for o in range(32):
                va, ta, sa_ = ring.load(hbm_slab(ctx, wname, o * 128,
                                                 (o + 1) * 128, a0=0, a1=16))
                vb, tb, sb_ = ring.load(hbm_slab(ctx, wname, o * 128,
                                                 (o + 1) * 128, a0=16, a1=32))
                pb, pid = (pk, 0) if o % 2 == 0 else (pk2, 1)
                nc.tensor.wait_ge(sw.h, tb)
                nc.tensor.wait_ge(sldh.h, ldtok)
                for q, tk in toks[-2:]:
                    if q == pid:
                        ctx.sems["se"].wait(nc.tensor, at=tk)
                pairs = [(va(k), src[:, k * src_w:k * src_w + 512]) for k in range(16)] +                         [(vb(k), src[:, (16 + k) * src_w:(16 + k) * src_w + 512]) for k in range(16)]
                mm_acc(ctx, pb[:, 0:512], pairs, sem=sm)
                sm.wait(nc.scalar)
                ev = nc.scalar.activation(dst[:, o * dst_w:o * dst_w + 512],
                                          pb[:, 0:512], func, scale=scol(o))
                toks.append((pid, ss.inc(ev)))
                ring.mark_free(sa_, ss, ss.count)
                ring.mark_free(sb_, ss, ss.count)

        big_mm(C, 512, A, 512, "l1f_h1", lambda o: sc(2, o), AF.Gelu)
        ss.wait(nc.tensor)
        big_mm(A, 512, C, 512, "l1f_h2", lambda o: sc(3, o), AF.Gelu)
        ss.wait(nc.tensor)
        ytoks = []
        it = 0
        for oq in range(8):
            vw, tw, sw_ = ring.load(hbm_slab(ctx, "l1f_w3", oq * 128,
                                             (oq + 1) * 128, a0=0, a1=16))
            vw2, tw2, sw2_ = ring.load(hbm_slab(ctx, "l1f_w3", oq * 128,
                                                (oq + 1) * 128, a0=16, a1=32))
            if len(ytoks) >= 8:
                nc.sync.wait_ge(sv.h, ytoks[-8][1] if False else ytoks[-5][1])
            s3tok = sld.inc(nc.sync.dma_start(
                out=ctx.ar["S32"][:, (oq % 2) * 128:(oq % 2) * 128 + 128],
                in_=ctx.dram["l1f_s3b"][:, oq * 128:(oq + 1) * 128]), 16)
            for tt in range(4):
                pb, pid = (pk, 0) if it % 2 == 0 else (pk2, 1)
                it += 1
                nc.tensor.wait_ge(sw.h, tw2)
                for q, tk in ytoks[-2:]:
                    if q == pid:
                        ctx.sems["ve"].wait(nc.tensor, at=tk)
                pairs = [(A[:, k * 512 + tt * 128:k * 512 + tt * 128 + 128],
                          vw(k)) for k in range(16)] +                         [(A[:, (16 + k) * 512 + tt * 128:(16 + k) * 512 + tt * 128 + 128],
                          vw2(k)) for k in range(16)]
                mm_acc(ctx, pb[:, 0:128], pairs, sem=sm)
                sm.wait(nc.vector)
                sld.wait(nc.vector, at=s3tok)
                gt = (c0 + tt * 128)
                xsl = X[:, (gt // 128) * D + oq * 128:(gt // 128) * D + (oq + 1) * 128]
                nc.vector.tensor_tensor(ctx.ar["F128"][:],
                                        pb[:, 0:128],
                                        ctx.ar["S32"][:, (oq % 2) * 128:(oq % 2) * 128 + 128],
                                        ALU.mult)
                ev = nc.vector.tensor_tensor(xsl, xsl, ctx.ar["F128"][:], ALU.add)
                ytoks.append((pid, sv.inc(ev)))
            ring.mark_free(sw_, sv, sv.count)
            ring.mark_free(sw2_, sv, sv.count)
        sv.wait(nc.sync)
        sv.wait(nc.tensor)
        ss.wait(nc.tensor)



def emit_l3f(ctx):
    nc = ctx.nc
    X, A, B, C, QS, S32, KB, F128 = (ctx.ar[k] for k in
                                     ("X", "A", "B", "C", "QS", "S32", "KB", "F128"))
    V32 = ctx.ar["V32"]
    sv, ss, sm, sw = ctx.sem("ve"), ctx.sem("se"), ctx.sem("mm"), ctx.sem("wld")
    sld = ctx.sem("ldx")
    pk, pk2 = ctx.psum["pk"], ctx.psum["pk2"]
    ptA = ctx.psum["ptA"]
    eyebf = ctx.ar["eye_bf"]
    ones = ctx.ar["ones_bf"]

    emit_ln(ctx, X, A)
    emit_transpose(ctx, A, B)
    ctx.sems["ve"].wait(nc.tensor)

    sw.inc(nc.gpsimd.dma_start(
        out=KB[:, 0:48].rearrange("p (a c) -> p a c", a=8),
        in_=hbm_slab(ctx, "l3f_proj", 0, 6)), 16)
    K32 = ctx.ar["K32"]
    sld.inc(nc.sync.dma_start(out=K32[0:6, 128:134], in_=ctx.dram["l3f_anchT"][:]), 16)
    sld.inc(nc.sync.dma_start(out=K32[:, 256:384], in_=ctx.dram["eye"][:]), 16)
    sld.inc(nc.sync.dma_start(out=V32[:, 245:246], in_=ctx.dram["l3f_temp"][:]), 16)
    sld.wait(nc.vector)
    vv(ctx, nc.vector.tensor_scalar(V32[:, 246:247], V32[:, 245:246], 0.1, None,
                                    ALU.max))
    vv(ctx, nc.vector.reciprocal(V32[:, 247:248], V32[:, 246:247]))
    sv.inc(nc.vector.tensor_scalar(F128[:, 10:11], V32[:, 247:248], 0.5, None,
                                   ALU.mult))
    sv.wait(nc.vector)
    sv.wait(nc.tensor)
    sw.wait(nc.tensor)

    ss.wait(nc.tensor)
    for t in range(NTT):
        ss.wait(nc.tensor)
        import os as _os2
        _dbg3 = _os2.environ.get("L3F_DEBUG") == "3"
        if _dbg3:
            sv.wait(nc.tensor)
        mm_acc(ctx, pk[:, 0:6],
               [(B[:, k * T + t * 128:k * T + (t + 1) * 128],
                 KB[:, k * 6:(k + 1) * 6]) for k in range(8)], sem=sm)
        if _dbg3:
            sm.wait(nc.vector)
            sv.inc(nc.vector.tensor_copy(K32[:, 640 + t * 6:646 + t * 6],
                                         pk[:, 0:6]))
        sm.wait(nc.scalar)
        ss.inc(nc.scalar.activation(F128[:, 64:70], pk[:, 0:6], AF.Tanh,
                                    scale=1.0 / 0.3))
        ss.wait(nc.vector)
        nc.vector.memset(F128[:, 70:72], 0.0)
        sv.inc(vspacer(ctx))
        ss.wait(nc.tensor)
        sv.wait(nc.tensor)
        sm.inc(nc.tensor.transpose(ctx.psum["ps0"][0:8, 0:128], F128[:, 64:72],
                                   K32[:, 256:384]))
        sm.wait(nc.vector)
        sv.inc(nc.vector.tensor_copy(K32[0:8, 0:128], ctx.psum["ps0"][0:8, 0:128]))
        sv.wait(nc.tensor)
        mm_acc(ctx, pk2[:, 0:6],
               [(K32[0:6, 0:128], K32[0:6, 128:134])], sem=sm)
        sm.wait(nc.vector)
        if _dbg3:
            nc.vector.tensor_copy(S32[:, 96 + t * 6:102 + t * 6], pk2[:, 0:6])
            sv.inc(nc.vector.tensor_copy(K32[:, 512 + t * 6:518 + t * 6],
                                         F128[:, 64:70]))
        d6 = pk2[:, 0:6]
        m1 = F128[:, 8:9]
        vv(ctx, nc.vector.tensor_reduce(m1, d6, AX.X, ALU.max))
        sv.inc(nc.vector.scalar_tensor_tensor(F128[:, 11:12], m1, -1.0,
                                              F128[:, 10:11], ALU.mult, ALU.mult))
        sv.wait(nc.vector)
        sv.wait(nc.scalar)
        ss.inc(nc.scalar.activation(F128[:, 40:46], d6, AF.Exp,
                                    scale=F128[:, 10:11], bias=F128[:, 11:12]))
        ss.wait(nc.vector)
        E = F128[:, 40:46]
        vv(ctx, nc.vector.tensor_reduce(F128[:, 15:16], E, AX.X, ALU.max))
        vv(ctx, nc.vector.tensor_scalar(F128[:, 16:22], E, F128[:, 15:16], None,
                                        ALU.is_equal))
        vv(ctx, nc.vector.scalar_tensor_tensor(F128[:, 24:30], F128[:, 16:22],
                                               NEG, E, ALU.mult, ALU.add))
        vv(ctx, nc.vector.tensor_reduce(F128[:, 9:10], F128[:, 24:30], AX.X,
                                        ALU.max))
        vv(ctx, nc.vector.tensor_scalar(F128[:, 13:14], F128[:, 9:10], 1.0,
                                        None, ALU.add))
        vv(ctx, nc.vector.reciprocal(F128[:, 14:15], F128[:, 13:14]))
        vv(ctx, nc.vector.tensor_scalar(F128[:, 48:54], E, F128[:, 9:10], None,
                                        ALU.is_ge))
        lw = nc.vector.scalar_tensor_tensor(S32[:, t * 6:t * 6 + 6], E,
                                            F128[:, 14:15], F128[:, 48:54],
                                            ALU.mult, ALU.mult)
        sv.inc(lw)
        sv.wait(nc.vector)
        sv.wait(nc.tensor)
    import os as _os
    if _os.environ.get("L3F_DEBUG"):
        dbg = ctx.dram["l3f_dbg"]
        sdb = ctx.sem("st")
        sv.wait(nc.sync)
        sdb.inc(nc.sync.dma_start(out=dbg[:, 0:96], in_=S32[:, 0:96]), 16)
        sdb.inc(nc.sync.dma_start(out=dbg[:, 96:224], in_=F128[:, 0:128]), 16)
        sdb.wait(nc.sync)

    import os as _os
    _ne = int(_os.environ.get("L3F_NE", "6"))
    ring = WRing(ctx, nslots=3, cols=256)
    for e in range(_ne):
        etoks = []
        for o in range(4):
            vw, tw, sl_ = ring.load(hbm_slab3(ctx, "l3f_w1", e, o * 128,
                                              (o + 1) * 128))
            for ts in range(4):
                nc.tensor.wait_ge(sw.h, tw)
                for q, tk in etoks[-2:]:
                    if q == 0:
                        ctx.sems["se"].wait(nc.tensor, at=tk)
                mm_acc(ctx, pk[:, 0:512],
                       [(vw(k, 0, 128), B[:, k * T + ts * 512:k * T + (ts + 1) * 512])
                        for k in range(8)], sem=sm)
                sm.wait(nc.scalar)
                ev = nc.scalar.activation(C[:, o * T + ts * 512:o * T + (ts + 1) * 512],
                                          pk[:, 0:512], AF.Gelu_apprx_tanh)
                etoks.append((0, ss.inc(ev)))
            ring.mark_free(sl_, ss, ss.count)
        import os as _os
        if _os.environ.get("L3F_PART") == "h":
            continue
        if _os.environ.get("L3F_DEBUG") == "3" and e == _ne - 1:
            sdb = ctx.sem("st")
            sv.wait(nc.sync)
            sdb.inc(nc.sync.dma_start(out=ctx.dram["l3f_dbg3"][:, 0:1024],
                                      in_=K32[:, 0:1024]), 16)
            sdb.inc(nc.sync.dma_start(out=ctx.dram["l3f_dbg3"][:, 1024:1280],
                                      in_=S32[:, 0:256]), 16)
            sdb.wait(nc.sync)
        if _os.environ.get("L3F_DEBUG") == "2" and e == 0:
            sdb = ctx.sem("st")
            ss.wait(nc.sync)
            sdb.inc(nc.sync.dma_start(out=ctx.dram["l3f_dbg2"][:],
                                      in_=C[:, 0:2048]), 16)
            sdb.wait(nc.sync)
        ss.wait(nc.tensor)
        ytoks = []
        it = 0
        for oq in range(4):
            vw, tw, sl_ = ring.load(hbm_slab3(ctx, "l3f_w2", e, oq * 256,
                                              (oq + 1) * 256, kt=4))
            for t in range(NTT):
                pb, pid = (pk2, 1) if it % 2 == 0 else (pk, 0)
                it += 1
                nc.tensor.wait_ge(sw.h, tw)
                for q, tk in ytoks[-2:]:
                    if q == pid:
                        ctx.sems["ve"].wait(nc.tensor, at=tk)
                mm_acc(ctx, pb[:, 0:256],
                       [(C[:, k * T + t * 128:k * T + (t + 1) * 128],
                         vw(k, 0, 256)) for k in range(4)], sem=sm)
                sm.wait(nc.vector)
                xsl = X[:, t * D + oq * 256:t * D + (oq + 1) * 256]
                ev = nc.vector.scalar_tensor_tensor(
                    xsl, pb[:, 0:256], S32[:, t * 6 + e:t * 6 + e + 1], xsl,
                    ALU.mult, ALU.add)
                ytoks.append((pid, sv.inc(ev)))
            ring.mark_free(sl_, sv, sv.count)
        sv.wait(nc.tensor)


def hbm_slab3(ctx, name, e, c0, c1, kt=8, a0=0, a1=None):
    t = ctx.dram[name]
    r = t.rearrange("e (a p) c -> e p a c", p=128)
    if a1 is None:
        a1 = r.shape[2]
    return r[e][:, a0:a1, c0:c1]



def emit_l0f(ctx):
    nc = ctx.nc
    X, A, B, C, QS, PS, S32, KB, K32, F128 = (ctx.ar[k] for k in
        ("X", "A", "B", "C", "QS", "PS", "S32", "KB", "K32", "F128"))
    V32 = ctx.ar["V32"]
    sv, ss, sm, sw = ctx.sem("ve"), ctx.sem("se"), ctx.sem("mm"), ctx.sem("wld")
    sp, sld = ctx.sem("spill"), ctx.sem("ldx")
    pk, pk2 = ctx.psum["pk"], ctx.psum["pk2"]
    ps_sc = [ctx.psum["ps0"], ctx.psum["ps1"], ctx.psum["ps2"], ctx.psum["ps3"]]
    eyebf = ctx.ar["eye_bf"]
    ones = ctx.ar["ones_bf"]
    maskA, maskB = ctx.ar["maskA"], ctx.ar["maskB"]
    maskWA, maskWB = ctx.ar["maskWA"], ctx.ar["maskWB"]
    scr = ctx.dram["scr_x"]

    sv.wait(nc.sync)
    ctx.sems["se"].wait(nc.sync)
    sp.inc(nc.sync.dma_start(out=scr.rearrange("(n p) d -> p n d", p=128),
                             in_=X[:].rearrange("p (n d) -> p n d", n=NTT)), 16)
    sp.wait(nc.scalar)
    sp.wait(nc.vector)
    scl, bia = emit_ln(ctx, X, A)
    lastf = None
    for t in range(NTT):
        lastf = nc.scalar.activation(X[:, t * D:(t + 1) * D], X[:, t * D:(t + 1) * D],
                                     AF.Identity, scale=scl[:, t:t + 1],
                                     bias=bia[:, t:t + 1])
    ss.inc(lastf)

    sw.inc(nc.gpsimd.dma_start(out=KB[:, 56:57], in_=ctx.dram["onescol"][:]), 16)
    sw.inc(nc.gpsimd.dma_start(out=KB[:, 58:60], in_=ctx.dram["hsel"][:]), 16)
    sw.inc(nc.gpsimd.dma_start(out=KB[0:2, 64:192], in_=ctx.dram["hselT"][:]), 16)
    sld.inc(nc.sync.dma_start(out=S32[:, 216:220], in_=ctx.dram["l0f_qks"][:]), 16)
    sld.inc(nc.sync.dma_start(out=S32[:, 184:216], in_=ctx.dram["l0f_lgb"][:]), 16)
    sld.inc(nc.sync.dma_start(out=S32[:, 96:140], in_=ctx.dram["l0f_b1a"][:]), 16)
    sld.inc(nc.sync.dma_start(out=S32[:, 140:184], in_=ctx.dram["l0f_b1g"][:]), 16)
    sw.inc(nc.gpsimd.dma_start(out=KB[0:1, 512:2560], in_=ctx.dram["l0f_b2"][:]), 16)
    sld.wait(nc.vector)
    sv.inc(nc.vector.tensor_scalar(S32[:, 220:222], S32[:, 218:220], 8.0, None,
                                   ALU.mult))

    ring = WRing(ctx, nslots=3, cols=256)

    import os as _os
    _stop = _os.environ.get("L0F_STOP", "full")
    _nl = 0 if _stop == "ln" else (1 if _stop in ("attn0", "layer0") else 2)
    for l in range(_nl):
        if l == 1:
            sv.wait(nc.scalar)
            ctx.sems["wld"].wait(nc.scalar)
            lastc = None
            for t in range(NTT):
                lastc = nc.scalar.activation(A[:, t * D:(t + 1) * D],
                                             X[:, t * D:(t + 1) * D], AF.Identity)
            ss.inc(lastc)
        emit_transpose(ctx, A, B)
        sv.wait(nc.vector)
        lastsh = None
        for ft in range(8):
            if ft < 4:
                lastsh = nc.vector.tensor_copy(C[:, ft * T:(ft + 1) * T],
                                               B[:, ft * T:(ft + 1) * T])
            else:
                nc.vector.memset(C[:, ft * T:ft * T + 1], 0.0)
                lastsh = nc.vector.tensor_copy(C[:, ft * T + 1:(ft + 1) * T],
                                               B[:, ft * T:(ft + 1) * T - 1])
        sv.inc(lastsh)
        sv.wait(nc.tensor)

        evt = []
        for which in range(2):
            for half in range(2):
                c0 = which * 512 + half * 256
                vw, tw, sl_ = ring.load(hbm_slab3(ctx, "l0f_wqkv", l, c0, c0 + 256))
                for so in range(2):
                    o = half * 2 + so
                    dst = (which * 4 + o) * T
                    for ts in range(4):
                        pb, pid = (pk, 0) if (len(evt)) % 2 == 0 else (pk2, 1)
                        nc.tensor.wait_ge(sw.h, tw)
                        for q, tk in evt[-2:]:
                            if q == pid:
                                ctx.sems["ve"].wait(nc.tensor, at=tk)
                        mm_acc(ctx, pb[:, 0:512],
                               [(vw(k, so * 128, so * 128 + 128),
                                 C[:, k * T + ts * 512:k * T + (ts + 1) * 512])
                                for k in range(8)], sem=sm)
                        sm.wait(nc.vector)
                        ev = nc.vector.tensor_copy(
                            B[:, dst + ts * 512:dst + (ts + 1) * 512], pb[:, 0:512])
                        evt.append((pid, sv.inc(ev)))
                ring.mark_free(sl_, sv, sv.count)
        for half in range(2):
            c0 = 1024 + half * 256
            vw, tw, sl_ = ring.load(hbm_slab3(ctx, "l0f_wqkv", l, c0, c0 + 256))
            for t in range(NTT):
                pb, pid = (pk, 0) if (len(evt)) % 2 == 0 else (pk2, 1)
                nc.tensor.wait_ge(sw.h, tw)
                for q, tk in evt[-2:]:
                    if q == pid:
                        ctx.sems["ve"].wait(nc.tensor, at=tk)
                mm_acc(ctx, pb[:, 0:256],
                       [(C[:, k * T + t * 128:k * T + (t + 1) * 128],
                         vw(k, 0, 256)) for k in range(8)], sem=sm)
                sm.wait(nc.vector)
                ev = nc.vector.tensor_copy(
                    A[:, t * 512 + half * 256:t * 512 + (half + 1) * 256],
                    pb[:, 0:256])
                evt.append((pid, sv.inc(ev)))
            ring.mark_free(sl_, sv, sv.count)
        sv.wait(nc.tensor)
        sv.wait(nc.scalar)

        for which in range(2):
            spc = S32[:, 220 + l:221 + l] if which == 0 else S32[:, 216 + l:217 + l]
            for p in range(4):
                bt = (which * 4 + p) * T
                for th in range(2):
                    for sl2 in range(2):
                        cols = th * 1024 + sl2 * 512
                        sm.wait(nc.scalar)
                        ss.inc(nc.scalar.activation(
                            QS[:, sl2 * 512:(sl2 + 1) * 512],
                            B[:, bt + cols:bt + cols + 512], AF.Square))
                        ss.wait(nc.tensor)
                        sv.wait(nc.tensor)
                        mm_acc(ctx, ps_sc[sl2][0:2, 0:512],
                               [(KB[:, 58:60], QS[:, sl2 * 512:(sl2 + 1) * 512])],
                               sem=sm)
                    sm.wait(nc.vector)
                    for sl2 in range(2):
                        vv(ctx, nc.vector.tensor_copy(
                            K32[0:2, sl2 * 512:(sl2 + 1) * 512],
                            ps_sc[sl2][0:2, 0:512]))
                    sv.wait(nc.scalar)
                    sssync(ctx, nc.scalar.activation(K32[32:34, 0:1024],
                                                     K32[0:2, 0:1024], AF.Sqrt))
                    ss.wait(nc.vector)
                    vv(ctx, nc.vector.tensor_scalar(K32[64:66, 0:1024],
                                                    K32[32:34, 0:1024], 1e-12,
                                                    None, ALU.max))
                    vv(ctx, nc.vector.reciprocal(K32[96:98, 0:1024],
                                                 K32[64:66, 0:1024]))
                    sv.inc(nc.vector.tensor_copy(QS[0:2, 1024:2048],
                                                 K32[96:98, 0:1024]))
                    sv.wait(nc.vector)
                    sv.wait(nc.tensor)
                    for sl2 in range(2):
                        mm_acc(ctx, ps_sc[2 + sl2][:, 0:512],
                               [(KB[0:2, 64:192],
                                 QS[0:2, 1024 + sl2 * 512:1024 + (sl2 + 1) * 512])],
                               sem=sm)
                    sm.wait(nc.vector)
                    for sl2 in range(2):
                        cols = th * 1024 + sl2 * 512
                        lastn = nc.vector.scalar_tensor_tensor(
                            B[:, bt + cols:bt + cols + 512],
                            B[:, bt + cols:bt + cols + 512], spc,
                            ps_sc[2 + sl2][:, 0:512], ALU.mult, ALU.mult)
                        sv.inc(lastn)
                    sv.wait(nc.tensor)

        for which in range(2):
            nm_c = "rotqc" if which == 0 else "rotkc"
            nm_s = "rotqs" if which == 0 else "rotks"
            vc, tc, slc = ring.load(ctx.dram[nm_c].rearrange("p (a c) -> p a c", a=1))
            vs_, ts_, sls = ring.load(ctx.dram[nm_s].rearrange("p (a c) -> p a c", a=1))
            nc.vector.wait_ge(sw.h, ts_)
            for p in range(4):
                bt = (which * 4 + p) * T
                nc.vector.tensor_scalar(QS[:, 2048:4096].rearrange(
                    "p (h c) -> p h c", h=2)[0:0 + 32, :, :] if False else
                    QS[0:32, 2048:4096], B[32:64, bt:bt + T], -1.0, None, ALU.mult)
                nc.vector.tensor_copy(QS[32:64, 2048:4096], B[0:32, bt:bt + T])
                nc.vector.tensor_scalar(QS[64:96, 2048:4096], B[96:128, bt:bt + T],
                                        -1.0, None, ALU.mult)
                nc.vector.tensor_copy(QS[96:128, 2048:4096], B[64:96, bt:bt + T])
                nc.vector.tensor_tensor(QS[:, 0:2048], B[:, bt:bt + T],
                                        vc(0, 0, 2048), ALU.mult)
                nc.vector.tensor_tensor(QS[:, 2048:4096], QS[:, 2048:4096],
                                        vs_(0, 0, 2048), ALU.mult)
                lastr = nc.vector.tensor_tensor(B[:, bt:bt + T], QS[:, 0:2048],
                                                QS[:, 2048:4096], ALU.add)
                sv.inc(lastr)
            ring.mark_free(slc, sv, sv.count)
            ring.mark_free(sls, sv, sv.count)
        sv.wait(nc.tensor)

        for p in range(4):
            for c in range(32):
                qbase = c * 64
                kv0 = max(0, (qbase - 512) // 128)
                kvL = (qbase + 63) // 128
                nkv = kvL - kv0 + 1
                sv.wait(nc.tensor)
                lastmm = None
                for s in range((nkv + 3) // 4):
                    t0 = kv0 + s * 4
                    ncols = min(4, kvL + 1 - t0) * 128
                    qa = B[0:64, p * T + qbase:p * T + qbase + 64]
                    qb = B[64:128, p * T + qbase:p * T + qbase + 64]
                    ka = B[0:64, (4 + p) * T + t0 * 128:(4 + p) * T + t0 * 128 + ncols]
                    kb = B[64:128, (4 + p) * T + t0 * 128:(4 + p) * T + t0 * 128 + ncols]
                    nc.tensor.matmul(ps_sc[s][0:64, 0:ncols], qa, ka,
                                     start=True, stop=True, tile_position=(0, 0))
                    lastmm = nc.tensor.matmul(ps_sc[s][64:128, 0:ncols], qb, kb,
                                              start=True, stop=True,
                                              tile_position=(64, 64))
                sm.inc(lastmm)
                sm.wait(nc.vector)
                dloc = kvL - kv0
                mk = maskA if (qbase % 128) == 0 else maskB
                nc.vector.tensor_tensor(
                    ps_sc[dloc // 4][:, (dloc % 4) * 128:(dloc % 4) * 128 + 128],
                    ps_sc[dloc // 4][:, (dloc % 4) * 128:(dloc % 4) * 128 + 128],
                    mk[:], ALU.add)
                if qbase >= 512:
                    mw = maskWA if (qbase % 128) == 0 else maskWB
                    nc.vector.tensor_tensor(ps_sc[0][:, 0:128], ps_sc[0][:, 0:128],
                                            mw[:], ALU.add)
                sv.inc(vspacer(ctx))
                sv.wait(nc.vector)
                nm = S32[:, 0:4]
                lastv = None
                for s in range((nkv + 3) // 4):
                    ncols = min(4, kvL + 1 - (kv0 + s * 4)) * 128
                    lastv = nc.vector.tensor_reduce(nm[:, s:s + 1],
                                                    ps_sc[s][:, 0:ncols],
                                                    AX.X, ALU.max, negate=True)
                sv.inc(lastv)
                sv.wait(nc.vector)
                lastv = nc.vector.tensor_reduce(S32[:, 8:9], nm[:, 0:(nkv + 3) // 4],
                                                AX.X, ALU.min)
                sv.inc(lastv)
                sv.wait(nc.vector)
                sv.wait(nc.scalar)
                sm.wait(nc.scalar)
                laste = None
                for s in range((nkv + 3) // 4):
                    ncols = min(4, kvL + 1 - (kv0 + s * 4)) * 128
                    laste = nc.scalar.activation(
                        PS[:, s * 512:s * 512 + ncols], ps_sc[s][:, 0:ncols],
                        AF.Exp, bias=S32[:, 8:9], accum_out=S32[:, 12 + s:13 + s])
                ss.inc(laste)
                ss.wait(nc.vector)
                vv(ctx, nc.vector.tensor_reduce(S32[:, 9:10],
                                                S32[:, 12:12 + (nkv + 3) // 4],
                                                AX.X, ALU.add))
                vv(ctx, nc.vector.reciprocal(S32[:, 11:12], S32[:, 9:10]))
                lastd = nc.vector.tensor_scalar(PS[:, 2048:2176], eyebf[:],
                                                S32[:, 11:12], None, ALU.mult)
                sv.inc(lastd)
                ss.wait(nc.tensor)
                sv.wait(nc.tensor)
                lastt = None
                for kt in range(nkv):
                    lastt = nc.tensor.matmul(
                        ps_sc[kt // 4][:, (kt % 4) * 128:(kt % 4) * 128 + 128],
                        PS[:, kt * 128:(kt + 1) * 128], PS[:, 2048:2176],
                        start=True, stop=True)
                sm.inc(lastt)
                sm.wait(nc.vector)
                lastev = None
                for s in range((nkv + 3) // 4):
                    ncols = min(4, kvL + 1 - (kv0 + s * 4)) * 128
                    lastev = nc.vector.tensor_copy(
                        PS[:, 2176 + s * 512:2176 + s * 512 + ncols],
                        ps_sc[s][:, 0:ncols])
                sv.inc(lastev)
                sv.wait(nc.tensor)
                lastpv = None
                for kt in range(nkv):
                    gt = kv0 + kt
                    va = A[:, gt * 512 + p * 128:gt * 512 + p * 128 + 64]
                    vb = A[:, gt * 512 + p * 128 + 64:gt * 512 + p * 128 + 128]
                    nc.tensor.matmul(pk[0:64, 0:64], va,
                                     PS[:, 2176 + kt * 128:2176 + kt * 128 + 64],
                                     start=(kt == 0), stop=(kt == nkv - 1),
                                     tile_position=(0, 0), skip_group_check=True)
                    lastpv = nc.tensor.matmul(
                        pk[64:128, 0:64], vb,
                        PS[:, 2176 + kt * 128 + 64:2176 + kt * 128 + 128],
                        start=(kt == 0), stop=(kt == nkv - 1),
                        tile_position=(0, 64), skip_group_check=True)
                sm.inc(lastpv)
                sm.wait(nc.vector)
                ev = nc.vector.tensor_copy(
                    A[:, 8192 + p * T + qbase:8192 + p * T + qbase + 64],
                    pk[:, 0:64])
                sv.inc(ev)

        evt = []
        it = 0
        for qtr in range(4):
            vw, tw, sl_ = ring.load(hbm_slab3(ctx, "l0f_wo", l, qtr * 256,
                                              (qtr + 1) * 256, kt=4))
            for t in range(NTT):
                pb, pid = (pk2, 1) if it % 2 == 0 else (pk, 0)
                it += 1
                nc.tensor.wait_ge(sw.h, tw)
                if it == 1:
                    sv.wait(nc.tensor)
                for q, tk in evt[-2:]:
                    if q == pid:
                        ctx.sems["ve"].wait(nc.tensor, at=tk)
                mm_acc(ctx, pb[:, 0:256],
                       [(A[:, 8192 + k * T + t * 128:8192 + k * T + (t + 1) * 128],
                         vw(k, 0, 256)) for k in range(4)], sem=sm)
                sm.wait(nc.vector)
                xsl = X[:, t * D + qtr * 256:t * D + (qtr + 1) * 256]
                ev = nc.vector.tensor_tensor(xsl, xsl, pb[:, 0:256], ALU.add)
                evt.append((pid, sv.inc(ev)))
            ring.mark_free(sl_, sv, sv.count)
        sv.wait(nc.scalar)
        if _stop == "attn0":
            break

        lastc = None
        for t in range(NTT):
            lastc = nc.scalar.activation(A[:, t * D:(t + 1) * D],
                                         X[:, t * D:(t + 1) * D], AF.Identity)
        ss.inc(lastc)
        emit_transpose(ctx, A, B)
        sv.wait(nc.vector)
        lastsh = None
        for ft in range(8):
            if ft < 4:
                lastsh = nc.vector.tensor_copy(C[:, ft * T:(ft + 1) * T],
                                               B[:, ft * T:(ft + 1) * T])
            else:
                nc.vector.memset(C[:, ft * T:ft * T + 1], 0.0)
                lastsh = nc.vector.tensor_copy(C[:, ft * T + 1:(ft + 1) * T],
                                               B[:, ft * T:(ft + 1) * T - 1])
        sv.inc(lastsh)
        sv.wait(nc.tensor)
        sv.wait(nc.scalar)
        for ts in range(4):
            sqtoks = []
            ctx.sems["ve"].wait(nc.scalar)
            for ft in range(8):
                qh = (ft % 2) * 512
                if len(sqtoks) >= 2:
                    ctx.sems["mm"].wait(nc.scalar, at=sqtoks[-2])
                ss.inc(nc.scalar.activation(
                    QS[:, qh:qh + 512],
                    C[:, ft * T + ts * 512:ft * T + (ts + 1) * 512], AF.Square))
                ss.wait(nc.tensor)
                if ft == 0:
                    sv.wait(nc.tensor)
                nc.tensor.matmul(ps_sc[0][0:1, 0:512], KB[:, 56:57],
                                 QS[:, qh:qh + 512], start=(ft == 0),
                                 stop=(ft == 7), skip_group_check=True)
                tok = sm.inc(nc.tensor.matmul(
                    ps_sc[1][0:1, 0:512], KB[:, 56:57],
                    C[:, ft * T + ts * 512:ft * T + (ts + 1) * 512],
                    start=(ft == 0), stop=(ft == 7), skip_group_check=True))
                sqtoks.append(tok)
            sm.wait(nc.vector)
            for hh in range(2):
                sa = K32[0:1, 0:256]
                sq = K32[0:1, 256:512]
                si = K32[0:1, 512:768]
                sb = K32[0:1, 768:1024]
                c0h = hh * 256
                vv(ctx, nc.vector.tensor_copy(sa, ps_sc[1][0:1, c0h:c0h + 256]))
                vv(ctx, nc.vector.tensor_copy(sq, ps_sc[0][0:1, c0h:c0h + 256]))
                vv(ctx, nc.vector.tensor_scalar(sa, sa, 1.0 / D, None, ALU.mult))
                vv(ctx, nc.vector.tensor_scalar(sq, sq, 1.0 / D, None, ALU.mult))
                vv(ctx, nc.vector.tensor_tensor(si, sa, sa, ALU.mult))
                sv.inc(nc.vector.tensor_tensor(si, sq, si, ALU.subtract))
                sv.wait(nc.vector)
                sv.wait(nc.scalar)
                sssync(ctx, nc.scalar.activation(si, si, AF.Ln,
                                                 bias=V32[0:1, 240:241]))
                sssync(ctx, nc.scalar.activation(si, si, AF.Exp, scale=-0.5))
                ss.wait(nc.vector)
                vv(ctx, nc.vector.scalar_tensor_tensor(sb, sa, -1.0, si,
                                                       ALU.mult, ALU.mult))
                vv(ctx, nc.vector.tensor_copy(QS[0:1, 1024 + c0h:1280 + c0h], si))
                sv.inc(nc.vector.tensor_copy(QS[0:1, 1536 + c0h:1792 + c0h], sb))
                sv.wait(nc.vector)
                sv.wait(nc.tensor)
                mm_acc(ctx, ps_sc[2][:, c0h:c0h + 256],
                       [(ones[0:1, 0:128], QS[0:1, 1024 + c0h:1280 + c0h])],
                       sem=None)
                mm_acc(ctx, ps_sc[3][:, c0h:c0h + 256],
                       [(ones[0:1, 0:128], QS[0:1, 1536 + c0h:1792 + c0h])],
                       sem=sm)
            sm.wait(nc.vector)
            for ft in range(8):
                lg = S32[:, 184 + l * 8 + ft:185 + l * 8 + ft]
                lb = S32[:, 200 + l * 8 + ft:201 + l * 8 + ft]
                nc.vector.tensor_tensor(QS[:, 0:512],
                                        C[:, ft * T + ts * 512:ft * T + (ts + 1) * 512],
                                        ps_sc[2][:, 0:512], ALU.mult)
                nc.vector.tensor_tensor(QS[:, 512:1024], QS[:, 0:512],
                                        ps_sc[3][:, 0:512], ALU.add)
                lastl = nc.vector.tensor_scalar(
                    B[:, ft * T + ts * 512:ft * T + (ts + 1) * 512],
                    QS[:, 512:1024], lg, lb, ALU.mult, ALU.add)
                sv.inc(lastl)
            sv.wait(nc.tensor)

        for ts in range(4):
            htoks = []
            for i in range(22):
                vg, tg, slg = ring.load(hbm_slab3(ctx, "l0f_w1g", l, i * 128,
                                                  (i + 1) * 128))
                va, ta, sla = ring.load(hbm_slab3(ctx, "l0f_w1a", l, i * 128,
                                                  (i + 1) * 128))
                nc.tensor.wait_ge(sw.h, tg)
                ctx.sems["se"].wait(nc.tensor)
                mm_acc(ctx, pk[:, 0:512],
                       [(vg(k, 0, 128), B[:, k * T + ts * 512:k * T + (ts + 1) * 512])
                        for k in range(8)], sem=sm)
                sm.wait(nc.scalar)
                if htoks:
                    ctx.sems["ve"].wait(nc.scalar, at=htoks[-1])
                ss.inc(nc.scalar.activation(QS[:, 0:512], pk[:, 0:512], AF.Gelu,
                                            bias=S32[:, 140 + 22 * l + i:
                                                     141 + 22 * l + i]))
                nc.tensor.wait_ge(sw.h, ta)
                ctx.sems["ve"].wait(nc.tensor)
                mm_acc(ctx, pk2[:, 0:512],
                       [(va(k, 0, 128), B[:, k * T + ts * 512:k * T + (ts + 1) * 512])
                        for k in range(8)], sem=sm)
                sm.wait(nc.vector)
                ss.wait(nc.vector)
                ev = nc.vector.scalar_tensor_tensor(
                    A[:, i * 512:(i + 1) * 512], pk2[:, 0:512],
                    S32[:, 96 + 22 * l + i:97 + 22 * l + i], QS[:, 0:512],
                    ALU.add, ALU.mult)
                htoks.append(sv.inc(ev))
                ring.mark_free(slg, sv, sv.count)
                ring.mark_free(sla, sv, sv.count)
            sv.wait(nc.tensor)
            ytoks = []
            it = 0
            for oq in range(4):
                loads = []
                for kk in range(3):
                    a0, a1 = kk * 8, min(22, (kk + 1) * 8)
                    loads.append(ring.load(hbm_slab3(ctx, "l0f_w2", l, oq * 256,
                                                     (oq + 1) * 256, a0=a0, a1=a1)))
                for tt in range(4):
                    pb, pid = (pk, 0) if it % 2 == 0 else (pk2, 1)
                    it += 1
                    nc.tensor.wait_ge(sw.h, loads[-1][1])
                    for q, tk in ytoks[-2:]:
                        if q == pid:
                            ctx.sems["ve"].wait(nc.tensor, at=tk)
                    pairs = []
                    for kk in range(3):
                        a0, a1 = kk * 8, min(22, (kk + 1) * 8)
                        for k in range(a1 - a0):
                            pairs.append((A[:, (a0 + k) * 512 + tt * 128:
                                            (a0 + k) * 512 + tt * 128 + 128],
                                          loads[kk][0](k, 0, 256)))
                    extra = [(ones[0:1, 0:128],
                              KB[0:1, 512 + l * 1024 + oq * 256:
                                 512 + l * 1024 + (oq + 1) * 256])]
                    mm_acc(ctx, pb[:, 0:256], pairs, sem=sm, extra=extra)
                    sm.wait(nc.vector)
                    gt = ts * 4 + tt
                    xsl = X[:, gt * D + oq * 256:gt * D + (oq + 1) * 256]
                    ev = nc.vector.tensor_tensor(xsl, xsl, pb[:, 0:256], ALU.add)
                    ytoks.append((pid, sv.inc(ev)))
                for (vw_, tw_, sl2_) in loads:
                    ring.mark_free(sl2_, sv, sv.count)
            sv.wait(nc.tensor)
            ss.wait(nc.tensor)

    if _stop == "full":
        sv.wait(nc.gpsimd)
        ctx.sems["se"].wait(nc.gpsimd)
        sp2 = ctx.sem("wld")
        sp2.inc(nc.gpsimd.dma_start(out=X[:].rearrange("p (n d) -> p n d", n=NTT),
                                    in_=scr.rearrange("(n p) d -> p n d", p=128),
                                    accum_op=ALU.add), 16)
        for eng in (nc.tensor, nc.vector, nc.scalar, nc.sync, nc.gpsimd):
            sp2.wait(eng)



def emit_l2a(ctx):
    nc = ctx.nc
    X, A, B, C, QS, PS, S32, KB, K32, F128 = (ctx.ar[k] for k in
        ("X", "A", "B", "C", "QS", "PS", "S32", "KB", "K32", "F128"))
    V32 = ctx.ar["V32"]
    sv, ss, sm, sw = ctx.sem("ve"), ctx.sem("se"), ctx.sem("mm"), ctx.sem("wld")
    sld, sp = ctx.sem("ldx"), ctx.sem("spill")
    pk, pk2 = ctx.psum["pk"], ctx.psum["pk2"]
    ps_sc = [ctx.psum["ps0"], ctx.psum["ps1"], ctx.psum["ps2"], ctx.psum["ps3"]]
    ptA, ptB = ctx.psum["ptA"], ctx.psum["ptB"]
    eyebf = ctx.ar["eye_bf"]
    ones = ctx.ar["ones_bf"]
    NH, DK, DV, CH = 4, 256, 256, 32
    NCK = T // CH
    DECAY = 1.0 - 1.0 / 3000.0
    EPSF = 0.08 * DECAY

    emit_ln(ctx, X, A)
    emit_transpose(ctx, A, B)
    ctx.sems["ve"].wait(nc.tensor)

    ring = WRing(ctx, nslots=3, cols=256)

    vb, tb, sb_ = ring.load(hbm_slab(ctx, "l2a_wb", 0, 4))
    for sl in range(4):
        pb = pk if sl % 2 == 0 else pk2
        nc.tensor.wait_ge(sw.h, tb)
        if sl >= 2:
            ctx.sems["se"].wait(nc.tensor)
        mm_acc(ctx, pb[0:4, 0:512],
               [(vb(k, 0, 4), B[:, k * T + sl * 512:k * T + (sl + 1) * 512])
                for k in range(8)], sem=sm)
        sm.wait(nc.scalar)
        ss.inc(nc.scalar.activation(QS[0:4, sl * 512:(sl + 1) * 512],
                                    pb[0:4, 0:512], AF.Sigmoid))
    ring.mark_free(sb_, ss, ss.count)
    ss.wait(nc.tensor)
    for tt in range(NTT):
        sm.inc(nc.tensor.transpose(ptA[:, 0:4],
                                   QS[0:4, tt * 128:(tt + 1) * 128], eyebf[0:4, 0:4]))
        sm.wait(nc.vector)
        sv.inc(nc.vector.tensor_copy(V32[:, tt * 4:(tt + 1) * 4], ptA[:, 0:4]))
        sv.wait(nc.tensor)

    def proj_raw(wname):
        evt = []
        for oc in range(4):
            vw, tw, so_ = ring.load(hbm_slab(ctx, wname, oc * 256, (oc + 1) * 256))
            for so in range(2):
                o = oc * 2 + so
                for sl in range(4):
                    pb, pid = (pk, 0) if (len(evt)) % 2 == 0 else (pk2, 1)
                    nc.tensor.wait_ge(sw.h, tw)
                    for q, tk in evt[-2:]:
                        if q == pid:
                            ctx.sems["ve"].wait(nc.tensor, at=tk)
                    mm_acc(ctx, pb[:, 0:512],
                           [(vw(k, so * 128, so * 128 + 128),
                             B[:, k * T + sl * 512:k * T + (sl + 1) * 512])
                            for k in range(8)], sem=sm)
                    sm.wait(nc.vector)
                    ev = nc.vector.tensor_copy(
                        C[:, o * T + sl * 512:o * T + (sl + 1) * 512],
                        pb[:, 0:512])
                    evt.append((pid, sv.inc(ev)))
            ring.mark_free(so_, sv, sv.count)
        sv.wait(nc.tensor)

    def conv_silu(wname, dst):
        K32_ = K32
        tok = sld.inc(nc.sync.dma_start(out=K32_[:, 0:32],
                                        in_=ctx.dram[wname][:]), 16)
        sld.wait(nc.vector, at=tok)
        for ct in range(8):
            sm.wait(nc.vector)
            lastd = None
            for j in range(4):
                lastd = nc.vector.tensor_scalar(
                    QS[:, 2048 + j * 128:2048 + (j + 1) * 128], eyebf[:],
                    K32_[:, ct * 4 + j:ct * 4 + j + 1], None, ALU.mult)
            sv.inc(lastd)
            sv.wait(nc.tensor)
            for sl in range(4):
                pb = pk if sl % 2 == 0 else pk2
                c0 = sl * 512
                ctx.sems["se"].wait(nc.tensor)
                last = None
                order = [3, 0, 1, 2]
                for oi, j in enumerate(order):
                    sh = 3 - j
                    lo = max(0, c0 - sh)
                    outoff = lo - (c0 - sh)
                    ncols = 512 - outoff
                    last = nc.tensor.matmul(
                        pb[:, outoff:512],
                        QS[:, 2048 + j * 128:2048 + (j + 1) * 128],
                        C[:, ct * T + lo:ct * T + lo + ncols],
                        start=(oi == 0), stop=(oi == 3), skip_group_check=True)
                sm.inc(last)
                sm.wait(nc.scalar)
                ss.inc(nc.scalar.activation(dst(ct, sl), pb[:, 0:512], AF.Silu))
            ss.wait(nc.tensor)

    for which, wname, cname in (("q", "l2a_wq", "l2a_cq"),
                                ("k", "l2a_wk", "l2a_ck"),
                                ("v", "l2a_wv", "l2a_cv")):
        proj_raw(wname)
        if which == "v":
            conv_silu(cname, lambda ct, sl: A[:, ct * T + sl * 512:
                                              ct * T + (sl + 1) * 512])
            sv.wait(nc.scalar)
        else:
            scr = ctx.dram["scr_qT" if which == "q" else "scr_kT"]
            conv_silu(cname, lambda ct, sl: A[:, ct * T + sl * 512:
                                              ct * T + (sl + 1) * 512])
            ss.wait(nc.sync)
            sv.wait(nc.sync)
            sp.inc(nc.sync.dma_start(
                out=scr.rearrange("(a p) c -> p a c", p=128),
                in_=A[:].rearrange("p (a c) -> p a c", a=8)), 16)
            sp.wait(nc.tensor)
            sp.wait(nc.scalar)

    sv.wait(nc.tensor)
    gi = 0
    sptok = {}
    for tt in range(NTT):
        for g2 in range(2):
            pt = (ptA, ptB)[gi % 2]
            if gi - 2 in sptok:
                ctx.sems["ve"].wait(nc.tensor, at=sptok[gi - 2][0])
                nc.vector.wait_ge(ctx.sems["spill"].h, sptok[gi - 2][1])
            lastt = None
            for j in range(4):
                ft = g2 * 4 + j
                lastt = nc.tensor.transpose(
                    pt[:, j * 128:(j + 1) * 128],
                    A[:, ft * T + tt * 128:ft * T + (tt + 1) * 128], eyebf[:])
            sm.inc(lastt)
            sm.wait(nc.vector)
            ev = nc.vector.tensor_copy(
                QS[:, (gi % 2) * 512:(gi % 2) * 512 + 512], pt[:, 0:512])
            evt_ = sv.inc(ev)
            sv.wait(nc.sync)
            spt = sp.inc(nc.sync.dma_start(
                out=ctx.dram["scr_vtok"].rearrange("(a p) c -> p a c", p=128)
                [:, tt:tt + 1, g2 * 512:(g2 + 1) * 512],
                in_=QS[:, (gi % 2) * 512:(gi % 2) * 512 + 512]
                .rearrange("p (a c) -> p a c", a=1)), 16)
            sptok[gi] = (evt_, spt)
            gi += 1
    sv.wait(nc.tensor)
    for wname, taps, scr_nm in (("l2a_firs", 5, "scr_fs"),
                                ("l2a_firl", 64, "scr_fl")):
        scr = ctx.dram[scr_nm]
        tokd = sld.inc(nc.sync.dma_start(out=K32[:, 0:8 * taps],
                                         in_=ctx.dram[wname][:]), 16)
        sld.wait(nc.vector, at=tokd)
        for ct in range(8):
            tap_order = [taps - 1] + list(range(taps - 1))
            for wi in range(0, taps, 16):
                wtaps = tap_order[wi:wi + 16]
                sm.wait(nc.vector)
                lastd = None
                for jj, tp in enumerate(wtaps):
                    lastd = nc.vector.tensor_scalar(
                        QS[:, 2048 + jj * 128:2048 + (jj + 1) * 128], eyebf[:],
                        K32[:, ct * taps + tp:ct * taps + tp + 1],
                        None, ALU.mult)
                sv.inc(lastd)
                sv.wait(nc.tensor)
                for sl in range(4):
                    pb = [pk, pk2, ps_sc[0], ps_sc[1]][sl]
                    c0 = sl * 512
                    last = None
                    for jj, tp in enumerate(wtaps):
                        sh = (taps - 1) - tp
                        lo = max(0, c0 - sh)
                        outoff = lo - (c0 - sh)
                        ncols = 512 - outoff
                        if ncols <= 0:
                            continue
                        last = nc.tensor.matmul(
                            pb[:, outoff:512],
                            QS[:, 2048 + jj * 128:2048 + (jj + 1) * 128],
                            A[:, ct * T + lo:ct * T + lo + ncols],
                            start=(wi == 0 and jj == 0),
                            stop=(wi + 16 >= taps and jj == len(wtaps) - 1),
                            skip_group_check=True)
                    if last is not None:
                        sm.inc(last)
            for sl in range(4):
                pb = [pk, pk2, ps_sc[0], ps_sc[1]][sl]
                sm.wait(nc.scalar)
                sp.wait(nc.scalar)
                ss.inc(nc.scalar.activation(QS[:, (sl % 2) * 512:(sl % 2) * 512 + 512],
                                            pb[:, 0:512], AF.Identity))
                ss.wait(nc.tensor)
                lastt = None
                for j in range(4):
                    lastt = nc.tensor.transpose(
                        (ptA, ptB)[sl % 2][:, j * 128:(j + 1) * 128],
                        QS[:, (sl % 2) * 512 + j * 128:(sl % 2) * 512 + (j + 1) * 128],
                        eyebf[:])
                sm.inc(lastt)
                sm.wait(nc.vector)
                sv.inc(nc.vector.tensor_copy(
                    QS[:, 1024 + (sl % 2) * 512:1024 + (sl % 2) * 512 + 512],
                    (ptA, ptB)[sl % 2][:, 0:512]))
                sv.wait(nc.sync)
                sp.inc(nc.sync.dma_start(
                    out=scr.rearrange("(a p) c -> p a c", p=128)
                    [:, sl * 4:(sl + 1) * 4, ct * 128:(ct + 1) * 128],
                    in_=QS[:, 1024 + (sl % 2) * 512:1024 + (sl % 2) * 512 + 512]
                    .rearrange("p (a c) -> p a c", a=4)), 16)
            sp.wait(nc.tensor)

    sv.wait(nc.vector)
    lastz = None
    for h in range(NH):
        lastz = nc.vector.memset(QS[:, 512 * h:512 * (h + 1)], 0.0)
    sv.inc(lastz)
    eye32 = eyebf[0:32, 0:32]
    mskSL = KB[0:32, 3616:3648]
    mskU = KB[0:32, 3648:3680]
    sw.inc(nc.gpsimd.dma_start(out=KB[0:32, 3616:3680],
                               in_=ctx.dram["masks32"][:]), 16)
    sw.wait(nc.tensor)
    TBC = 3072
    tcols = {"A": 0, "B": 32, "A2": 64, "B2": 96, "A4": 128, "B4": 160,
             "A8": 192, "B8": 224, "B16": 256, "C1": 288, "D1": 320,
             "C2": 352, "D2": 384, "C3": 416, "D3": 448, "TT": 480}

    def tb(nm):
        return QS[0:32, TBC + tcols[nm]:TBC + tcols[nm] + 32]

    for h in range(NH):
        if h == 0:
            sp.wait(nc.sync)
        ctx.sems["mm"].wait(nc.sync)
        ctx.sems["ve"].wait(nc.sync)
        sld.inc(nc.sync.dma_start(
            out=A[:, 0:4096].rearrange("p (a c) -> p a c", a=2),
            in_=ctx.dram["scr_qT"].rearrange("(a p) c -> p a c", p=128)
            [:, 2 * h:2 * h + 2, :]), 16)
        ldt = sld.inc(nc.sync.dma_start(
            out=A[:, 4096:8192].rearrange("p (a c) -> p a c", a=2),
            in_=ctx.dram["scr_kT"].rearrange("(a p) c -> p a c", p=128)
            [:, 2 * h:2 * h + 2, :]), 16)
        sld.wait(nc.tensor, at=ldt)
        sv.wait(nc.tensor)
        ctx.sems["mm"].wait(nc.vector)
        gi = 0
        evtok = {}
        for tt in range(NTT):
            pt = (ptA, ptB)[gi % 2]
            if gi - 2 in evtok:
                ctx.sems["ve"].wait(nc.tensor, at=evtok[gi - 2])
            nc.tensor.transpose(
                pt[:, 0:128],
                A[:, 4096 + tt * 128:4096 + (tt + 1) * 128], eyebf[:])
            lastt = nc.tensor.transpose(
                pt[:, 128:256],
                A[:, 4096 + T + tt * 128:4096 + T + (tt + 1) * 128], eyebf[:])
            sm.inc(lastt)
            sm.wait(nc.vector)
            ev = nc.vector.tensor_copy(
                A[:, 8192 + tt * 256:8192 + (tt + 1) * 256], pt[:, 0:256])
            evtok[gi] = sv.inc(ev)
            gi += 1
        import os as _osA
        if _osA.environ.get("L2A_SPILL") == "ktokraw" and h == 0:
            ctx.barrier()
            sp.inc(nc.sync.dma_start(
                out=ctx.dram["scr_od"].rearrange("(a p) c -> p a c", p=128)
                [:, 0:4, :],
                in_=A[:, 8192:12288].rearrange("p (a c) -> p a c", a=4)), 16)
            sp.wait(nc.sync)
            ctx.barrier()
        import os as _osB
        _nonorm = _osB.environ.get("L2A_NONORM")
        sv.wait(nc.tensor)
        gi = 0
        evtok2 = {}
        for tt in range(NTT):
            pt = (ptA, ptB)[gi % 2]
            if gi - 2 in evtok2:
                ctx.sems["ve"].wait(nc.tensor, at=evtok2[gi - 2])
            nc.tensor.transpose(pt[:, 0:128],
                                A[:, tt * 128:(tt + 1) * 128], eyebf[:])
            lastt = nc.tensor.transpose(
                pt[:, 128:256], A[:, T + tt * 128:T + (tt + 1) * 128], eyebf[:])
            sm.inc(lastt)
            sm.wait(nc.vector)
            ev = nc.vector.tensor_copy(PS[:, tt * 256:(tt + 1) * 256],
                                       pt[:, 0:256])
            evtok2[gi] = sv.inc(ev)
            gi += 1
        sv.wait(nc.scalar)
        lastq1 = lastq2 = None
        for tt in range(NTT):
            lastq1 = nc.scalar.activation(QS[:, 2048 + (tt % 4) * 256:
                                             2048 + (tt % 4) * 256 + 256],
                                          PS[:, tt * 256:(tt + 1) * 256],
                                          AF.Square, accum_out=F128[:, tt:tt + 1])
        for tt in range(NTT):
            lastq2 = nc.scalar.activation(QS[:, 2048 + (tt % 4) * 256:
                                             2048 + (tt % 4) * 256 + 256],
                                          A[:, 8192 + tt * 256:8192 + (tt + 1) * 256],
                                          AF.Square, accum_out=F128[:, 32 + tt:33 + tt])
        ss.inc(lastq2)
        ss.wait(nc.scalar)
        ss.wait(nc.vector)
        sssync(ctx, nc.scalar.activation(F128[:, 16:32], F128[:, 0:16], AF.Ln,
                                         bias=V32[:, 241:242]))
        sssync(ctx, nc.scalar.activation(F128[:, 16:32], F128[:, 16:32], AF.Exp,
                                         scale=-0.5))
        sssync(ctx, nc.scalar.activation(F128[:, 48:64], F128[:, 32:48], AF.Ln,
                                         bias=V32[:, 241:242]))
        sssync(ctx, nc.scalar.activation(F128[:, 48:64], F128[:, 48:64], AF.Exp,
                                         scale=-0.5))
        ss.wait(nc.vector)
        if _osB.environ.get("L2A_RK1"):
            vv(ctx, nc.vector.memset(F128[:, 48:64], 1.0))
            vv(ctx, nc.vector.memset(F128[:, 16:32], 1.0))
        lastkn = None
        for tt in range(NTT):
            lastkn = nc.vector.tensor_scalar(
                A[:, 8192 + tt * 256:8192 + (tt + 1) * 256],
                A[:, 8192 + tt * 256:8192 + (tt + 1) * 256],
                F128[:, 48 + tt:49 + tt], None, ALU.mult)
        sv.inc(lastkn)
        import os as _os9
        if _os9.environ.get("L2A_SPILL") == "ktok" and h == 0:
            sv.wait(nc.sync)
            spk = sp.inc(nc.sync.dma_start(
                out=ctx.dram["scr_od"].rearrange("(a p) c -> p a c", p=128)
                [:, 0:4, :],
                in_=A[:, 8192:12288].rearrange("p (a c) -> p a c", a=4)), 16)
            sp.wait(nc.sync)
        for rid, fcol, base in ((0, 16, 0), (1, 48, 4096)):
            vv(ctx, nc.vector.tensor_copy(QS[0:128, 2048:2064],
                                          F128[:, fcol:fcol + 16]))
            sv.wait(nc.tensor)
            sm.inc(nc.tensor.transpose(ptA[0:16, 0:128], QS[:, 2048:2064], eyebf[:]))
            sm.wait(nc.vector)
            sv.inc(nc.vector.tensor_copy(QS[0:16, 2064:2192], ptA[0:16, 0:128]))
            sv.wait(nc.sync)
            spr = sp.inc(nc.sync.dma_start(out=ctx.dram["scr_row"][:],
                                           in_=QS[0:16, 2064:2192]), 16)
            nc.sync.wait_ge(sp.h, spr)
            sp.inc(nc.sync.dma_start(
                out=QS[0:1, 2192:4240] if False else QS[0:1, 2048:4096],
                in_=ctx.dram["scr_row"].rearrange("(o a) c -> o (a c)", o=1)), 16)
            sp.wait(nc.tensor)
            sp.wait(nc.vector)
            for sl in range(4):
                mm_acc(ctx, pk[:, 0:512],
                       [(ones[0:1, 0:128],
                         QS[0:1, 2048 + sl * 512:2048 + (sl + 1) * 512])], sem=sm)
                sm.wait(nc.vector)
                lastn2 = None
                for dt_ in range(2):
                    lastn2 = nc.vector.tensor_tensor(
                        A[:, base + dt_ * T + sl * 512:base + dt_ * T + (sl + 1) * 512],
                        A[:, base + dt_ * T + sl * 512:base + dt_ * T + (sl + 1) * 512],
                        pk[:, 0:512], ALU.mult)
                sv.inc(lastn2)
                sv.wait(nc.tensor)

        Spsum = ps_sc[h]
        last_osp = 0
        import os as _os7
        _nck = int(_os7.environ.get("L2A_NCK", str(NCK)))
        _c0 = int(_os7.environ.get("L2A_C0", "0"))
        for c in range(_c0, _nck):
            tt, pb32 = c // 4, (c % 4) * 32
            qoff = c * 32
            if c % 4 == 0:
                vbase = 1312 + ((c // 4) % 2) * 256
                sv.wait(nc.sync)
                vt_tok = sld.inc(nc.sync.dma_start(
                    out=PS[:, vbase:vbase + 256].rearrange("p (a c) -> p a c", a=1),
                    in_=ctx.dram["scr_vtok"].rearrange("(a p) c -> p a c", p=128)
                    [:, tt:tt + 1, 256 * h:256 * (h + 1)]), 16)
                sld.wait(nc.vector, at=vt_tok)
            ctx.sems["mm"].wait(nc.vector)
            nc.vector.tensor_copy(PS[0:32, 0:256],
                                  A[pb32:pb32 + 32,
                                    8192 + tt * 256:8192 + (tt + 1) * 256])
            vv(ctx, nc.vector.tensor_copy(
                F128[0:32, 121:122],
                V32[pb32:pb32 + 32, tt * 4 + h:tt * 4 + h + 1]))
            nc.vector.tensor_scalar(PS[0:32, 256:512], PS[0:32, 0:256],
                                    F128[0:32, 121:122], None, ALU.mult)
            nc.vector.tensor_scalar(
                PS[0:32, 512:768],
                PS[pb32:pb32 + 32,
                   1312 + ((c // 4) % 2) * 256:1568 + ((c // 4) % 2) * 256],
                F128[0:32, 121:122], None, ALU.mult)
            lastcp = nc.vector.tensor_scalar(
                F128[0:32, 120:121], F128[0:32, 121:122], -1.0, None, ALU.mult)
            sv.inc(lastcp)
            sv.wait(nc.tensor)
            import os as _os5
            if _os5.environ.get("L2A_DBG_CHUNK") == f"{h}:{c}":
                print(f"[DBG] emitting chunk dump at h={h} c={c}")
                ctx.barrier()
                sdb = ctx.sem("st")
                sdb.inc(nc.sync.dma_start(out=ctx.dram["l2a_dbgc"][0:32, 0:1312],
                                          in_=PS[0:32, 0:1312]), 16)
                sdb.inc(nc.sync.dma_start(out=ctx.dram["l2a_dbgc"][0:32, 1312:1824],
                                          in_=QS[0:32, 512 * h:512 * h + 512]), 16)
                sdb.wait(nc.sync)
                ctx.barrier()
            kTc = [A[:, 4096 + j * T + qoff:4096 + j * T + qoff + 32]
                   for j in range(2)]
            qTc = [A[:, j * T + qoff:j * T + qoff + 32] for j in range(2)]
            mm_acc(ctx, pk[0:32, 0:32], [(kTc[0], kTc[0]), (kTc[1], kTc[1])],
                   sem=sm)
            sm.wait(nc.vector)
            vv(ctx, nc.vector.tensor_scalar(PS[0:32, 1024:1056], pk[0:32, 0:32],
                                            F128[0:32, 120:121], None, ALU.mult))
            sv.inc(nc.vector.tensor_tensor(tb("A"), PS[0:32, 1024:1056],
                                           mskSL, ALU.mult))
            sv.wait(nc.tensor)
            sm.inc(nc.tensor.transpose(ptA[0:32, 0:32], tb("A"), eye32))
            sm.wait(nc.vector)
            sv.inc(nc.vector.tensor_copy(tb("B"), ptA[0:32, 0:32]))
            sv.wait(nc.tensor)

            def mm32(dst, lhs_nm, rhs_nm, adds=()):
                pairs = [(tb(lhs_nm), tb(rhs_nm))]
                for en in adds:
                    pairs.append((eye32, eye32 if en == "I" else tb(en)))
                mm_acc(ctx, pk[0:32, 32:64], pairs, sem=sm)
                sm.wait(nc.vector)
                sv.inc(nc.vector.tensor_copy(tb(dst), pk[0:32, 32:64]))
                sv.wait(nc.tensor)

            mm32("A2", "B", "A")
            mm32("B2", "A", "B")
            mm32("A4", "B2", "A2")
            mm32("B4", "A2", "B2")
            mm32("A8", "B4", "A4")
            mm32("B8", "A4", "B4")
            mm32("B16", "A8", "B8")
            mm32("C1", "A", "B2", ["B", "B2", "I"])
            mm32("D1", "B2", "A", ["A", "A2", "I"])
            mm32("C2", "D1", "B4", ["C1"])
            mm32("D2", "B4", "D1", ["D1"])
            mm32("C3", "D2", "B8", ["C2"])
            mm32("D3", "B8", "D2", ["D2"])
            mm32("TT", "D3", "B16", ["C3"])

            mm_acc(ctx, pk[0:32, 64:96], [(kTc[0], qTc[0]), (kTc[1], qTc[1])],
                   sem=sm)
            sm.wait(nc.vector)
            sv.inc(nc.vector.tensor_tensor(PS[0:32, 3072:3104], pk[0:32, 64:96],
                                           mskU, ALU.mult))
            sv.wait(nc.tensor)
            lastw = None
            for dh in range(2):
                lastw = nc.tensor.matmul(
                    pk2[:, 256 + 32 * dh:256 + 32 * (dh + 1)],
                    PS[0:32, 256 + 128 * dh:256 + 128 * (dh + 1)],
                    tb("TT"), start=True, stop=True)
            sm.inc(lastw)
            sm.wait(nc.scalar)
            nc.scalar.activation(PS[:, 3584:3616], pk2[:, 256:288],
                                 AF.Identity, scale=-1.0)
            ss.inc(nc.scalar.activation(PS[:, 3616:3648], pk2[:, 288:320],
                                        AF.Identity, scale=-1.0))
            ss.wait(nc.tensor)
            import os as _os6
            _nos = _os6.environ.get("L2A_NOS")
            pairs = [(tb("TT"), PS[0:32, 512:768])]
            if not _nos:
                pairs += [(PS[:, 3584:3616], QS[:, 512 * h:512 * h + 256]),
                          (PS[:, 3616:3648], QS[:, 512 * h + 256:512 * h + 512])]
            mm_acc(ctx, pk[0:32, 128:384], pairs, sem=sm)
            sm.wait(nc.vector)
            sv.inc(nc.vector.tensor_copy(PS[0:32, 3328:3584], pk[0:32, 128:384]))
            sv.wait(nc.tensor)
            pairs = [(PS[0:32, 3072:3104], PS[0:32, 3328:3584])]
            if not _nos:
                pairs = [(qTc[0], QS[:, 512 * h:512 * h + 256]),
                         (qTc[1], QS[:, 512 * h + 256:512 * h + 512])] + pairs
            mm_acc(ctx, pk2[0:32, 0:256], pairs, sem=sm)
            lastS = None
            for dh in range(2):
                lastS = nc.tensor.matmul(
                    Spsum[:, 256 * dh:256 * (dh + 1)],
                    PS[0:32, 128 * dh:128 * (dh + 1)],
                    PS[0:32, 3328:3584], start=True, stop=True,
                    skip_group_check=True)
            sm.inc(lastS)
            sm.wait(nc.vector)
            import os as _os8
            nc.vector.wait_ge(sp.h, last_osp)
            _src_sp = {"u": PS[0:32, 3328:3584], "k": PS[0:32, 0:256],
                       "kb": PS[0:32, 256:512], "vb": PS[0:32, 512:768],
                       "att": None}.get(_os8.environ.get("L2A_SPILL", "o"))
            if _src_sp is None and _os8.environ.get("L2A_SPILL") == "att":
                nc.vector.tensor_copy(PS[0:32, 1056:1312][:, 0:32],
                                      PS[0:32, 3072:3104])
                nc.vector.tensor_copy(PS[0:32, 1088:1312],
                                      pk2[0:32, 32:256])
            elif _src_sp is not None and _os8.environ.get("L2A_SPILL") != "o":
                nc.vector.tensor_copy(PS[0:32, 1056:1312], _src_sp)
            else:
                nc.vector.tensor_copy(PS[0:32, 1056:1312], pk2[0:32, 0:256])
            if c == 0:
                nc.vector.tensor_copy(K32[:, 0:512], Spsum[:, 0:512])
            else:
                nc.vector.tensor_tensor(K32[:, 0:512], K32[:, 0:512],
                                        Spsum[:, 0:512], ALU.add)
            vspacer(ctx)
            lasts = nc.vector.tensor_copy(QS[:, 512 * h:512 * (h + 1)],
                                          K32[:, 0:512])
            sv.inc(lasts)
            sv.wait(nc.sync)
            last_osp = sp.inc(nc.sync.dma_start(
                out=ctx.dram["scr_od"].rearrange("(a p) c -> p a c", p=128)
                [pb32:pb32 + 32, tt:tt + 1, 256 * h:256 * (h + 1)],
                in_=PS[0:32, 1056:1312].rearrange("p (a c) -> p a c", a=1)), 16)
            sv.wait(nc.tensor)

    sv.wait(nc.tensor)
    ss.wait(nc.tensor)
    ctx.sems["ve"].wait(nc.gpsimd)
    ctx.sems["mm"].wait(nc.gpsimd)
    sw.inc(nc.gpsimd.dma_start(
        out=KB[:, 3584:3616].rearrange("p (a c) -> p a c", a=8),
        in_=hbm_slab(ctx, "l2a_w2t", 0, 4)), 16)
    sw.inc(nc.gpsimd.dma_start(
        out=KB[0:16, 2560:3584].rearrange("p (a c) -> p a c", a=1),
        in_=ctx.dram["l2a_w1st"].rearrange("p (a c) -> p a c", a=1)), 16)
    sldc = ctx.sem("ldx")
    sldc.inc(nc.sync.dma_start(out=K32[:, 960:968], in_=ctx.dram["l2a_b1pc"][:]), 16)
    sldc.inc(nc.sync.dma_start(out=F128[0:4, 64:68], in_=ctx.dram["l2a_ltemp4"][:]), 16)
    sldc.inc(nc.sync.dma_start(out=F128[0:4, 68:72], in_=ctx.dram["l2a_cb4"][:]), 16)
    sldc.inc(nc.sync.dma_start(out=F128[0:4, 72:73], in_=ctx.dram["l2a_b24"][:]), 16)
    sldc.inc(nc.sync.dma_start(out=F128[0:4, 73:74], in_=ctx.dram["maskD41"][:]), 16)
    sw.inc(nc.gpsimd.dma_start(out=QS[:, 3072:4096], in_=ctx.dram["l2a_onw"][:]), 16)
    sldc.wait(nc.vector)
    sldc.wait(nc.scalar)
    sw.wait(nc.vector)
    sw.wait(nc.scalar)
    sw.wait(nc.tensor)

    sv.wait(nc.vector)
    sv.inc(nc.vector.memset(K32[:, 64:192], 0.0))
    for tt in range(NTT):
        sp.wait(nc.sync)
        ldt = None
        for si, nm in enumerate(("scr_fs", "scr_fl", "scr_od", "scr_vtok")):
            ldt = sld.inc(nc.sync.dma_start(
                out=PS[:, si * 1024:(si + 1) * 1024].rearrange(
                    "p (a c) -> p a c", a=1),
                in_=ctx.dram[nm].rearrange("(a p) c -> p a c", p=128)
                [:, tt:tt + 1, :]), 16)
        sld.wait(nc.vector, at=ldt)
        sld.wait(nc.scalar, at=ldt)
        for si in range(4):
            s3 = PS[:, si * 1024:(si + 1) * 1024].rearrange(
                "p (h d) -> p h d", h=4)
            vv(ctx, nc.vector.tensor_reduce(K32[:, si * 4:si * 4 + 4], s3,
                                            AX.X, ALU.add))
            vv(ctx, nc.vector.tensor_reduce(K32[:, 16 + si * 4:16 + si * 4 + 4],
                                            s3, AX.X, ALU.add,
                                            apply_absolute_value=True))
            sssync(ctx, nc.scalar.activation(QS[:, 0:1024],
                                             PS[:, si * 1024:(si + 1) * 1024],
                                             AF.Square))
            ss.wait(nc.vector)
            vv(ctx, nc.vector.tensor_reduce(
                K32[:, 32 + si * 4:32 + si * 4 + 4],
                QS[:, 0:1024].rearrange("p (h d) -> p h d", h=4), AX.X, ALU.add))
        for si in range(4):
            for h in range(NH):
                s_ = K32[:, si * 4 + h:si * 4 + h + 1]
                a_ = K32[:, 16 + si * 4 + h:16 + si * 4 + h + 1]
                q_ = K32[:, 32 + si * 4 + h:32 + si * 4 + h + 1]
                base = 64 + h * 32 + si * 4
                nc.vector.tensor_scalar(K32[:, base:base + 1], s_, 1.0 / 256,
                                        None, ALU.mult)
                nc.vector.tensor_scalar(K32[:, base + 2:base + 3], a_, 1.0 / 256,
                                        None, ALU.mult)
        sv.inc(vspacer(ctx))
        sv.wait(nc.vector)
        for si in range(4):
            for h in range(NH):
                q_ = K32[:, 32 + si * 4 + h:32 + si * 4 + h + 1]
                base = 64 + h * 32 + si * 4
                nc.vector.tensor_tensor(K32[:, base + 1:base + 2],
                                        K32[:, base:base + 1],
                                        K32[:, base:base + 1], ALU.mult)
        sv.inc(vspacer(ctx))
        sv.wait(nc.vector)
        for si in range(4):
            for h in range(NH):
                q_ = K32[:, 32 + si * 4 + h:32 + si * 4 + h + 1]
                base = 64 + h * 32 + si * 4
                nc.vector.scalar_tensor_tensor(
                    K32[:, base + 1:base + 2], q_, 1.0 / 256,
                    K32[:, base + 1:base + 2], ALU.mult, ALU.subtract)
        sv.inc(vspacer(ctx))
        sv.wait(nc.scalar)
        for si in range(4):
            for h in range(NH):
                q_ = K32[:, 32 + si * 4 + h:32 + si * 4 + h + 1]
                base = 64 + h * 32 + si * 4
                nc.scalar.activation(K32[:, base + 3:base + 4], q_, AF.Sqrt)
        ss.inc(sspacer(ctx))
        ss.wait(nc.vector)
        sv.inc(nc.vector.tensor_copy(QS[0:128, 1024:1152], K32[:, 64:192]))
        sv.wait(nc.tensor)
        sm.inc(nc.tensor.transpose(ptA[:, 0:128], QS[:, 1024:1152], eyebf[:]))
        sm.wait(nc.vector)
        lastst = None
        for h in range(NH):
            lastst = nc.vector.tensor_copy(
                C[0:16, h * 2048 + tt * 128:h * 2048 + (tt + 1) * 128],
                ptA[32 * h:32 * h + 16, 0:128])
        sv.inc(lastst)
        sv.wait(nc.tensor)
        sv.wait(nc.sync)

    for h in range(NH):
        sssync(ctx, nc.scalar.activation(F128[0:4, 74:75], F128[0:4, 64 + h:65 + h],
                                         AF.Exp))
        ss.wait(nc.vector)
        vv(ctx, nc.vector.reciprocal(F128[0:4, 75:76], F128[0:4, 74:75]))
        vv(ctx, nc.vector.scalar_tensor_tensor(
            F128[0:4, 76:77], F128[0:4, 73:74], F128[0:4, 68 + h:69 + h],
            F128[0:4, 72:73], ALU.mult, ALU.add))
        sv.inc(nc.vector.tensor_tensor(F128[0:4, 77:78], F128[0:4, 76:77],
                                       F128[0:4, 75:76], ALU.mult))
        sv.wait(nc.scalar)
        sv.wait(nc.tensor)
        htoks = []
        for o in range(8):
            vw, tw, sw_ = ring.load(hbm_slab(ctx, "l2a_w1hs", o * 128,
                                             (o + 1) * 128))
            for sl in range(4):
                pb, pid = (pk, 0) if len(htoks) % 2 == 0 else (pk2, 1)
                nc.tensor.wait_ge(sw.h, tw)
                for q, tk in htoks[-2:]:
                    if q == pid:
                        ctx.sems["se"].wait(nc.tensor, at=tk)
                pairs = [(vw(k, 0, 128),
                          B[:, k * T + sl * 512:k * T + (sl + 1) * 512])
                         for k in range(8)]
                pairs.append((KB[0:16, 2560 + o * 128:2560 + (o + 1) * 128],
                              C[0:16, h * 2048 + sl * 512:h * 2048 + (sl + 1) * 512]))
                mm_acc(ctx, pb[:, 0:512], pairs, sem=sm)
                sm.wait(nc.scalar)
                ev = nc.scalar.activation(
                    A[:, o * T + sl * 512:o * T + (sl + 1) * 512], pb[:, 0:512],
                    AF.Gelu_apprx_tanh, bias=K32[:, 960 + o:961 + o])
                htoks.append((pid, ss.inc(ev)))
            ring.mark_free(sw_, ss, ss.count)
        ss.wait(nc.tensor)
        for sl in range(4):
            pb = pk if sl % 2 == 0 else pk2
            ctx.sems["se"].wait(nc.tensor)
            mm_acc(ctx, pb[0:4, 0:512],
                   [(KB[:, 3584 + k * 4:3584 + (k + 1) * 4],
                     A[:, k * T + sl * 512:k * T + (sl + 1) * 512])
                    for k in range(8)], sem=sm)
            sm.wait(nc.scalar)
            ss.inc(nc.scalar.activation(QS[0:4, sl * 512:(sl + 1) * 512],
                                        pb[0:4, 0:512], AF.Identity,
                                        scale=F128[0:4, 75:76],
                                        bias=F128[0:4, 77:78]))
        ss.wait(nc.tensor)
        for tt in range(NTT):
            sm.inc(nc.tensor.transpose(ptB[:, 0:4],
                                       QS[0:4, tt * 128:(tt + 1) * 128],
                                       eyebf[0:4, 0:4]))
            sm.wait(nc.vector)
            vv(ctx, nc.vector.tensor_copy(F128[:, 80:84], ptB[:, 0:4]))
            vv(ctx, nc.vector.tensor_reduce(F128[:, 84:85], F128[:, 80:84],
                                            AX.X, ALU.max, negate=True))
            sv.wait(nc.scalar)
            ss.inc(nc.scalar.activation(F128[:, 85:89], F128[:, 80:84], AF.Exp,
                                        bias=F128[:, 84:85],
                                        accum_out=F128[:, 89:90]))
            ss.wait(nc.vector)
            vv(ctx, nc.vector.reciprocal(F128[:, 90:91], F128[:, 89:90]))
            vv(ctx, nc.vector.tensor_scalar(F128[:, 91:95], F128[:, 85:89],
                                            F128[:, 90:91], None, ALU.mult))
            base = 256 + tt * 16 + h * 4
            lastw = nc.vector.tensor_scalar(K32[:, base:base + 4],
                                            F128[:, 91:95], 1.0 - 4.0 * EPSF,
                                            EPSF, ALU.mult, ALU.add)
            sv.inc(lastw)
            sv.wait(nc.tensor)

    for tt in range(NTT):
        sp.wait(nc.sync)
        ldt = None
        for si, nm in enumerate(("scr_fs", "scr_fl", "scr_od", "scr_vtok")):
            ldt = sld.inc(nc.sync.dma_start(
                out=PS[:, si * 1024:(si + 1) * 1024].rearrange(
                    "p (a c) -> p a c", a=1),
                in_=ctx.dram[nm].rearrange("(a p) c -> p a c", p=128)
                [:, tt:tt + 1, :]), 16)
        sld.wait(nc.vector, at=ldt)
        for h in range(NH):
            wb_ = 256 + tt * 16 + h * 4
            acc = QS[:, 2048 + h * 256:2048 + (h + 1) * 256]
            nc.vector.tensor_scalar(acc, PS[:, h * 256:(h + 1) * 256],
                                    K32[:, wb_:wb_ + 1], None, ALU.mult)
            nc.vector.scalar_tensor_tensor(
                acc, PS[:, 1024 + h * 256:1024 + (h + 1) * 256],
                K32[:, wb_ + 1:wb_ + 2], acc, ALU.mult, ALU.add)
            nc.vector.scalar_tensor_tensor(
                acc, PS[:, 2048 + h * 256:2048 + (h + 1) * 256],
                K32[:, wb_ + 2:wb_ + 3], acc, ALU.mult, ALU.add)
            lastm = nc.vector.scalar_tensor_tensor(
                acc, PS[:, 3072 + h * 256:3072 + (h + 1) * 256],
                K32[:, wb_ + 3:wb_ + 4], acc, ALU.mult, ALU.add)
            sv.inc(lastm)
        sv.wait(nc.scalar)
        sssync(ctx, nc.scalar.activation(QS[:, 0:1024], QS[:, 2048:3072],
                                         AF.Square))
        ss.wait(nc.vector)
        vv(ctx, nc.vector.tensor_reduce(F128[:, 96:100],
                                        QS[:, 0:1024].rearrange(
                                            "p (h d) -> p h d", h=4),
                                        AX.X, ALU.add))
        vv(ctx, nc.vector.tensor_scalar(F128[:, 100:104], F128[:, 96:100],
                                        1.0 / 256, None, ALU.mult))
        sv.wait(nc.scalar)
        sssync(ctx, nc.scalar.activation(F128[:, 104:108], F128[:, 100:104],
                                         AF.Ln, bias=V32[:, 240:241]))
        sssync(ctx, nc.scalar.activation(F128[:, 108:112], F128[:, 104:108],
                                         AF.Exp, scale=-0.5))
        ss.wait(nc.vector)
        for h in range(NH):
            lastn = nc.vector.scalar_tensor_tensor(
                QS[:, 2048 + h * 256:2048 + (h + 1) * 256],
                QS[:, 2048 + h * 256:2048 + (h + 1) * 256],
                F128[:, 108 + h:109 + h],
                QS[:, 3072 + h * 256:3072 + (h + 1) * 256], ALU.mult, ALU.mult)
            sv.inc(lastn)
        sv.wait(nc.tensor)
        for g2 in range(2):
            pt = (ptA, ptB)[g2]
            lastt = None
            for j in range(4):
                lastt = nc.tensor.transpose(
                    pt[:, j * 128:(j + 1) * 128],
                    QS[:, 2048 + g2 * 512 + j * 128:2048 + g2 * 512 + (j + 1) * 128],
                    eyebf[:])
            sm.inc(lastt)
            sm.wait(nc.vector)
            lastc = None
            for j in range(4):
                ft = g2 * 4 + j
                lastc = nc.vector.tensor_copy(
                    C[:, ft * T + tt * 128:ft * T + (tt + 1) * 128],
                    pt[:, j * 128:(j + 1) * 128])
            sv.inc(lastc)
        sv.wait(nc.sync)

    evt2 = []
    it = 0
    for qtr in range(4):
        vw, tw, sw_ = ring.load(hbm_slab(ctx, "l2a_wo", qtr * 256, (qtr + 1) * 256))
        for t in range(NTT):
            pb, pid = (pk2, 1) if it % 2 == 0 else (pk, 0)
            it += 1
            nc.tensor.wait_ge(sw.h, tw)
            if it == 1:
                sv.wait(nc.tensor)
                ctx.sems["se"].wait(nc.tensor)
            for q, tk in evt2[-2:]:
                if q == pid:
                    ctx.sems["ve"].wait(nc.tensor, at=tk)
            mm_acc(ctx, pb[:, 0:256],
                   [(C[:, k * T + t * 128:k * T + (t + 1) * 128], vw(k, 0, 256))
                    for k in range(8)], sem=sm)
            sm.wait(nc.vector)
            xsl = X[:, t * D + qtr * 256:t * D + (qtr + 1) * 256]
            ev = nc.vector.tensor_tensor(xsl, xsl, pb[:, 0:256], ALU.add)
            evt2.append((pid, sv.inc(ev)))
        ring.mark_free(sw_, sv, sv.count)
    sv.wait(nc.tensor)



BLOCK_WEIGHTS = {
    "l1a": {"wq": "l1_wq", "wk": "l1_wk", "wv": "l1_wv", "wo": "l1_wo"},
}


def declare_common(ctx, consts=True):
    nc = ctx.nc
    ctx.sbuf("X", [128, NTT * D], F32)
    ctx.sbuf("A", [128, NTT * D], BF16)
    ctx.sbuf("B", [128, NFT * T], BF16)
    ctx.sbuf("C", [128, NFT * T], BF16)
    ctx.sbuf("QS", [128, 4096], BF16)
    ctx.sbuf("PS", [128, 4224], BF16)
    ctx.sbuf("W", [128, 6144], BF16)
    ctx.sbuf("S32", [128, 256], F32)
    ctx.sbuf("V32", [128, 256], F32)
    for nm in ("lnsum", "lnssq", "lnscl", "lnbia", "lntmpA", "lntmpB"):
        ctx.sbuf(nm, [128, 16], F32)
    ctx.sbuf("JNK", [128, 4], F32)
    ctx.sbuf("F128", [128, 128], F32)
    ctx.sbuf("K32", [128, 1024], F32)
    ctx.sbuf("KB", [128, 4608], BF16)
    ctx.sbuf("eye_bf", [128, 128], BF16)
    ctx.sbuf("ones_bf", [1, 512], BF16)
    ctx.sbuf("maskA", [128, 128], F32)
    ctx.sbuf("maskB", [128, 128], F32)
    ctx.sbuf("maskWA", [128, 128], F32)
    ctx.sbuf("maskWB", [128, 128], F32)
    ctx.psum_t("pk", [128, 512], F32)
    ctx.psum_t("pk2", [128, 512], F32)
    ctx.psum_t("ps0", [128, 512], F32)
    ctx.psum_t("ps1", [128, 512], F32)
    ctx.psum_t("ps2", [128, 512], F32)
    ctx.psum_t("ps3", [128, 512], F32)
    ctx.psum_t("ptA", [128, 1024], BF16)
    ctx.psum_t("ptB", [128, 1024], BF16)


def emit_prologue(ctx):
    nc = ctx.nc
    s = ctx.sem("ldx")
    s.inc(nc.sync.dma_start(
        out=ctx.ar["X"][:].rearrange("p (n d) -> p n d", n=NTT),
        in_=ctx.dram["x"].rearrange("(n p) d -> p n d", p=128)), 16)
    for nm in ("maskA", "maskB", "maskWA", "maskWB"):
        s.inc(nc.sync.dma_start(out=ctx.ar[nm][:], in_=ctx.dram[nm][:]), 16)

    sg = ctx.sem("wld")
    sg.inc(nc.gpsimd.dma_start(out=ctx.ar["eye_bf"][:], in_=ctx.dram["eye"][:]), 16)
    sg.inc(nc.gpsimd.dma_start(out=ctx.ar["ones_bf"][:],
                               in_=ctx.dram["ones_row"][:, 0:512]), 16)
    nc.vector.memset(ctx.ar["V32"][:, 240:241], 1e-5)
    nc.vector.memset(ctx.ar["V32"][:, 241:242], 1e-6)
    nc.vector.memset(ctx.ar["JNK"][:], 0.0)
    for eng in (nc.tensor, nc.vector, nc.scalar, nc.gpsimd, nc.sync):
        s.wait(eng)
        sg.wait(eng)


def emit_epilogue(ctx):
    nc = ctx.nc
    st = ctx.sem("st")
    sv = ctx.sem("ve")
    ctx.barrier()
    st.inc(nc.sync.dma_start(
        out=ctx.dram["y"].rearrange("(n p) d -> p n d", p=128),
        in_=ctx.ar["X"][:].rearrange("p (n d) -> p n d", n=NTT)), 16)
    st.wait(nc.sync)


def build_program(block_list, dram_specs):
    nc = bass.Bass("TRN2", target_bir_lowering=False,
                   detect_race_conditions=False)
    ctx = Ctx(nc)
    for name, shape, dt, kind in dram_specs:
        ctx.dram[name] = nc.dram_tensor(name, list(shape), dt, kind=kind).ap()
    if "l1f" in block_list:
        ctx.dram["scr_h"] = nc.dram_tensor("scr_h", [4096, T], BF16).ap()
    if "l0f" in block_list:
        ctx.dram["scr_x"] = nc.dram_tensor("scr_x", [T, D], F32).ap()
    if "l2a" in block_list:
        for nm in ("scr_qT", "scr_kT"):
            ctx.dram[nm] = nc.dram_tensor(nm, [1024, T], BF16).ap()
        for nm in ("scr_vtok", "scr_fs", "scr_fl", "scr_od"):
            ctx.dram[nm] = nc.dram_tensor(nm, [T, D], BF16).ap()
        ctx.dram["scr_row"] = nc.dram_tensor("scr_row", [16, 128], BF16).ap()
        import os as _os4
        if _os4.environ.get("L2A_DBG_CHUNK"):
            ctx.dram["l2a_dbgc"] = nc.dram_tensor("l2a_dbgc", [32, 1824], BF16,
                                                  kind="ExternalOutput").ap()
            ctx.dram["l2a_dbgc2"] = nc.dram_tensor("l2a_dbgc2", [128, 512], BF16,
                                                   kind="ExternalOutput").ap()
    import os as _os
    if "l3f" in block_list and _os.environ.get("L3F_DEBUG"):
        ctx.dram["l3f_dbg"] = nc.dram_tensor("l3f_dbg", [128, 224], F32,
                                             kind="ExternalOutput").ap()
    if "l3f" in block_list and _os.environ.get("L3F_DEBUG") == "3":
        ctx.dram["l3f_dbg3"] = nc.dram_tensor("l3f_dbg3", [128, 1280], F32,
                                              kind="ExternalOutput").ap()
    if "l3f" in block_list and _os.environ.get("L3F_DEBUG") == "2":
        ctx.dram["l3f_dbg2"] = nc.dram_tensor("l3f_dbg2", [128, 2048], BF16,
                                              kind="ExternalOutput").ap()
    declare_common(ctx)
    emit_prologue(ctx)
    for b in block_list:
        import os as _os
        _sa = _os.environ.get("MHA_STOP") or None
        if b == "l0a":
            emit_mha(ctx, {"wqkv": "l0a_wqkv", "wo": "l0a_wo",
                           "causal": True, "scale": 0.125})
        elif b == "l1a":
            emit_mha(ctx, {"wq": "l1_wq", "wk": "l1_wk", "wv": "l1_wv",
                           "wo": "l1_wo", "causal": True, "scale": 0.125,
                           "stop_after": _sa})
        elif b == "l3a":
            emit_mha(ctx, {"wqkv": "l3_wqkv", "wo": "l3_wo", "causal": False,
                           "scale": 0.125, "bq": "l3_bq", "bk": "l3_bk",
                           "bv": "l3_bv", "bo": "l3_bo"})
        elif b == "l2f":
            emit_l2f(ctx)
        elif b == "l1f":
            emit_l1f(ctx)
        elif b == "l3f":
            emit_l3f(ctx)
        elif b == "l0f":
            emit_l0f(ctx)
        elif b == "l2a":
            emit_l2a(ctx)
        else:
            raise NotImplementedError(b)
        ctx.barrier()
    emit_epilogue(ctx)
    return nc, ctx



def prep_weights(inp):
    d = {}
    wqkv = np.asarray(inp["L0_attn_w_qkv"])
    d["l0a_wqkv"] = np.ascontiguousarray(
        wqkv.reshape(1024, 16, 3, 64).transpose(0, 2, 1, 3).reshape(1024, 3072))
    d["l0a_wo"] = np.ascontiguousarray(
        np.asarray(inp["L0_attn_w_out"]).reshape(1024, 1024))
    d["l1_wq"] = _t(inp["L1_attn_q_heads"])
    d["l1_wk"] = _t(inp["L1_attn_k_heads"])
    d["l1_wv"] = _t(inp["L1_attn_v_heads"])
    d["l1_wo"] = _t(inp["L1_attn_output_proj"])
    wq0 = np.asarray(inp["L0_ffn_wqkv"], dtype=np.float32)
    d["l0f_wqkv"] = np.ascontiguousarray(wq0)
    d["l0f_wo"] = np.ascontiguousarray(np.asarray(inp["L0_ffn_wo"], np.float32))
    w1 = np.asarray(inp["L0_ffn_w1"], np.float32)
    w1a = np.zeros((2, 1024, 2816), np.float32)
    w1g = np.zeros((2, 1024, 2816), np.float32)
    w1a[:, :, :2730] = w1[:, :, :2730]
    w1g[:, :, :2730] = w1[:, :, 2730:]
    d["l0f_w1a"] = w1a
    d["l0f_w1g"] = w1g
    w2 = np.asarray(inp["L0_ffn_w2"], np.float32)
    w2p = np.zeros((2, 2816, 1024), np.float32)
    w2p[:, :2730, :] = w2
    d["l0f_w2"] = w2p
    b1 = np.asarray(inp["L0_ffn_b1"], np.float32)
    b1a = np.zeros((2, 2816), np.float32); b1a[:, :2730] = b1[:, :2730]
    b1g = np.zeros((2, 2816), np.float32); b1g[:, :2730] = b1[:, 2730:]
    d["l0f_b1a"] = np.concatenate(
        [_pcols(b1a[0]), _pcols(b1a[1])], axis=1)
    d["l0f_b1g"] = np.concatenate(
        [_pcols(b1g[0]), _pcols(b1g[1])], axis=1)
    d["l0f_b2"] = np.asarray(inp["L0_ffn_b2"], np.float32).reshape(1, 2048)
    lg = np.asarray(inp["L0_ffn_ln_g"], np.float32)
    lb = np.asarray(inp["L0_ffn_ln_b"], np.float32)
    d["l0f_lgb"] = np.concatenate(
        [_pcols(lg[0]), _pcols(lg[1]), _pcols(lb[0]), _pcols(lb[1])], axis=1)
    qs = np.asarray(inp["L0_ffn_q_scale"], np.float32)
    ks = np.asarray(inp["L0_ffn_k_scale"], np.float32)
    d["l0f_qks"] = np.ascontiguousarray(np.stack(
        [np.tile(ks[0], 2), np.tile(ks[1], 2),
         np.tile(qs[0], 2), np.tile(qs[1], 2)], axis=1))
    DH, WIN, NSEQ = 64, 512, 2048
    half = np.arange(0, DH, 2, dtype=np.float64)
    inv_freq = 1.0 / 10000.0 ** (half / DH)
    tpos = np.arange(NSEQ, dtype=np.float64)
    fr = tpos[:, None] * inv_freq[None, :]
    fr = np.concatenate([fr, fr], axis=-1)
    base = (half + 0.4 * DH) / (1.4 * DH)
    power = (tpos - NSEQ // 2) / float(WIN // 2)
    scale = base[None, :] ** power[:, None]
    scale = np.concatenate([scale, scale], axis=-1)
    cos, sin = np.cos(fr), np.sin(fr)
    qc = (cos * scale).astype(np.float32).T
    qsn = (sin * scale).astype(np.float32).T
    kc = (cos / scale).astype(np.float32).T
    ksn = (sin / scale).astype(np.float32).T
    d["rotqc"] = np.ascontiguousarray(np.concatenate([qc, qc], axis=0))
    d["rotqs"] = np.ascontiguousarray(np.concatenate([qsn, qsn], axis=0))
    d["rotkc"] = np.ascontiguousarray(np.concatenate([kc, kc], axis=0))
    d["rotks"] = np.ascontiguousarray(np.concatenate([ksn, ksn], axis=0))
    def _chpc(w, taps):
        w = np.asarray(w, np.float32).reshape(1024, taps)
        return np.ascontiguousarray(
            w.reshape(8, 128, taps).transpose(1, 0, 2).reshape(128, 8 * taps))
    d["l2a_wq"] = _t(inp["L2_attn_Wq"])
    d["l2a_wk"] = _t(inp["L2_attn_Wk"])
    d["l2a_wv"] = _t(inp["L2_attn_Wv"])
    d["l2a_wb"] = _t(inp["L2_attn_Wb"])
    d["l2a_cq"] = _chpc(inp["L2_attn_conv_q_w"], 4)
    d["l2a_ck"] = _chpc(inp["L2_attn_conv_k_w"], 4)
    d["l2a_cv"] = _chpc(inp["L2_attn_conv_v_w"], 4)
    d["l2a_firs"] = _chpc(inp["L2_attn_fir_short_filt"], 5)
    d["l2a_firl"] = _chpc(inp["L2_attn_fir_long_filt"], 64)
    w1t = _t(inp["L2_attn_gate_W1"])
    d["l2a_w1hs"] = np.ascontiguousarray(w1t[0:1024])
    d["l2a_w1st"] = np.ascontiguousarray(w1t[1024:1040])
    d["l2a_w2t"] = _t(inp["L2_attn_gate_W2"])
    d["l2a_b1pc"] = _pcols(inp["L2_attn_gate_b1"])
    d["l2a_ltemp4"] = np.ascontiguousarray(np.broadcast_to(
        np.asarray(inp["L2_attn_gate_log_temp"], np.float32)[None, :], (4, 4)))
    d["l2a_cb4"] = np.ascontiguousarray(np.broadcast_to(
        np.asarray(inp["L2_attn_gate_copy_bias"], np.float32)[None, :], (4, 4)))
    d["l2a_b24"] = np.asarray(inp["L2_attn_gate_b2"], np.float32).reshape(4, 1)
    d["l2a_onw"] = np.ascontiguousarray(np.broadcast_to(
        np.tile(np.asarray(inp["L2_attn_o_norm_w"], np.float32), 4)[None, :],
        (128, 1024)))
    d["l2a_wo"] = _t(inp["L2_attn_Wo"])
    d["l1f_w1"] = _t(inp["L1_ffn_w1_w"])
    d["l1f_w2"] = _t(inp["L1_ffn_w2_w"])
    d["l1f_h1"] = _t(inp["L1_ffn_h1_w"])
    d["l1f_h2"] = _t(inp["L1_ffn_h2_w"])
    d["l1f_w3"] = _t(inp["L1_ffn_w3_w"])
    d["l1f_s1"] = _pcols(inp["L1_ffn_w1_s"])
    d["l1f_s2"] = _pcols(inp["L1_ffn_w2_s"])
    d["l1f_sh1"] = _pcols(inp["L1_ffn_h1_s"])
    d["l1f_sh2"] = _pcols(inp["L1_ffn_h2_s"])
    d["l1f_s3b"] = np.ascontiguousarray(np.broadcast_to(
        np.asarray(inp["L1_ffn_w3_s"], dtype=np.float32)[None, :], (128, 1024)))
    d["l3f_proj"] = _t(inp["L3_ffn_proj_w"])
    d["l3f_anchT"] = _t(inp["L3_ffn_anchors"])
    d["l3f_temp"] = np.ascontiguousarray(np.broadcast_to(
        np.asarray(inp["L3_ffn_routing_temp"], dtype=np.float32).reshape(1, 1),
        (128, 1)))
    d["l3f_w1"] = np.ascontiguousarray(np.asarray(inp["L3_ffn_w1"]))
    d["l3f_w2"] = np.ascontiguousarray(np.asarray(inp["L3_ffn_w2"]))
    d["l2f_wg"] = _t(inp["L2_ffn_w_gate"])
    d["l2f_wu"] = _t(inp["L2_ffn_w_up"])
    d["l2f_wd"] = _t(inp["L2_ffn_w_down"])
    d["l3_wqkv"] = np.ascontiguousarray(np.concatenate(
        [np.asarray(inp["L3_attn_Wq"]), np.asarray(inp["L3_attn_Wk"]),
         np.asarray(inp["L3_attn_Wv"])], axis=1))
    d["l3_wo"] = np.ascontiguousarray(np.asarray(inp["L3_attn_Wo"]))
    for bn, k in (("l3_bq", "L3_attn_bq"), ("l3_bk", "L3_attn_bk"),
                  ("l3_bv", "L3_attn_bv"), ("l3_bo", "L3_attn_bo")):
        d[bn] = np.asarray(inp[k]).reshape(1, 1024)
    return d


def run_blocks(block_list, x, weights, consts):
    dram_specs = [("x", (T, D), F32, "ExternalInput"),
                  ("y", (T, D), F32, "ExternalOutput")]
    need = set()
    for b in block_list:
        import os as _os
        _sa = _os.environ.get("MHA_STOP") or None
        if b == "l0a":
            need |= {"l0a_wqkv", "l0a_wo"}
        elif b == "l1a":
            need |= {"l1_wq", "l1_wk", "l1_wv", "l1_wo"}
        elif b == "l3a":
            need |= {"l3_wqkv", "l3_wo", "l3_bq", "l3_bk", "l3_bv", "l3_bo"}
        elif b == "l2f":
            need |= {"l2f_wg", "l2f_wu", "l2f_wd"}
        elif b == "l1f":
            need |= {"l1f_w1", "l1f_w2", "l1f_h1", "l1f_h2", "l1f_w3",
                     "l1f_s1", "l1f_s2", "l1f_sh1", "l1f_sh2", "l1f_s3b"}
        elif b == "l3f":
            need |= {"l3f_proj", "l3f_anchT", "l3f_temp", "l3f_w1", "l3f_w2"}
        elif b == "l0f":
            need |= {"l0f_wqkv", "l0f_wo", "l0f_w1a", "l0f_w1g", "l0f_w2",
                     "l0f_b1a", "l0f_b1g", "l0f_b2", "l0f_lgb", "l0f_qks",
                     "rotqc", "rotqs", "rotkc", "rotks"}
        elif b == "l2a":
            need |= {"l2a_wq", "l2a_wk", "l2a_wv", "l2a_wb", "l2a_cq", "l2a_ck",
                     "l2a_cv", "l2a_firs", "l2a_firl", "l2a_w1hs", "l2a_w1st",
                     "l2a_w2t", "l2a_b1pc", "l2a_ltemp4", "l2a_cb4", "l2a_b24",
                     "l2a_onw", "l2a_wo"}
    for nm in sorted(need):
        dt = I32 if weights[nm].dtype == np.int32 else F32
        dram_specs.append((nm, weights[nm].shape, dt, "ExternalInput"))
    for nm in ("eye", "ones_row", "maskA", "maskB", "maskWA", "maskWB",
               "hsel", "hselT", "onescol", "masks32", "maskD41"):
        dram_specs.append((nm, consts[nm].shape, F32, "ExternalInput"))
    nc, ctx = build_program(block_list, dram_specs)
    in_maps = []
    for b in range(x.shape[0]):
        m = {"x": np.ascontiguousarray(x[b])}
        for nm in sorted(need):
            m[nm] = weights[nm]
        for nm in ("eye", "ones_row", "maskA", "maskB", "maskWA", "maskWB",
                   "hsel", "hselT", "onescol", "masks32", "maskD41"):
            m[nm] = consts[nm]
        in_maps.append(m)
    res = run_bass_kernel_spmd(nc, in_maps, core_ids=list(range(x.shape[0])))
    ctx.close()
    import os as _os
    if _os.environ.get("L3F_DEBUG") and "l3f_dbg" in res.results[0]:
        np.save("/tmp/l3f_dbg.npy", res.results[0]["l3f_dbg"])
    if "l2a_dbgc" in res.results[0]:
        np.save("/tmp/l2a_dbgc.npy", res.results[0]["l2a_dbgc"].astype(np.float32))
    if "l2a_dbgc2" in res.results[0]:
        np.save("/tmp/l2a_dbgc2.npy", res.results[0]["l2a_dbgc2"].astype(np.float32))
    if "l3f_dbg3" in res.results[0]:
        np.save("/tmp/l3f_dbg3.npy", res.results[0]["l3f_dbg3"])
    if "l3f_dbg2" in res.results[0]:
        np.save("/tmp/l3f_dbg2.npy", res.results[0]["l3f_dbg2"].astype(np.float32))
    return np.stack([r["y"] for r in res.results], axis=0)


ALL_BLOCKS = ["l0a", "l0f", "l1a", "l1f", "l2a", "l2f", "l3a", "l3f"]


def kernel(**inputs):
    x = np.ascontiguousarray(np.asarray(inputs["L0_attn_x"], dtype=np.float32))
    weights = prep_weights(inputs)
    consts = host_consts()
    groups = [["l0a", "l0f", "l1a", "l1f", "l2a", "l2f", "l3a", "l3f"]]
    for group in groups:
        x = run_blocks(group, x, weights, consts)
    return x


# version 03
# speedup vs baseline: 1.0421x; 1.0270x over previous
import numpy as np
import ml_dtypes
import concourse.bass as bass
import concourse.mybir as mybir
from concourse.bass_utils import run_bass_kernel_spmd

F32 = mybir.dt.float32
BF16 = mybir.dt.bfloat16
I32 = mybir.dt.int32
AF = mybir.ActivationFunctionType
ALU = mybir.AluOpType
AX = mybir.AxisListType
BF = ml_dtypes.bfloat16

T = 2048
D = 1024
NTT = T // 128
NFT = D // 128
NEG = -1.0e30



def _t(a):
    return np.ascontiguousarray(np.asarray(a).T)


def _pcols(v):
    v = np.asarray(v).reshape(-1)
    n = v.shape[0]
    nt = (n + 127) // 128
    out = np.zeros((nt, 128), v.dtype)
    out.reshape(-1)[:n] = v
    return np.ascontiguousarray(out.T)


def host_consts():
    d = {}
    d["eye"] = np.eye(128, dtype=np.float32)
    d["ones_row"] = np.ones((1, 2048), dtype=np.float32)
    r = np.arange(128)[:, None]
    j = np.arange(128)[None, :]
    d["maskA"] = np.where(j <= (r % 64), 0.0, NEG).astype(np.float32)
    d["maskB"] = np.where(j <= 64 + (r % 64), 0.0, NEG).astype(np.float32)
    d["maskWA"] = np.where(j >= (r % 64), 0.0, NEG).astype(np.float32)
    d["maskWB"] = np.where(j >= 64 + (r % 64), 0.0, NEG).astype(np.float32)
    m32 = np.zeros((32, 64), np.float32)
    r32 = np.arange(32)[:, None]
    c32 = np.arange(32)[None, :]
    m32[:, 0:32] = (c32 < r32).astype(np.float32)
    m32[:, 32:64] = (c32 >= r32).astype(np.float32)
    d["masks32"] = m32
    d["maskD41"] = np.array([[0.0], [0.0], [0.0], [1.0 - 1.0 / 3000.0]],
                            np.float32)
    hs = np.zeros((128, 2), np.float32)
    hs[0:64, 0] = 1.0
    hs[64:128, 1] = 1.0
    d["hsel"] = hs
    d["hselT"] = np.ascontiguousarray(hs.T)
    d["onescol"] = np.ones((128, 1), np.float32)
    return d



class Sem:
    def __init__(self, h):
        self.h = h
        self.count = 0

    def inc(self, instr, n=1):
        instr.then_inc(self.h, n)
        self.count += n
        return self.count

    def wait(self, eng, at=None):
        eng.wait_ge(self.h, self.count if at is None else at)


class Ctx:
    def __init__(self, nc):
        self.nc = nc
        self._stack = []
        self.sems = {}
        self.psum = {}
        self.ar = {}
        self.dram = {}

    def sem(self, name):
        if name not in self.sems:
            h = self._enter(self.nc.semaphore(name))
            self.sems[name] = Sem(h)
        return self.sems[name]

    def _enter(self, cm):
        v = cm.__enter__()
        self._stack.append(cm)
        return v

    def sbuf(self, name, shape, dtype):
        t = self._enter(self.nc.sbuf_tensor("sb_" + name, shape, dtype))
        self.ar[name] = t
        return t

    def psum_t(self, name, shape, dtype):
        t = self._enter(self.nc.psum_tensor(name, shape, dtype))
        self.psum[name] = t
        return t

    def close(self):
        for cm in reversed(self._stack):
            cm.__exit__(None, None, None)

    def barrier(self):
        self.nc.all_engine_barrier()


def vv(ctx, instr):
    sv = ctx.sem("ve")
    sv.inc(instr)
    sv.wait(ctx.nc.vector)
    return instr


def sssync(ctx, instr):
    ss = ctx.sem("se")
    ss.inc(instr)
    ss.wait(ctx.nc.scalar)
    return instr


def vspacer(ctx):
    return ctx.nc.vector.memset(ctx.ar["JNK"][:, 0:1], 0.0)


def sspacer(ctx):
    return ctx.nc.scalar.activation(ctx.ar["JNK"][:, 2:3],
                                    ctx.ar["JNK"][:, 1:2], AF.Identity)


def mm_acc(ctx, out_ap, pairs, sem=None, extra=()):
    nc = ctx.nc
    allp = list(pairs) + list(extra)
    n = len(allp)
    last = None
    for i, p in enumerate(allp):
        tp = p[2] if len(p) > 2 else None
        last = nc.tensor.matmul(out_ap, p[0], p[1], start=(i == 0),
                                stop=(i == n - 1), tile_position=tp)
    if sem is not None:
        sem.inc(last)
    return last


class WRing:

    def __init__(self, ctx, arena="W", nslots=4, cols=256, kt=8):
        self.ctx = ctx
        self.nslots = nslots
        self.cols = cols
        self.kt = kt
        self.W = ctx.ar[arena]
        self.sem = ctx.sem("wld")
        self.free_toks = [None] * nslots
        self.i = 0

    def load(self, hbm_ap):
        ctx, nc = self.ctx, self.ctx.nc
        slot = self.i % self.nslots
        self.i += 1
        base = slot * self.kt * self.cols
        ncols = hbm_ap.shape[-1]
        kt = hbm_ap.shape[1]
        assert kt * ncols <= self.kt * self.cols
        dst = self.W[:, base:base + kt * ncols].rearrange("p (a c) -> p a c", a=kt)
        ft = self.free_toks[slot]
        if ft is not None:
            nc.gpsimd.wait_ge(ft[0].h, ft[1])
        tok = self.sem.inc(nc.gpsimd.dma_start(out=dst, in_=hbm_ap), 16)

        def view(k, c0=0, c1=None):
            c1 = ncols if c1 is None else c1
            return self.W[:, base + k * ncols + c0: base + k * ncols + c1]
        return view, tok, slot

    def mark_free(self, slot, sem, count):
        self.free_toks[slot] = (sem, count)


def hbm_slab(ctx, name, c0, c1, a0=0, a1=None):
    t = ctx.dram[name]
    if len(t.shape) == 5:
        nA, nC, _, kt, cg = t.shape
        assert c1 - c0 == cg and c0 % cg == 0 and a0 % kt == 0, (name, c0, c1, a0)
        return t[a0 // kt][c0 // cg]
    r = t.rearrange("(a p) c -> p a c", p=128)
    if a1 is None:
        a1 = r.shape[1]
    return r[:, a0:a1, c0:c1]



def emit_ln(ctx, x_tensor, out_tensor, n_tiles=NTT, width=D, eps_col=240, c0=0):
    nc = ctx.nc
    sv, ss = ctx.sem("ve"), ctx.sem("se")
    nt = n_tiles
    sums = ctx.ar["lnsum"][:, c0:c0 + nt]
    ssqs = ctx.ar["lnssq"][:, c0:c0 + nt]
    scl = ctx.ar["lnscl"][:, c0:c0 + nt]
    bia = ctx.ar["lnbia"][:, c0:c0 + nt]
    tmpA = ctx.ar["lntmpA"][:, c0:c0 + nt]
    tmpB = ctx.ar["lntmpB"][:, c0:c0 + nt]
    lastq = lastr = None
    for t in range(nt):
        xt = x_tensor[:, t * width:(t + 1) * width]
        lastq = nc.scalar.activation(out_tensor[:, t * width:(t + 1) * width], xt,
                                     AF.Square, accum_out=ssqs[:, t:t + 1])
        lastr = nc.vector.tensor_reduce(sums[:, t:t + 1], xt, AX.X, ALU.add)
    sv.inc(lastr)
    ss.inc(lastq)
    sv.wait(nc.vector)
    ss.wait(nc.vector)
    vv(ctx, nc.vector.tensor_tensor(tmpA[:], sums[:], sums[:], ALU.mult))
    vv(ctx, nc.vector.tensor_scalar(ssqs[:], ssqs[:], 1.0 / width, None, ALU.mult))
    vv(ctx, nc.vector.tensor_scalar(tmpA[:], tmpA[:], 1.0 / (width * width),
                                    None, ALU.mult))
    vv(ctx, nc.vector.tensor_scalar(sums[:], sums[:], 1.0 / width, None, ALU.mult))
    sv.inc(nc.vector.tensor_tensor(tmpB[:], ssqs[:], tmpA[:], ALU.subtract))
    sv.wait(nc.vector)
    sv.wait(nc.scalar)
    sssync(ctx, nc.scalar.activation(tmpB[:], tmpB[:], AF.Ln,
                                     bias=ctx.ar["V32"][:, eps_col:eps_col + 1]))
    ss.inc(nc.scalar.activation(scl[:], tmpB[:], AF.Exp, scale=-0.5))
    ss.wait(nc.vector)
    sv.inc(nc.vector.scalar_tensor_tensor(bia[:], sums[:], -1.0, scl[:],
                                          ALU.mult, ALU.mult))
    sv.wait(nc.vector)
    sv.wait(nc.scalar)
    last = None
    for t in range(nt):
        xt = x_tensor[:, t * width:(t + 1) * width]
        last = nc.scalar.activation(out_tensor[:, t * width:(t + 1) * width], xt,
                                    AF.Identity, scale=scl[:, t:t + 1],
                                    bias=bia[:, t:t + 1])
    ss.inc(last)
    return scl, bia


def emit_transpose(ctx, src, dst, n_tok_tiles=NTT, n_feat_tiles=NFT,
                   src_width=None, dst_row=None, shift=None):
    nc = ctx.nc
    if src_width is None:
        src_width = n_feat_tiles * 128
    if dst_row is None:
        dst_row = n_tok_tiles * 128
    sm, sv = ctx.sem("mm"), ctx.sem("ve")
    eye = ctx.ar["eye_bf"]
    pts = [ctx.psum["ptA"], ctx.psum["ptB"]]
    ctx.sems["se"].wait(nc.tensor)
    ctx.sems["ve"].wait(nc.tensor)
    evac_tok = {}
    gi = 0
    for o in range(n_feat_tiles):
        for half in range(0, n_tok_tiles, 8):
            pt = pts[gi % 2]
            nh = min(8, n_tok_tiles - half)
            if gi - 2 in evac_tok:
                ctx.sems["ve"].wait(nc.tensor, at=evac_tok[gi - 2])
            lastt = None
            for j in range(nh):
                t = half + j
                lastt = nc.tensor.transpose(
                    pt[:, j * 128:(j + 1) * 128],
                    src[:, t * src_width + o * 128: t * src_width + o * 128 + 128],
                    eye[:])
            sm.inc(lastt)
            sm.wait(nc.vector)
            ev = nc.vector.tensor_copy(
                dst[:, o * dst_row + half * 128: o * dst_row + (half + nh) * 128],
                pt[:, 0:nh * 128])
            evac_tok[gi] = sv.inc(ev)
            gi += 1
    return sv.count



def emit_mha(ctx, cfg):
    nc = ctx.nc
    X, A, B, C, QS, PS = (ctx.ar[k] for k in ("X", "A", "B", "C", "QS", "PS"))
    S32, KB = ctx.ar["S32"], ctx.ar["KB"]
    sv, ss, sm, sw = ctx.sem("ve"), ctx.sem("se"), ctx.sem("mm"), ctx.sem("wld")
    causal = cfg["causal"]
    scale = cfg["scale"]
    has_bias = "bq" in cfg
    ones = ctx.ar["ones_bf"]
    eyebf = ctx.ar["eye_bf"]
    maskA, maskB = ctx.ar["maskA"], ctx.ar["maskB"]
    pk, pk2 = ctx.psum["pk"], ctx.psum["pk2"]
    ps_sc = [ctx.psum["ps0"], ctx.psum["ps1"], ctx.psum["ps2"], ctx.psum["ps3"]]

    stop_after = cfg.get("stop_after")
    emit_ln(ctx, X, A)
    if stop_after == "ln":
        return
    emit_transpose(ctx, A, B)
    tr_done = sv.count
    if stop_after == "tr":
        return

    ring = WRing(ctx, nslots=3, cols=256)

    def qkv_ap(which, c0, c1):
        return hbm_slab(ctx, cfg[("wq", "wk", "wv")[which]], c0, c1)

    if has_bias:
        for i, bn in enumerate(("bq", "bk", "bv", "bo")):
            sw.inc(nc.gpsimd.dma_start(
                out=KB[0:1, 512 + i * 1024:512 + (i + 1) * 1024],
                in_=ctx.dram[cfg[bn]][:]), 16)

    def brow(i, c0, c1):
        return KB[0:1, 512 + i * 1024 + c0:512 + i * 1024 + c1]

    evt = []

    def proj_guard(pid):
        for q, tk in evt[-2:]:
            if q == pid:
                ctx.sems["ve"].wait(nc.tensor, at=tk)

    ctx.sems["ve"].wait(nc.tensor, at=tr_done)
    it = 0
    import os as _os
    _kn = int(_os.environ.get("MHA_KN", "8"))
    for o in range(_kn):
        view, tok, slot = ring.load(qkv_ap(1, o * 128, (o + 1) * 128))
        for ts in range(4):
            pb, pid = (pk, 0) if it % 2 == 0 else (pk2, 1)
            it += 1
            nc.tensor.wait_ge(sw.h, tok)
            proj_guard(pid)
            extra = [(brow(1, o * 128, (o + 1) * 128), ones[0:1, 0:512])] \
                if has_bias else []
            mm_acc(ctx, pb[:, 0:512],
                   [(view(k, 0, 128), B[:, k * T + ts * 512:k * T + (ts + 1) * 512])
                    for k in range(8)], sem=sm, extra=extra)
            sm.wait(nc.vector)
            ev = nc.vector.tensor_copy(C[:, o * T + ts * 512:o * T + (ts + 1) * 512],
                                       pb[:, 0:512])
            evt.append((pid, sv.inc(ev)))
        ring.mark_free(slot, sv, sv.count)

    for qtr in range(4):
        view, tok, slot = ring.load(qkv_ap(2, qtr * 256, (qtr + 1) * 256))
        for t in range(NTT):
            pb, pid = (pk, 0) if it % 2 == 0 else (pk2, 1)
            it += 1
            nc.tensor.wait_ge(sw.h, tok)
            proj_guard(pid)
            extra = [(ones[0:1, 0:128], brow(2, qtr * 256, (qtr + 1) * 256))] \
                if has_bias else []
            mm_acc(ctx, pb[:, 0:256],
                   [(B[:, k * T + t * 128:k * T + (t + 1) * 128], view(k, 0, 256))
                    for k in range(8)], sem=sm, extra=extra)
            sm.wait(nc.vector)
            ev = nc.vector.tensor_copy(
                A[:, t * D + qtr * 256:t * D + (qtr + 1) * 256], pb[:, 0:256])
            evt.append((pid, sv.inc(ev)))
        ring.mark_free(slot, sv, sv.count)

    if stop_after == "v":
        return
    for g in range(4):
        qs_toks = []
        sv.wait(nc.tensor)
        for o in range(8):
            view, tok, slot = ring.load(qkv_ap(0, o * 128, (o + 1) * 128))
            pb, pid = (pk, 0) if it % 2 == 0 else (pk2, 1)
            it += 1
            nc.tensor.wait_ge(sw.h, tok)
            for q, tk in qs_toks[-2:]:
                if q == pid:
                    ctx.sems["se"].wait(nc.tensor, at=tk)
            extra = [(brow(0, o * 128, (o + 1) * 128), ones[0:1, 0:512])] \
                if has_bias else []
            mm_acc(ctx, pb[:, 0:512],
                   [(view(k, 0, 128), B[:, k * T + g * 512:k * T + (g + 1) * 512])
                    for k in range(8)], sem=sm, extra=extra)
            sm.wait(nc.scalar)
            ev = nc.scalar.activation(QS[:, o * 512:(o + 1) * 512], pb[:, 0:512],
                                      AF.Identity, scale=scale)
            qs_toks.append((pid, ss.inc(ev)))
            ring.mark_free(slot, ss, ss.count)
        ss.wait(nc.tensor)
        if stop_after == "q":
            continue

        for p in range(8):
            for c in range(8):
                qbase = g * 512 + c * 64
                kv_end = (qbase + 64) if causal else T
                nkv = (kv_end + 127) // 128
                nslab = (nkv + 3) // 4
                sv.wait(nc.tensor)
                lastmm = None
                for s in range(nslab):
                    ncols = min(512, nkv * 128 - s * 512)
                    qa = QS[0:64, p * 512 + c * 64:p * 512 + (c + 1) * 64]
                    qb = QS[64:128, p * 512 + c * 64:p * 512 + (c + 1) * 64]
                    ka = C[0:64, p * T + s * 512:p * T + s * 512 + ncols]
                    kb = C[64:128, p * T + s * 512:p * T + s * 512 + ncols]
                    nc.tensor.matmul(ps_sc[s][0:64, 0:ncols], qa, ka,
                                     start=True, stop=True, tile_position=(0, 0))
                    lastmm = nc.tensor.matmul(ps_sc[s][64:128, 0:ncols], qb, kb,
                                              start=True, stop=True,
                                              tile_position=(64, 64))
                sm.inc(lastmm)
                sm.wait(nc.vector)
                if causal:
                    dt = nkv - 1
                    sl, off = dt // 4, (dt % 4) * 128
                    mk = maskA if (qbase % 128) == 0 else maskB
                    nc.vector.tensor_tensor(ps_sc[sl][:, off:off + 128],
                                            ps_sc[sl][:, off:off + 128],
                                            mk[:], ALU.add)
                    vspacer(ctx)
                nm = S32[:, 0:4]
                lastv = None
                for s in range(nslab):
                    ncols = min(512, nkv * 128 - s * 512)
                    lastv = nc.vector.tensor_reduce(nm[:, s:s + 1],
                                                    ps_sc[s][:, 0:ncols],
                                                    AX.X, ALU.max, negate=True)
                sv.inc(lastv)
                sv.wait(nc.vector)
                lastv = nc.vector.tensor_reduce(S32[:, 8:9], nm[:, 0:nslab], AX.X,
                                                ALU.min)
                sv.inc(lastv)
                sv.wait(nc.scalar)
                sm.wait(nc.scalar)
                laste = None
                for s in range(nslab):
                    ncols = min(512, nkv * 128 - s * 512)
                    laste = nc.scalar.activation(
                        PS[:, s * 512:s * 512 + ncols], ps_sc[s][:, 0:ncols],
                        AF.Exp, bias=S32[:, 8:9], accum_out=S32[:, 12 + s:13 + s])
                ss.inc(laste)
                ss.wait(nc.vector)
                vv(ctx, nc.vector.tensor_reduce(S32[:, 9:10], S32[:, 12:12 + nslab],
                                                AX.X, ALU.add))
                vv(ctx, nc.vector.reciprocal(S32[:, 11:12], S32[:, 9:10]))
                lastd = nc.vector.tensor_scalar(PS[:, 2048:2176], eyebf[:],
                                                S32[:, 11:12], None, ALU.mult)
                sv.inc(lastd)
                ss.wait(nc.tensor)
                sv.wait(nc.tensor)
                lastt = None
                for kt in range(nkv):
                    lastt = nc.tensor.matmul(
                        ps_sc[kt // 4][:, (kt % 4) * 128:(kt % 4) * 128 + 128],
                        PS[:, kt * 128:(kt + 1) * 128], PS[:, 2048:2176],
                        start=True, stop=True)
                sm.inc(lastt)
                sm.wait(nc.vector)
                lastev = None
                for s in range(nslab):
                    ncols = min(512, nkv * 128 - s * 512)
                    lastev = nc.vector.tensor_copy(
                        PS[:, 2176 + s * 512:2176 + s * 512 + ncols],
                        ps_sc[s][:, 0:ncols])
                sv.inc(lastev)
                sv.wait(nc.tensor)
                lastpv = None
                for kt in range(nkv):
                    va = A[:, kt * D + p * 128:kt * D + p * 128 + 64]
                    vb = A[:, kt * D + p * 128 + 64:kt * D + p * 128 + 128]
                    nc.tensor.matmul(pk[0:64, 0:64], va,
                                     PS[:, 2176 + kt * 128:2176 + kt * 128 + 64],
                                     start=(kt == 0), stop=(kt == nkv - 1),
                                     tile_position=(0, 0), skip_group_check=True)
                    lastpv = nc.tensor.matmul(
                        pk[64:128, 0:64], vb,
                        PS[:, 2176 + kt * 128 + 64:2176 + kt * 128 + 128],
                        start=(kt == 0), stop=(kt == nkv - 1),
                        tile_position=(0, 64), skip_group_check=True)
                sm.inc(lastpv)
                sm.wait(nc.vector)
                ev = nc.vector.tensor_copy(B[:, p * T + qbase:p * T + qbase + 64],
                                           pk[:, 0:64])
                sv.inc(ev)

    if stop_after in ("q", "attn"):
        return
    evt = []
    it = 0
    for qtr in range(4):
        view, tok, slot = ring.load(hbm_slab(ctx, cfg["wo"], qtr * 256,
                                             (qtr + 1) * 256))
        for t in range(NTT):
            pb, pid = (pk2, 1) if it % 2 == 0 else (pk, 0)
            it += 1
            nc.tensor.wait_ge(sw.h, tok)
            if it == 1:
                sv.wait(nc.tensor)
            proj_guard(pid)
            extra = [(ones[0:1, 0:128], brow(3, qtr * 256, (qtr + 1) * 256))] \
                if has_bias else []
            mm_acc(ctx, pb[:, 0:256],
                   [(B[:, k * T + t * 128:k * T + (t + 1) * 128], view(k, 0, 256))
                    for k in range(8)], sem=sm, extra=extra)
            sm.wait(nc.vector)
            ev = nc.vector.tensor_tensor(
                X[:, t * D + qtr * 256:t * D + (qtr + 1) * 256],
                X[:, t * D + qtr * 256:t * D + (qtr + 1) * 256],
                pb[:, 0:256], ALU.add)
            evt.append((pid, sv.inc(ev)))
        ring.mark_free(slot, sv, sv.count)



def emit_l2f(ctx):
    nc = ctx.nc
    X, A, B, C, QS = (ctx.ar[k] for k in ("X", "A", "B", "C", "QS"))
    sv, ss, sm, sw = ctx.sem("ve"), ctx.sem("se"), ctx.sem("mm"), ctx.sem("wld")
    pk, pk2 = ctx.psum["pk"], ctx.psum["pk2"]
    S = 1.0 / 127.0

    emit_ln(ctx, X, A)
    emit_transpose(ctx, A, B)
    tr_done = sv.count
    ctx.sems["ve"].wait(nc.tensor, at=tr_done)

    ring = WRing(ctx, nslots=3, cols=256)
    for hc in range(3):
        last_stt = 0
        for ph in range(4):
            c0 = hc * 1024 + ph * 256
            vg, tg, sg_ = ring.load(hbm_slab(ctx, "l2f_wg", c0, c0 + 256))
            vu, tu, su_ = ring.load(hbm_slab(ctx, "l2f_wu", c0, c0 + 256))
            for so in range(2):
                ho = ph * 2 + so
                for ts in range(4):
                    nc.tensor.wait_ge(sw.h, tg)
                    ctx.sems["se"].wait(nc.tensor)
                    mm_acc(ctx, pk[:, 0:512],
                           [(vg(k, so * 128, so * 128 + 128),
                             B[:, k * T + ts * 512:k * T + (ts + 1) * 512])
                            for k in range(8)], sem=sm)
                    sm.wait(nc.scalar)
                    ctx.sems["ve"].wait(nc.scalar, at=last_stt)
                    ss.inc(nc.scalar.activation(QS[:, 0:512], pk[:, 0:512],
                                                AF.Silu, scale=S))
                    nc.tensor.wait_ge(sw.h, tu)
                    ctx.sems["ve"].wait(nc.tensor)
                    mm_acc(ctx, pk2[:, 0:512],
                           [(vu(k, so * 128, so * 128 + 128),
                             B[:, k * T + ts * 512:k * T + (ts + 1) * 512])
                            for k in range(8)], sem=sm)
                    sm.wait(nc.vector)
                    ss.wait(nc.vector)
                    ev = nc.vector.scalar_tensor_tensor(
                        C[:, ho * T + ts * 512:ho * T + (ts + 1) * 512],
                        pk2[:, 0:512], S, QS[:, 0:512], ALU.mult, ALU.mult)
                    last_stt = sv.inc(ev)
            ring.mark_free(sg_, sv, sv.count)
            ring.mark_free(su_, sv, sv.count)
        dn_toks = []
        it = 0
        for qtr in range(4):
            vd, td, sd_ = ring.load(hbm_slab(ctx, "l2f_wd", qtr * 256,
                                             (qtr + 1) * 256, a0=8 * hc,
                                             a1=8 * hc + 8))
            for t in range(NTT):
                pb, pid = (pk, 0) if it % 2 == 0 else (pk2, 1)
                it += 1
                nc.tensor.wait_ge(sw.h, td)
                if it <= 2:
                    sv.wait(nc.tensor)
                    ctx.sems["se"].wait(nc.tensor)
                for q, tk in dn_toks[-2:]:
                    if q == pid:
                        ctx.sems["ve"].wait(nc.tensor, at=tk)
                mm_acc(ctx, pb[:, 0:256],
                       [(C[:, k * T + t * 128:k * T + (t + 1) * 128],
                         vd(k, 0, 256)) for k in range(8)], sem=sm)
                sm.wait(nc.vector)
                ev = nc.vector.scalar_tensor_tensor(
                    X[:, t * D + qtr * 256:t * D + (qtr + 1) * 256],
                    pb[:, 0:256], S,
                    X[:, t * D + qtr * 256:t * D + (qtr + 1) * 256],
                    ALU.mult, ALU.add)
                dn_toks.append((pid, sv.inc(ev)))
            ring.mark_free(sd_, sv, sv.count)



def emit_l1f(ctx):
    nc = ctx.nc
    X, A, B, C, QS, K32 = (ctx.ar[k] for k in ("X", "A", "B", "C", "QS", "K32"))
    sv, ss, sm, sw = ctx.sem("ve"), ctx.sem("se"), ctx.sem("mm"), ctx.sem("wld")
    sp = ctx.sem("spill")
    pk, pk2 = ctx.psum["pk"], ctx.psum["pk2"]
    scr = ctx.dram["scr_h"]

    emit_ln(ctx, X, A)
    emit_transpose(ctx, A, B)
    ctx.sems["ve"].wait(nc.tensor)

    sld = ctx.sem("ldx")
    for i, nm in enumerate(("l1f_s1", "l1f_s2", "l1f_sh1", "l1f_sh2")):
        sld.inc(nc.sync.dma_start(out=K32[:, i * 32:(i + 1) * 32],
                                  in_=ctx.dram[nm][:]), 16)
    sld.inc(nc.sync.dma_start(out=ctx.ar["S32"][:, 0:128],
                              in_=ctx.dram["l1f_s3b"][:, 0:128]), 16)
    sld.wait(nc.vector)
    sld.wait(nc.scalar)

    ring = WRing(ctx, nslots=3, cols=256)
    sc = lambda i, o: K32[:, i * 32 + o:i * 32 + o + 1]

    last_stt = 0
    last_spill = [0, 0]
    for ob in range(16):
        v1, t1, s1_ = ring.load(hbm_slab(ctx, "l1f_w1", ob * 256, (ob + 1) * 256))
        v2, t2, s2_ = ring.load(hbm_slab(ctx, "l1f_w2", ob * 256, (ob + 1) * 256))
        buf = 2 + (ob % 2) * 2
        for so in range(2):
            go = ob * 2 + so
            for ts in range(4):
                nc.tensor.wait_ge(sw.h, t1)
                ctx.sems["se"].wait(nc.tensor)
                mm_acc(ctx, pk[:, 0:512],
                       [(v1(k, so * 128, so * 128 + 128),
                         B[:, k * T + ts * 512:k * T + (ts + 1) * 512])
                        for k in range(8)], sem=sm)
                sm.wait(nc.scalar)
                ctx.sems["ve"].wait(nc.scalar, at=last_stt)
                ss.inc(nc.scalar.activation(
                    C[:, so * T + ts * 512:so * T + (ts + 1) * 512],
                    pk[:, 0:512], AF.Silu, scale=sc(0, go)))
                nc.tensor.wait_ge(sw.h, t2)
                ctx.sems["ve"].wait(nc.tensor)
                mm_acc(ctx, pk2[:, 0:512],
                       [(v2(k, so * 128, so * 128 + 128),
                         B[:, k * T + ts * 512:k * T + (ts + 1) * 512])
                        for k in range(8)], sem=sm)
                sm.wait(nc.vector)
                ss.wait(nc.vector)
                ctx.sems["spill"].wait(nc.vector, at=last_spill[ob % 2])
                ev = nc.vector.scalar_tensor_tensor(
                    C[:, (buf + so) * T + ts * 512:(buf + so) * T + (ts + 1) * 512],
                    pk2[:, 0:512], sc(1, go),
                    C[:, so * T + ts * 512:so * T + (ts + 1) * 512],
                    ALU.mult, ALU.mult)
                last_stt = sv.inc(ev)
        ring.mark_free(s1_, sv, sv.count)
        ring.mark_free(s2_, sv, sv.count)
        sv.wait(nc.sync)
        last_spill[ob % 2] = sp.inc(nc.sync.dma_start(
            out=scr.rearrange("(a p) c -> p a c", p=128)[:, 2 * ob:2 * ob + 2, :],
            in_=C[:, buf * T:(buf + 2) * T].rearrange("p (a c) -> p a c", a=2)), 16)

    sldh = ctx.sem("ldh")
    for ch in range(4):
        c0 = ch * 512
        sp.wait(nc.sync)
        if ch == 0:
            ctx.sems["ve"].wait(nc.sync)
        ldtok = sldh.inc(nc.sync.dma_start(
            out=A[:].rearrange("p (a c) -> p a c", a=32),
            in_=scr.rearrange("(a p) c -> p a c", p=128)[:, :, c0:c0 + 512]), 16)

        def big_mm(dst, dst_w, src, src_w, wname, scol, func):
            toks = []
            for o in range(32):
                va, ta, sa_ = ring.load(hbm_slab(ctx, wname, o * 128,
                                                 (o + 1) * 128, a0=0, a1=16))
                vb, tb, sb_ = ring.load(hbm_slab(ctx, wname, o * 128,
                                                 (o + 1) * 128, a0=16, a1=32))
                pb, pid = (pk, 0) if o % 2 == 0 else (pk2, 1)
                nc.tensor.wait_ge(sw.h, tb)
                nc.tensor.wait_ge(sldh.h, ldtok)
                for q, tk in toks[-2:]:
                    if q == pid:
                        ctx.sems["se"].wait(nc.tensor, at=tk)
                pairs = [(va(k), src[:, k * src_w:k * src_w + 512]) for k in range(16)] +                         [(vb(k), src[:, (16 + k) * src_w:(16 + k) * src_w + 512]) for k in range(16)]
                mm_acc(ctx, pb[:, 0:512], pairs, sem=sm)
                sm.wait(nc.scalar)
                ev = nc.scalar.activation(dst[:, o * dst_w:o * dst_w + 512],
                                          pb[:, 0:512], func, scale=scol(o))
                toks.append((pid, ss.inc(ev)))
                ring.mark_free(sa_, ss, ss.count)
                ring.mark_free(sb_, ss, ss.count)

        big_mm(C, 512, A, 512, "l1f_h1", lambda o: sc(2, o), AF.Gelu)
        ss.wait(nc.tensor)
        big_mm(A, 512, C, 512, "l1f_h2", lambda o: sc(3, o), AF.Gelu)
        ss.wait(nc.tensor)
        ytoks = []
        it = 0
        for oq in range(8):
            vw, tw, sw_ = ring.load(hbm_slab(ctx, "l1f_w3", oq * 128,
                                             (oq + 1) * 128, a0=0, a1=16))
            vw2, tw2, sw2_ = ring.load(hbm_slab(ctx, "l1f_w3", oq * 128,
                                                (oq + 1) * 128, a0=16, a1=32))
            if len(ytoks) >= 8:
                nc.sync.wait_ge(sv.h, ytoks[-8][1] if False else ytoks[-5][1])
            s3tok = sld.inc(nc.sync.dma_start(
                out=ctx.ar["S32"][:, (oq % 2) * 128:(oq % 2) * 128 + 128],
                in_=ctx.dram["l1f_s3b"][:, oq * 128:(oq + 1) * 128]), 16)
            for tt in range(4):
                pb, pid = (pk, 0) if it % 2 == 0 else (pk2, 1)
                it += 1
                nc.tensor.wait_ge(sw.h, tw2)
                for q, tk in ytoks[-2:]:
                    if q == pid:
                        ctx.sems["ve"].wait(nc.tensor, at=tk)
                pairs = [(A[:, k * 512 + tt * 128:k * 512 + tt * 128 + 128],
                          vw(k)) for k in range(16)] +                         [(A[:, (16 + k) * 512 + tt * 128:(16 + k) * 512 + tt * 128 + 128],
                          vw2(k)) for k in range(16)]
                mm_acc(ctx, pb[:, 0:128], pairs, sem=sm)
                sm.wait(nc.vector)
                sld.wait(nc.vector, at=s3tok)
                gt = (c0 + tt * 128)
                xsl = X[:, (gt // 128) * D + oq * 128:(gt // 128) * D + (oq + 1) * 128]
                nc.vector.tensor_tensor(ctx.ar["F128"][:],
                                        pb[:, 0:128],
                                        ctx.ar["S32"][:, (oq % 2) * 128:(oq % 2) * 128 + 128],
                                        ALU.mult)
                ev = nc.vector.tensor_tensor(xsl, xsl, ctx.ar["F128"][:], ALU.add)
                ytoks.append((pid, sv.inc(ev)))
            ring.mark_free(sw_, sv, sv.count)
            ring.mark_free(sw2_, sv, sv.count)
        sv.wait(nc.sync)
        sv.wait(nc.tensor)
        ss.wait(nc.tensor)



def emit_l3f(ctx):
    nc = ctx.nc
    X, A, B, C, QS, S32, KB, F128 = (ctx.ar[k] for k in
                                     ("X", "A", "B", "C", "QS", "S32", "KB", "F128"))
    V32 = ctx.ar["V32"]
    sv, ss, sm, sw = ctx.sem("ve"), ctx.sem("se"), ctx.sem("mm"), ctx.sem("wld")
    sld = ctx.sem("ldx")
    pk, pk2 = ctx.psum["pk"], ctx.psum["pk2"]
    ptA = ctx.psum["ptA"]
    eyebf = ctx.ar["eye_bf"]
    ones = ctx.ar["ones_bf"]

    emit_ln(ctx, X, A)
    emit_transpose(ctx, A, B)
    ctx.sems["ve"].wait(nc.tensor)

    sw.inc(nc.gpsimd.dma_start(
        out=KB[:, 0:48].rearrange("p (a c) -> p a c", a=8),
        in_=hbm_slab(ctx, "l3f_proj", 0, 6)), 16)
    K32 = ctx.ar["K32"]
    sld.inc(nc.sync.dma_start(out=K32[0:6, 128:134], in_=ctx.dram["l3f_anchT"][:]), 16)
    sld.inc(nc.sync.dma_start(out=K32[:, 256:384], in_=ctx.dram["eye"][:]), 16)
    sld.inc(nc.sync.dma_start(out=V32[:, 245:246], in_=ctx.dram["l3f_temp"][:]), 16)
    sld.wait(nc.vector)
    vv(ctx, nc.vector.tensor_scalar(V32[:, 246:247], V32[:, 245:246], 0.1, None,
                                    ALU.max))
    vv(ctx, nc.vector.reciprocal(V32[:, 247:248], V32[:, 246:247]))
    sv.inc(nc.vector.tensor_scalar(F128[:, 10:11], V32[:, 247:248], 0.5, None,
                                   ALU.mult))
    sv.wait(nc.vector)
    sv.wait(nc.tensor)
    sw.wait(nc.tensor)

    ss.wait(nc.tensor)
    for t in range(NTT):
        ss.wait(nc.tensor)
        import os as _os2
        _dbg3 = _os2.environ.get("L3F_DEBUG") == "3"
        if _dbg3:
            sv.wait(nc.tensor)
        mm_acc(ctx, pk[:, 0:6],
               [(B[:, k * T + t * 128:k * T + (t + 1) * 128],
                 KB[:, k * 6:(k + 1) * 6]) for k in range(8)], sem=sm)
        if _dbg3:
            sm.wait(nc.vector)
            sv.inc(nc.vector.tensor_copy(K32[:, 640 + t * 6:646 + t * 6],
                                         pk[:, 0:6]))
        sm.wait(nc.scalar)
        ss.inc(nc.scalar.activation(F128[:, 64:70], pk[:, 0:6], AF.Tanh,
                                    scale=1.0 / 0.3))
        ss.wait(nc.vector)
        nc.vector.memset(F128[:, 70:72], 0.0)
        sv.inc(vspacer(ctx))
        ss.wait(nc.tensor)
        sv.wait(nc.tensor)
        sm.inc(nc.tensor.transpose(ctx.psum["ps0"][0:8, 0:128], F128[:, 64:72],
                                   K32[:, 256:384]))
        sm.wait(nc.vector)
        sv.inc(nc.vector.tensor_copy(K32[0:8, 0:128], ctx.psum["ps0"][0:8, 0:128]))
        sv.wait(nc.tensor)
        mm_acc(ctx, pk2[:, 0:6],
               [(K32[0:6, 0:128], K32[0:6, 128:134])], sem=sm)
        sm.wait(nc.vector)
        if _dbg3:
            nc.vector.tensor_copy(S32[:, 96 + t * 6:102 + t * 6], pk2[:, 0:6])
            sv.inc(nc.vector.tensor_copy(K32[:, 512 + t * 6:518 + t * 6],
                                         F128[:, 64:70]))
        d6 = pk2[:, 0:6]
        m1 = F128[:, 8:9]
        vv(ctx, nc.vector.tensor_reduce(m1, d6, AX.X, ALU.max))
        sv.inc(nc.vector.scalar_tensor_tensor(F128[:, 11:12], m1, -1.0,
                                              F128[:, 10:11], ALU.mult, ALU.mult))
        sv.wait(nc.vector)
        sv.wait(nc.scalar)
        ss.inc(nc.scalar.activation(F128[:, 40:46], d6, AF.Exp,
                                    scale=F128[:, 10:11], bias=F128[:, 11:12]))
        ss.wait(nc.vector)
        E = F128[:, 40:46]
        vv(ctx, nc.vector.tensor_reduce(F128[:, 15:16], E, AX.X, ALU.max))
        vv(ctx, nc.vector.tensor_scalar(F128[:, 16:22], E, F128[:, 15:16], None,
                                        ALU.is_equal))
        vv(ctx, nc.vector.scalar_tensor_tensor(F128[:, 24:30], F128[:, 16:22],
                                               NEG, E, ALU.mult, ALU.add))
        vv(ctx, nc.vector.tensor_reduce(F128[:, 9:10], F128[:, 24:30], AX.X,
                                        ALU.max))
        vv(ctx, nc.vector.tensor_scalar(F128[:, 13:14], F128[:, 9:10], 1.0,
                                        None, ALU.add))
        vv(ctx, nc.vector.reciprocal(F128[:, 14:15], F128[:, 13:14]))
        vv(ctx, nc.vector.tensor_scalar(F128[:, 48:54], E, F128[:, 9:10], None,
                                        ALU.is_ge))
        lw = nc.vector.scalar_tensor_tensor(S32[:, t * 6:t * 6 + 6], E,
                                            F128[:, 14:15], F128[:, 48:54],
                                            ALU.mult, ALU.mult)
        sv.inc(lw)
        sv.wait(nc.vector)
        sv.wait(nc.tensor)
    import os as _os
    if _os.environ.get("L3F_DEBUG"):
        dbg = ctx.dram["l3f_dbg"]
        sdb = ctx.sem("st")
        sv.wait(nc.sync)
        sdb.inc(nc.sync.dma_start(out=dbg[:, 0:96], in_=S32[:, 0:96]), 16)
        sdb.inc(nc.sync.dma_start(out=dbg[:, 96:224], in_=F128[:, 0:128]), 16)
        sdb.wait(nc.sync)

    import os as _os
    _ne = int(_os.environ.get("L3F_NE", "6"))
    ring = WRing(ctx, nslots=3, cols=256)
    for e in range(_ne):
        etoks = []
        for o in range(4):
            vw, tw, sl_ = ring.load(hbm_slab3(ctx, "l3f_w1", e, o * 128,
                                              (o + 1) * 128))
            for ts in range(4):
                nc.tensor.wait_ge(sw.h, tw)
                for q, tk in etoks[-2:]:
                    if q == 0:
                        ctx.sems["se"].wait(nc.tensor, at=tk)
                mm_acc(ctx, pk[:, 0:512],
                       [(vw(k, 0, 128), B[:, k * T + ts * 512:k * T + (ts + 1) * 512])
                        for k in range(8)], sem=sm)
                sm.wait(nc.scalar)
                ev = nc.scalar.activation(C[:, o * T + ts * 512:o * T + (ts + 1) * 512],
                                          pk[:, 0:512], AF.Gelu_apprx_tanh)
                etoks.append((0, ss.inc(ev)))
            ring.mark_free(sl_, ss, ss.count)
        import os as _os
        if _os.environ.get("L3F_PART") == "h":
            continue
        if _os.environ.get("L3F_DEBUG") == "3" and e == _ne - 1:
            sdb = ctx.sem("st")
            sv.wait(nc.sync)
            sdb.inc(nc.sync.dma_start(out=ctx.dram["l3f_dbg3"][:, 0:1024],
                                      in_=K32[:, 0:1024]), 16)
            sdb.inc(nc.sync.dma_start(out=ctx.dram["l3f_dbg3"][:, 1024:1280],
                                      in_=S32[:, 0:256]), 16)
            sdb.wait(nc.sync)
        if _os.environ.get("L3F_DEBUG") == "2" and e == 0:
            sdb = ctx.sem("st")
            ss.wait(nc.sync)
            sdb.inc(nc.sync.dma_start(out=ctx.dram["l3f_dbg2"][:],
                                      in_=C[:, 0:2048]), 16)
            sdb.wait(nc.sync)
        ss.wait(nc.tensor)
        ytoks = []
        it = 0
        for oq in range(4):
            vw, tw, sl_ = ring.load(hbm_slab3(ctx, "l3f_w2", e, oq * 256,
                                              (oq + 1) * 256, kt=4))
            for t in range(NTT):
                pb, pid = (pk2, 1) if it % 2 == 0 else (pk, 0)
                it += 1
                nc.tensor.wait_ge(sw.h, tw)
                for q, tk in ytoks[-2:]:
                    if q == pid:
                        ctx.sems["ve"].wait(nc.tensor, at=tk)
                mm_acc(ctx, pb[:, 0:256],
                       [(C[:, k * T + t * 128:k * T + (t + 1) * 128],
                         vw(k, 0, 256)) for k in range(4)], sem=sm)
                sm.wait(nc.vector)
                xsl = X[:, t * D + oq * 256:t * D + (oq + 1) * 256]
                ev = nc.vector.scalar_tensor_tensor(
                    xsl, pb[:, 0:256], S32[:, t * 6 + e:t * 6 + e + 1], xsl,
                    ALU.mult, ALU.add)
                ytoks.append((pid, sv.inc(ev)))
            ring.mark_free(sl_, sv, sv.count)
        sv.wait(nc.tensor)


def hbm_slab3(ctx, name, e, c0, c1, kt=8, a0=0, a1=None):
    t = ctx.dram[name]
    if len(t.shape) == 6:
        _, nA, nC, _, ktg, cg = t.shape
        assert c1 - c0 == cg and c0 % cg == 0 and a0 % ktg == 0, (name, c0, c1, a0)
        return t[e][a0 // ktg][c0 // cg]
    r = t.rearrange("e (a p) c -> e p a c", p=128)
    if a1 is None:
        a1 = r.shape[2]
    return r[e][:, a0:a1, c0:c1]



def emit_l0f(ctx):
    nc = ctx.nc
    X, A, B, C, QS, PS, S32, KB, K32, F128 = (ctx.ar[k] for k in
        ("X", "A", "B", "C", "QS", "PS", "S32", "KB", "K32", "F128"))
    V32 = ctx.ar["V32"]
    sv, ss, sm, sw = ctx.sem("ve"), ctx.sem("se"), ctx.sem("mm"), ctx.sem("wld")
    sp, sld = ctx.sem("spill"), ctx.sem("ldx")
    pk, pk2 = ctx.psum["pk"], ctx.psum["pk2"]
    ps_sc = [ctx.psum["ps0"], ctx.psum["ps1"], ctx.psum["ps2"], ctx.psum["ps3"]]
    eyebf = ctx.ar["eye_bf"]
    ones = ctx.ar["ones_bf"]
    maskA, maskB = ctx.ar["maskA"], ctx.ar["maskB"]
    maskWA, maskWB = ctx.ar["maskWA"], ctx.ar["maskWB"]
    scr = ctx.dram["scr_x"]

    sv.wait(nc.sync)
    ctx.sems["se"].wait(nc.sync)
    sp.inc(nc.sync.dma_start(out=scr.rearrange("(n p) d -> p n d", p=128),
                             in_=X[:].rearrange("p (n d) -> p n d", n=NTT)), 16)
    sp.wait(nc.scalar)
    sp.wait(nc.vector)
    scl, bia = emit_ln(ctx, X, A)
    lastf = None
    for t in range(NTT):
        lastf = nc.scalar.activation(X[:, t * D:(t + 1) * D], X[:, t * D:(t + 1) * D],
                                     AF.Identity, scale=scl[:, t:t + 1],
                                     bias=bia[:, t:t + 1])
    ss.inc(lastf)

    sw.inc(nc.gpsimd.dma_start(out=KB[:, 56:57], in_=ctx.dram["onescol"][:]), 16)
    sw.inc(nc.gpsimd.dma_start(out=KB[:, 58:60], in_=ctx.dram["hsel"][:]), 16)
    sw.inc(nc.gpsimd.dma_start(out=KB[0:2, 64:192], in_=ctx.dram["hselT"][:]), 16)
    sld.inc(nc.sync.dma_start(out=S32[:, 216:220], in_=ctx.dram["l0f_qks"][:]), 16)
    sld.inc(nc.sync.dma_start(out=S32[:, 184:216], in_=ctx.dram["l0f_lgb"][:]), 16)
    sld.inc(nc.sync.dma_start(out=S32[:, 96:140], in_=ctx.dram["l0f_b1a"][:]), 16)
    sld.inc(nc.sync.dma_start(out=S32[:, 140:184], in_=ctx.dram["l0f_b1g"][:]), 16)
    sw.inc(nc.gpsimd.dma_start(out=KB[0:1, 512:2560], in_=ctx.dram["l0f_b2"][:]), 16)
    sld.wait(nc.vector)
    sv.inc(nc.vector.tensor_scalar(S32[:, 220:222], S32[:, 218:220], 8.0, None,
                                   ALU.mult))

    ring = WRing(ctx, nslots=3, cols=256)

    import os as _os
    _stop = _os.environ.get("L0F_STOP", "full")
    _nl = 0 if _stop == "ln" else (1 if _stop in ("attn0", "layer0") else 2)
    for l in range(_nl):
        if l == 1:
            sv.wait(nc.scalar)
            ctx.sems["wld"].wait(nc.scalar)
            lastc = None
            for t in range(NTT):
                lastc = nc.scalar.activation(A[:, t * D:(t + 1) * D],
                                             X[:, t * D:(t + 1) * D], AF.Identity)
            ss.inc(lastc)
        emit_transpose(ctx, A, B)
        sv.wait(nc.vector)
        lastsh = None
        for ft in range(8):
            if ft < 4:
                lastsh = nc.vector.tensor_copy(C[:, ft * T:(ft + 1) * T],
                                               B[:, ft * T:(ft + 1) * T])
            else:
                nc.vector.memset(C[:, ft * T:ft * T + 1], 0.0)
                lastsh = nc.vector.tensor_copy(C[:, ft * T + 1:(ft + 1) * T],
                                               B[:, ft * T:(ft + 1) * T - 1])
        sv.inc(lastsh)
        sv.wait(nc.tensor)

        evt = []
        for which in range(2):
            for half in range(2):
                c0 = which * 512 + half * 256
                vw, tw, sl_ = ring.load(hbm_slab3(ctx, "l0f_wqkv", l, c0, c0 + 256))
                for so in range(2):
                    o = half * 2 + so
                    dst = (which * 4 + o) * T
                    for ts in range(4):
                        pb, pid = (pk, 0) if (len(evt)) % 2 == 0 else (pk2, 1)
                        nc.tensor.wait_ge(sw.h, tw)
                        for q, tk in evt[-2:]:
                            if q == pid:
                                ctx.sems["ve"].wait(nc.tensor, at=tk)
                        mm_acc(ctx, pb[:, 0:512],
                               [(vw(k, so * 128, so * 128 + 128),
                                 C[:, k * T + ts * 512:k * T + (ts + 1) * 512])
                                for k in range(8)], sem=sm)
                        sm.wait(nc.vector)
                        ev = nc.vector.tensor_copy(
                            B[:, dst + ts * 512:dst + (ts + 1) * 512], pb[:, 0:512])
                        evt.append((pid, sv.inc(ev)))
                ring.mark_free(sl_, sv, sv.count)
        for half in range(2):
            c0 = 1024 + half * 256
            vw, tw, sl_ = ring.load(hbm_slab3(ctx, "l0f_wqkv", l, c0, c0 + 256))
            for t in range(NTT):
                pb, pid = (pk, 0) if (len(evt)) % 2 == 0 else (pk2, 1)
                nc.tensor.wait_ge(sw.h, tw)
                for q, tk in evt[-2:]:
                    if q == pid:
                        ctx.sems["ve"].wait(nc.tensor, at=tk)
                mm_acc(ctx, pb[:, 0:256],
                       [(C[:, k * T + t * 128:k * T + (t + 1) * 128],
                         vw(k, 0, 256)) for k in range(8)], sem=sm)
                sm.wait(nc.vector)
                ev = nc.vector.tensor_copy(
                    A[:, t * 512 + half * 256:t * 512 + (half + 1) * 256],
                    pb[:, 0:256])
                evt.append((pid, sv.inc(ev)))
            ring.mark_free(sl_, sv, sv.count)
        sv.wait(nc.tensor)
        sv.wait(nc.scalar)

        for which in range(2):
            spc = S32[:, 220 + l:221 + l] if which == 0 else S32[:, 216 + l:217 + l]
            for p in range(4):
                bt = (which * 4 + p) * T
                for th in range(2):
                    for sl2 in range(2):
                        cols = th * 1024 + sl2 * 512
                        sm.wait(nc.scalar)
                        ss.inc(nc.scalar.activation(
                            QS[:, sl2 * 512:(sl2 + 1) * 512],
                            B[:, bt + cols:bt + cols + 512], AF.Square))
                        ss.wait(nc.tensor)
                        sv.wait(nc.tensor)
                        mm_acc(ctx, ps_sc[sl2][0:2, 0:512],
                               [(KB[:, 58:60], QS[:, sl2 * 512:(sl2 + 1) * 512])],
                               sem=sm)
                    sm.wait(nc.vector)
                    for sl2 in range(2):
                        vv(ctx, nc.vector.tensor_copy(
                            K32[0:2, sl2 * 512:(sl2 + 1) * 512],
                            ps_sc[sl2][0:2, 0:512]))
                    sv.wait(nc.scalar)
                    sssync(ctx, nc.scalar.activation(K32[32:34, 0:1024],
                                                     K32[0:2, 0:1024], AF.Sqrt))
                    ss.wait(nc.vector)
                    vv(ctx, nc.vector.tensor_scalar(K32[64:66, 0:1024],
                                                    K32[32:34, 0:1024], 1e-12,
                                                    None, ALU.max))
                    vv(ctx, nc.vector.reciprocal(K32[96:98, 0:1024],
                                                 K32[64:66, 0:1024]))
                    sv.inc(nc.vector.tensor_copy(QS[0:2, 1024:2048],
                                                 K32[96:98, 0:1024]))
                    sv.wait(nc.vector)
                    sv.wait(nc.tensor)
                    for sl2 in range(2):
                        mm_acc(ctx, ps_sc[2 + sl2][:, 0:512],
                               [(KB[0:2, 64:192],
                                 QS[0:2, 1024 + sl2 * 512:1024 + (sl2 + 1) * 512])],
                               sem=sm)
                    sm.wait(nc.vector)
                    for sl2 in range(2):
                        cols = th * 1024 + sl2 * 512
                        lastn = nc.vector.scalar_tensor_tensor(
                            B[:, bt + cols:bt + cols + 512],
                            B[:, bt + cols:bt + cols + 512], spc,
                            ps_sc[2 + sl2][:, 0:512], ALU.mult, ALU.mult)
                        sv.inc(lastn)
                    sv.wait(nc.tensor)

        for which in range(2):
            nm_c = "rotqc" if which == 0 else "rotkc"
            nm_s = "rotqs" if which == 0 else "rotks"
            vc, tc, slc = ring.load(ctx.dram[nm_c].rearrange("p (a c) -> p a c", a=1))
            vs_, ts_, sls = ring.load(ctx.dram[nm_s].rearrange("p (a c) -> p a c", a=1))
            nc.vector.wait_ge(sw.h, ts_)
            for p in range(4):
                bt = (which * 4 + p) * T
                nc.vector.tensor_scalar(QS[:, 2048:4096].rearrange(
                    "p (h c) -> p h c", h=2)[0:0 + 32, :, :] if False else
                    QS[0:32, 2048:4096], B[32:64, bt:bt + T], -1.0, None, ALU.mult)
                nc.vector.tensor_copy(QS[32:64, 2048:4096], B[0:32, bt:bt + T])
                nc.vector.tensor_scalar(QS[64:96, 2048:4096], B[96:128, bt:bt + T],
                                        -1.0, None, ALU.mult)
                nc.vector.tensor_copy(QS[96:128, 2048:4096], B[64:96, bt:bt + T])
                nc.vector.tensor_tensor(QS[:, 0:2048], B[:, bt:bt + T],
                                        vc(0, 0, 2048), ALU.mult)
                nc.vector.tensor_tensor(QS[:, 2048:4096], QS[:, 2048:4096],
                                        vs_(0, 0, 2048), ALU.mult)
                lastr = nc.vector.tensor_tensor(B[:, bt:bt + T], QS[:, 0:2048],
                                                QS[:, 2048:4096], ALU.add)
                sv.inc(lastr)
            ring.mark_free(slc, sv, sv.count)
            ring.mark_free(sls, sv, sv.count)
        sv.wait(nc.tensor)

        for p in range(4):
            for c in range(32):
                qbase = c * 64
                kv0 = max(0, (qbase - 512) // 128)
                kvL = (qbase + 63) // 128
                nkv = kvL - kv0 + 1
                sv.wait(nc.tensor)
                lastmm = None
                for s in range((nkv + 3) // 4):
                    t0 = kv0 + s * 4
                    ncols = min(4, kvL + 1 - t0) * 128
                    qa = B[0:64, p * T + qbase:p * T + qbase + 64]
                    qb = B[64:128, p * T + qbase:p * T + qbase + 64]
                    ka = B[0:64, (4 + p) * T + t0 * 128:(4 + p) * T + t0 * 128 + ncols]
                    kb = B[64:128, (4 + p) * T + t0 * 128:(4 + p) * T + t0 * 128 + ncols]
                    nc.tensor.matmul(ps_sc[s][0:64, 0:ncols], qa, ka,
                                     start=True, stop=True, tile_position=(0, 0))
                    lastmm = nc.tensor.matmul(ps_sc[s][64:128, 0:ncols], qb, kb,
                                              start=True, stop=True,
                                              tile_position=(64, 64))
                sm.inc(lastmm)
                sm.wait(nc.vector)
                dloc = kvL - kv0
                mk = maskA if (qbase % 128) == 0 else maskB
                nc.vector.tensor_tensor(
                    ps_sc[dloc // 4][:, (dloc % 4) * 128:(dloc % 4) * 128 + 128],
                    ps_sc[dloc // 4][:, (dloc % 4) * 128:(dloc % 4) * 128 + 128],
                    mk[:], ALU.add)
                if qbase >= 512:
                    mw = maskWA if (qbase % 128) == 0 else maskWB
                    nc.vector.tensor_tensor(ps_sc[0][:, 0:128], ps_sc[0][:, 0:128],
                                            mw[:], ALU.add)
                sv.inc(vspacer(ctx))
                sv.wait(nc.vector)
                nm = S32[:, 0:4]
                lastv = None
                for s in range((nkv + 3) // 4):
                    ncols = min(4, kvL + 1 - (kv0 + s * 4)) * 128
                    lastv = nc.vector.tensor_reduce(nm[:, s:s + 1],
                                                    ps_sc[s][:, 0:ncols],
                                                    AX.X, ALU.max, negate=True)
                sv.inc(lastv)
                sv.wait(nc.vector)
                lastv = nc.vector.tensor_reduce(S32[:, 8:9], nm[:, 0:(nkv + 3) // 4],
                                                AX.X, ALU.min)
                sv.inc(lastv)
                sv.wait(nc.vector)
                sv.wait(nc.scalar)
                sm.wait(nc.scalar)
                laste = None
                for s in range((nkv + 3) // 4):
                    ncols = min(4, kvL + 1 - (kv0 + s * 4)) * 128
                    laste = nc.scalar.activation(
                        PS[:, s * 512:s * 512 + ncols], ps_sc[s][:, 0:ncols],
                        AF.Exp, bias=S32[:, 8:9], accum_out=S32[:, 12 + s:13 + s])
                ss.inc(laste)
                ss.wait(nc.vector)
                vv(ctx, nc.vector.tensor_reduce(S32[:, 9:10],
                                                S32[:, 12:12 + (nkv + 3) // 4],
                                                AX.X, ALU.add))
                vv(ctx, nc.vector.reciprocal(S32[:, 11:12], S32[:, 9:10]))
                lastd = nc.vector.tensor_scalar(PS[:, 2048:2176], eyebf[:],
                                                S32[:, 11:12], None, ALU.mult)
                sv.inc(lastd)
                ss.wait(nc.tensor)
                sv.wait(nc.tensor)
                lastt = None
                for kt in range(nkv):
                    lastt = nc.tensor.matmul(
                        ps_sc[kt // 4][:, (kt % 4) * 128:(kt % 4) * 128 + 128],
                        PS[:, kt * 128:(kt + 1) * 128], PS[:, 2048:2176],
                        start=True, stop=True)
                sm.inc(lastt)
                sm.wait(nc.vector)
                lastev = None
                for s in range((nkv + 3) // 4):
                    ncols = min(4, kvL + 1 - (kv0 + s * 4)) * 128
                    lastev = nc.vector.tensor_copy(
                        PS[:, 2176 + s * 512:2176 + s * 512 + ncols],
                        ps_sc[s][:, 0:ncols])
                sv.inc(lastev)
                sv.wait(nc.tensor)
                lastpv = None
                for kt in range(nkv):
                    gt = kv0 + kt
                    va = A[:, gt * 512 + p * 128:gt * 512 + p * 128 + 64]
                    vb = A[:, gt * 512 + p * 128 + 64:gt * 512 + p * 128 + 128]
                    nc.tensor.matmul(pk[0:64, 0:64], va,
                                     PS[:, 2176 + kt * 128:2176 + kt * 128 + 64],
                                     start=(kt == 0), stop=(kt == nkv - 1),
                                     tile_position=(0, 0), skip_group_check=True)
                    lastpv = nc.tensor.matmul(
                        pk[64:128, 0:64], vb,
                        PS[:, 2176 + kt * 128 + 64:2176 + kt * 128 + 128],
                        start=(kt == 0), stop=(kt == nkv - 1),
                        tile_position=(0, 64), skip_group_check=True)
                sm.inc(lastpv)
                sm.wait(nc.vector)
                ev = nc.vector.tensor_copy(
                    A[:, 8192 + p * T + qbase:8192 + p * T + qbase + 64],
                    pk[:, 0:64])
                sv.inc(ev)

        evt = []
        it = 0
        for qtr in range(4):
            vw, tw, sl_ = ring.load(hbm_slab3(ctx, "l0f_wo", l, qtr * 256,
                                              (qtr + 1) * 256, kt=4))
            for t in range(NTT):
                pb, pid = (pk2, 1) if it % 2 == 0 else (pk, 0)
                it += 1
                nc.tensor.wait_ge(sw.h, tw)
                if it == 1:
                    sv.wait(nc.tensor)
                for q, tk in evt[-2:]:
                    if q == pid:
                        ctx.sems["ve"].wait(nc.tensor, at=tk)
                mm_acc(ctx, pb[:, 0:256],
                       [(A[:, 8192 + k * T + t * 128:8192 + k * T + (t + 1) * 128],
                         vw(k, 0, 256)) for k in range(4)], sem=sm)
                sm.wait(nc.vector)
                xsl = X[:, t * D + qtr * 256:t * D + (qtr + 1) * 256]
                ev = nc.vector.tensor_tensor(xsl, xsl, pb[:, 0:256], ALU.add)
                evt.append((pid, sv.inc(ev)))
            ring.mark_free(sl_, sv, sv.count)
        sv.wait(nc.scalar)
        if _stop == "attn0":
            break

        lastc = None
        for t in range(NTT):
            lastc = nc.scalar.activation(A[:, t * D:(t + 1) * D],
                                         X[:, t * D:(t + 1) * D], AF.Identity)
        ss.inc(lastc)
        emit_transpose(ctx, A, B)
        sv.wait(nc.vector)
        lastsh = None
        for ft in range(8):
            if ft < 4:
                lastsh = nc.vector.tensor_copy(C[:, ft * T:(ft + 1) * T],
                                               B[:, ft * T:(ft + 1) * T])
            else:
                nc.vector.memset(C[:, ft * T:ft * T + 1], 0.0)
                lastsh = nc.vector.tensor_copy(C[:, ft * T + 1:(ft + 1) * T],
                                               B[:, ft * T:(ft + 1) * T - 1])
        sv.inc(lastsh)
        sv.wait(nc.tensor)
        sv.wait(nc.scalar)
        for ts in range(4):
            sqtoks = []
            ctx.sems["ve"].wait(nc.scalar)
            for ft in range(8):
                qh = (ft % 2) * 512
                if len(sqtoks) >= 2:
                    ctx.sems["mm"].wait(nc.scalar, at=sqtoks[-2])
                ss.inc(nc.scalar.activation(
                    QS[:, qh:qh + 512],
                    C[:, ft * T + ts * 512:ft * T + (ts + 1) * 512], AF.Square))
                ss.wait(nc.tensor)
                if ft == 0:
                    sv.wait(nc.tensor)
                nc.tensor.matmul(ps_sc[0][0:1, 0:512], KB[:, 56:57],
                                 QS[:, qh:qh + 512], start=(ft == 0),
                                 stop=(ft == 7), skip_group_check=True)
                tok = sm.inc(nc.tensor.matmul(
                    ps_sc[1][0:1, 0:512], KB[:, 56:57],
                    C[:, ft * T + ts * 512:ft * T + (ts + 1) * 512],
                    start=(ft == 0), stop=(ft == 7), skip_group_check=True))
                sqtoks.append(tok)
            sm.wait(nc.vector)
            for hh in range(2):
                sa = K32[0:1, 0:256]
                sq = K32[0:1, 256:512]
                si = K32[0:1, 512:768]
                sb = K32[0:1, 768:1024]
                c0h = hh * 256
                vv(ctx, nc.vector.tensor_copy(sa, ps_sc[1][0:1, c0h:c0h + 256]))
                vv(ctx, nc.vector.tensor_copy(sq, ps_sc[0][0:1, c0h:c0h + 256]))
                vv(ctx, nc.vector.tensor_scalar(sa, sa, 1.0 / D, None, ALU.mult))
                vv(ctx, nc.vector.tensor_scalar(sq, sq, 1.0 / D, None, ALU.mult))
                vv(ctx, nc.vector.tensor_tensor(si, sa, sa, ALU.mult))
                sv.inc(nc.vector.tensor_tensor(si, sq, si, ALU.subtract))
                sv.wait(nc.vector)
                sv.wait(nc.scalar)
                sssync(ctx, nc.scalar.activation(si, si, AF.Ln,
                                                 bias=V32[0:1, 240:241]))
                sssync(ctx, nc.scalar.activation(si, si, AF.Exp, scale=-0.5))
                ss.wait(nc.vector)
                vv(ctx, nc.vector.scalar_tensor_tensor(sb, sa, -1.0, si,
                                                       ALU.mult, ALU.mult))
                vv(ctx, nc.vector.tensor_copy(QS[0:1, 1024 + c0h:1280 + c0h], si))
                sv.inc(nc.vector.tensor_copy(QS[0:1, 1536 + c0h:1792 + c0h], sb))
                sv.wait(nc.vector)
                sv.wait(nc.tensor)
                mm_acc(ctx, ps_sc[2][:, c0h:c0h + 256],
                       [(ones[0:1, 0:128], QS[0:1, 1024 + c0h:1280 + c0h])],
                       sem=None)
                mm_acc(ctx, ps_sc[3][:, c0h:c0h + 256],
                       [(ones[0:1, 0:128], QS[0:1, 1536 + c0h:1792 + c0h])],
                       sem=sm)
            sm.wait(nc.vector)
            for ft in range(8):
                lg = S32[:, 184 + l * 8 + ft:185 + l * 8 + ft]
                lb = S32[:, 200 + l * 8 + ft:201 + l * 8 + ft]
                nc.vector.tensor_tensor(QS[:, 0:512],
                                        C[:, ft * T + ts * 512:ft * T + (ts + 1) * 512],
                                        ps_sc[2][:, 0:512], ALU.mult)
                nc.vector.tensor_tensor(QS[:, 512:1024], QS[:, 0:512],
                                        ps_sc[3][:, 0:512], ALU.add)
                lastl = nc.vector.tensor_scalar(
                    B[:, ft * T + ts * 512:ft * T + (ts + 1) * 512],
                    QS[:, 512:1024], lg, lb, ALU.mult, ALU.add)
                sv.inc(lastl)
            sv.wait(nc.tensor)

        for ts in range(4):
            htoks = []
            for i in range(22):
                vg, tg, slg = ring.load(hbm_slab3(ctx, "l0f_w1g", l, i * 128,
                                                  (i + 1) * 128))
                va, ta, sla = ring.load(hbm_slab3(ctx, "l0f_w1a", l, i * 128,
                                                  (i + 1) * 128))
                nc.tensor.wait_ge(sw.h, tg)
                ctx.sems["se"].wait(nc.tensor)
                mm_acc(ctx, pk[:, 0:512],
                       [(vg(k, 0, 128), B[:, k * T + ts * 512:k * T + (ts + 1) * 512])
                        for k in range(8)], sem=sm)
                sm.wait(nc.scalar)
                if htoks:
                    ctx.sems["ve"].wait(nc.scalar, at=htoks[-1])
                ss.inc(nc.scalar.activation(QS[:, 0:512], pk[:, 0:512], AF.Gelu,
                                            bias=S32[:, 140 + 22 * l + i:
                                                     141 + 22 * l + i]))
                nc.tensor.wait_ge(sw.h, ta)
                ctx.sems["ve"].wait(nc.tensor)
                mm_acc(ctx, pk2[:, 0:512],
                       [(va(k, 0, 128), B[:, k * T + ts * 512:k * T + (ts + 1) * 512])
                        for k in range(8)], sem=sm)
                sm.wait(nc.vector)
                ss.wait(nc.vector)
                ev = nc.vector.scalar_tensor_tensor(
                    A[:, i * 512:(i + 1) * 512], pk2[:, 0:512],
                    S32[:, 96 + 22 * l + i:97 + 22 * l + i], QS[:, 0:512],
                    ALU.add, ALU.mult)
                htoks.append(sv.inc(ev))
                ring.mark_free(slg, sv, sv.count)
                ring.mark_free(sla, sv, sv.count)
            sv.wait(nc.tensor)
            ytoks = []
            it = 0
            for oq in range(4):
                loads = []
                for kk in range(3):
                    a0, a1 = kk * 8, min(22, (kk + 1) * 8)
                    loads.append(ring.load(hbm_slab3(ctx, "l0f_w2", l, oq * 256,
                                                     (oq + 1) * 256, a0=a0, a1=a1)))
                for tt in range(4):
                    pb, pid = (pk, 0) if it % 2 == 0 else (pk2, 1)
                    it += 1
                    nc.tensor.wait_ge(sw.h, loads[-1][1])
                    for q, tk in ytoks[-2:]:
                        if q == pid:
                            ctx.sems["ve"].wait(nc.tensor, at=tk)
                    pairs = []
                    for kk in range(3):
                        a0, a1 = kk * 8, min(22, (kk + 1) * 8)
                        for k in range(a1 - a0):
                            pairs.append((A[:, (a0 + k) * 512 + tt * 128:
                                            (a0 + k) * 512 + tt * 128 + 128],
                                          loads[kk][0](k, 0, 256)))
                    extra = [(ones[0:1, 0:128],
                              KB[0:1, 512 + l * 1024 + oq * 256:
                                 512 + l * 1024 + (oq + 1) * 256])]
                    mm_acc(ctx, pb[:, 0:256], pairs, sem=sm, extra=extra)
                    sm.wait(nc.vector)
                    gt = ts * 4 + tt
                    xsl = X[:, gt * D + oq * 256:gt * D + (oq + 1) * 256]
                    ev = nc.vector.tensor_tensor(xsl, xsl, pb[:, 0:256], ALU.add)
                    ytoks.append((pid, sv.inc(ev)))
                for (vw_, tw_, sl2_) in loads:
                    ring.mark_free(sl2_, sv, sv.count)
            sv.wait(nc.tensor)
            ss.wait(nc.tensor)

    if _stop == "full":
        sv.wait(nc.gpsimd)
        ctx.sems["se"].wait(nc.gpsimd)
        sp2 = ctx.sem("wld")
        sp2.inc(nc.gpsimd.dma_start(out=X[:].rearrange("p (n d) -> p n d", n=NTT),
                                    in_=scr.rearrange("(n p) d -> p n d", p=128),
                                    accum_op=ALU.add), 16)
        for eng in (nc.tensor, nc.vector, nc.scalar, nc.sync, nc.gpsimd):
            sp2.wait(eng)



def emit_l2a(ctx):
    nc = ctx.nc
    X, A, B, C, QS, PS, S32, KB, K32, F128 = (ctx.ar[k] for k in
        ("X", "A", "B", "C", "QS", "PS", "S32", "KB", "K32", "F128"))
    V32 = ctx.ar["V32"]
    sv, ss, sm, sw = ctx.sem("ve"), ctx.sem("se"), ctx.sem("mm"), ctx.sem("wld")
    sld, sp = ctx.sem("ldx"), ctx.sem("spill")
    pk, pk2 = ctx.psum["pk"], ctx.psum["pk2"]
    ps_sc = [ctx.psum["ps0"], ctx.psum["ps1"], ctx.psum["ps2"], ctx.psum["ps3"]]
    ptA, ptB = ctx.psum["ptA"], ctx.psum["ptB"]
    eyebf = ctx.ar["eye_bf"]
    ones = ctx.ar["ones_bf"]
    NH, DK, DV, CH = 4, 256, 256, 32
    NCK = T // CH
    DECAY = 1.0 - 1.0 / 3000.0
    EPSF = 0.08 * DECAY

    emit_ln(ctx, X, A)
    emit_transpose(ctx, A, B)
    ctx.sems["ve"].wait(nc.tensor)

    ring = WRing(ctx, nslots=3, cols=256)

    vb, tb, sb_ = ring.load(hbm_slab(ctx, "l2a_wb", 0, 4))
    for sl in range(4):
        pb = pk if sl % 2 == 0 else pk2
        nc.tensor.wait_ge(sw.h, tb)
        if sl >= 2:
            ctx.sems["se"].wait(nc.tensor)
        mm_acc(ctx, pb[0:4, 0:512],
               [(vb(k, 0, 4), B[:, k * T + sl * 512:k * T + (sl + 1) * 512])
                for k in range(8)], sem=sm)
        sm.wait(nc.scalar)
        ss.inc(nc.scalar.activation(QS[0:4, sl * 512:(sl + 1) * 512],
                                    pb[0:4, 0:512], AF.Sigmoid))
    ring.mark_free(sb_, ss, ss.count)
    ss.wait(nc.tensor)
    for tt in range(NTT):
        sm.inc(nc.tensor.transpose(ptA[:, 0:4],
                                   QS[0:4, tt * 128:(tt + 1) * 128], eyebf[0:4, 0:4]))
        sm.wait(nc.vector)
        sv.inc(nc.vector.tensor_copy(V32[:, tt * 4:(tt + 1) * 4], ptA[:, 0:4]))
        sv.wait(nc.tensor)

    def proj_raw(wname):
        evt = []
        for oc in range(4):
            vw, tw, so_ = ring.load(hbm_slab(ctx, wname, oc * 256, (oc + 1) * 256))
            for so in range(2):
                o = oc * 2 + so
                for sl in range(4):
                    pb, pid = (pk, 0) if (len(evt)) % 2 == 0 else (pk2, 1)
                    nc.tensor.wait_ge(sw.h, tw)
                    for q, tk in evt[-2:]:
                        if q == pid:
                            ctx.sems["ve"].wait(nc.tensor, at=tk)
                    mm_acc(ctx, pb[:, 0:512],
                           [(vw(k, so * 128, so * 128 + 128),
                             B[:, k * T + sl * 512:k * T + (sl + 1) * 512])
                            for k in range(8)], sem=sm)
                    sm.wait(nc.vector)
                    ev = nc.vector.tensor_copy(
                        C[:, o * T + sl * 512:o * T + (sl + 1) * 512],
                        pb[:, 0:512])
                    evt.append((pid, sv.inc(ev)))
            ring.mark_free(so_, sv, sv.count)
        sv.wait(nc.tensor)

    def conv_silu(wname, dst):
        K32_ = K32
        tok = sld.inc(nc.sync.dma_start(out=K32_[:, 0:32],
                                        in_=ctx.dram[wname][:]), 16)
        sld.wait(nc.vector, at=tok)
        for ct in range(8):
            sm.wait(nc.vector)
            lastd = None
            for j in range(4):
                lastd = nc.vector.tensor_scalar(
                    QS[:, 2048 + j * 128:2048 + (j + 1) * 128], eyebf[:],
                    K32_[:, ct * 4 + j:ct * 4 + j + 1], None, ALU.mult)
            sv.inc(lastd)
            sv.wait(nc.tensor)
            for sl in range(4):
                pb = pk if sl % 2 == 0 else pk2
                c0 = sl * 512
                ctx.sems["se"].wait(nc.tensor)
                last = None
                order = [3, 0, 1, 2]
                for oi, j in enumerate(order):
                    sh = 3 - j
                    lo = max(0, c0 - sh)
                    outoff = lo - (c0 - sh)
                    ncols = 512 - outoff
                    last = nc.tensor.matmul(
                        pb[:, outoff:512],
                        QS[:, 2048 + j * 128:2048 + (j + 1) * 128],
                        C[:, ct * T + lo:ct * T + lo + ncols],
                        start=(oi == 0), stop=(oi == 3), skip_group_check=True)
                sm.inc(last)
                sm.wait(nc.scalar)
                ss.inc(nc.scalar.activation(dst(ct, sl), pb[:, 0:512], AF.Silu))
            ss.wait(nc.tensor)

    for which, wname, cname in (("q", "l2a_wq", "l2a_cq"),
                                ("k", "l2a_wk", "l2a_ck"),
                                ("v", "l2a_wv", "l2a_cv")):
        proj_raw(wname)
        if which == "v":
            conv_silu(cname, lambda ct, sl: A[:, ct * T + sl * 512:
                                              ct * T + (sl + 1) * 512])
            sv.wait(nc.scalar)
        else:
            scr = ctx.dram["scr_qT" if which == "q" else "scr_kT"]
            conv_silu(cname, lambda ct, sl: A[:, ct * T + sl * 512:
                                              ct * T + (sl + 1) * 512])
            ss.wait(nc.sync)
            sv.wait(nc.sync)
            sp.inc(nc.sync.dma_start(
                out=scr.rearrange("(a p) c -> p a c", p=128),
                in_=A[:].rearrange("p (a c) -> p a c", a=8)), 16)
            sp.wait(nc.tensor)
            sp.wait(nc.scalar)

    sv.wait(nc.tensor)
    gi = 0
    sptok = {}
    for tt in range(NTT):
        for g2 in range(2):
            pt = (ptA, ptB)[gi % 2]
            if gi - 2 in sptok:
                ctx.sems["ve"].wait(nc.tensor, at=sptok[gi - 2][0])
                nc.vector.wait_ge(ctx.sems["spill"].h, sptok[gi - 2][1])
            lastt = None
            for j in range(4):
                ft = g2 * 4 + j
                lastt = nc.tensor.transpose(
                    pt[:, j * 128:(j + 1) * 128],
                    A[:, ft * T + tt * 128:ft * T + (tt + 1) * 128], eyebf[:])
            sm.inc(lastt)
            sm.wait(nc.vector)
            ev = nc.vector.tensor_copy(
                QS[:, (gi % 2) * 512:(gi % 2) * 512 + 512], pt[:, 0:512])
            evt_ = sv.inc(ev)
            sv.wait(nc.sync)
            spt = sp.inc(nc.sync.dma_start(
                out=ctx.dram["scr_vtok"].rearrange("(a p) c -> p a c", p=128)
                [:, tt:tt + 1, g2 * 512:(g2 + 1) * 512],
                in_=QS[:, (gi % 2) * 512:(gi % 2) * 512 + 512]
                .rearrange("p (a c) -> p a c", a=1)), 16)
            sptok[gi] = (evt_, spt)
            gi += 1
    sv.wait(nc.tensor)
    for wname, taps, scr_nm in (("l2a_firs", 5, "scr_fs"),
                                ("l2a_firl", 64, "scr_fl")):
        scr = ctx.dram[scr_nm]
        tokd = sld.inc(nc.sync.dma_start(out=K32[:, 0:8 * taps],
                                         in_=ctx.dram[wname][:]), 16)
        sld.wait(nc.vector, at=tokd)
        for ct in range(8):
            tap_order = [taps - 1] + list(range(taps - 1))
            for wi in range(0, taps, 16):
                wtaps = tap_order[wi:wi + 16]
                sm.wait(nc.vector)
                lastd = None
                for jj, tp in enumerate(wtaps):
                    lastd = nc.vector.tensor_scalar(
                        QS[:, 2048 + jj * 128:2048 + (jj + 1) * 128], eyebf[:],
                        K32[:, ct * taps + tp:ct * taps + tp + 1],
                        None, ALU.mult)
                sv.inc(lastd)
                sv.wait(nc.tensor)
                for sl in range(4):
                    pb = [pk, pk2, ps_sc[0], ps_sc[1]][sl]
                    c0 = sl * 512
                    last = None
                    for jj, tp in enumerate(wtaps):
                        sh = (taps - 1) - tp
                        lo = max(0, c0 - sh)
                        outoff = lo - (c0 - sh)
                        ncols = 512 - outoff
                        if ncols <= 0:
                            continue
                        last = nc.tensor.matmul(
                            pb[:, outoff:512],
                            QS[:, 2048 + jj * 128:2048 + (jj + 1) * 128],
                            A[:, ct * T + lo:ct * T + lo + ncols],
                            start=(wi == 0 and jj == 0),
                            stop=(wi + 16 >= taps and jj == len(wtaps) - 1),
                            skip_group_check=True)
                    if last is not None:
                        sm.inc(last)
            for sl in range(4):
                pb = [pk, pk2, ps_sc[0], ps_sc[1]][sl]
                sm.wait(nc.scalar)
                sp.wait(nc.scalar)
                ss.inc(nc.scalar.activation(QS[:, (sl % 2) * 512:(sl % 2) * 512 + 512],
                                            pb[:, 0:512], AF.Identity))
                ss.wait(nc.tensor)
                lastt = None
                for j in range(4):
                    lastt = nc.tensor.transpose(
                        (ptA, ptB)[sl % 2][:, j * 128:(j + 1) * 128],
                        QS[:, (sl % 2) * 512 + j * 128:(sl % 2) * 512 + (j + 1) * 128],
                        eyebf[:])
                sm.inc(lastt)
                sm.wait(nc.vector)
                sv.inc(nc.vector.tensor_copy(
                    QS[:, 1024 + (sl % 2) * 512:1024 + (sl % 2) * 512 + 512],
                    (ptA, ptB)[sl % 2][:, 0:512]))
                sv.wait(nc.sync)
                sp.inc(nc.sync.dma_start(
                    out=scr.rearrange("(a p) c -> p a c", p=128)
                    [:, sl * 4:(sl + 1) * 4, ct * 128:(ct + 1) * 128],
                    in_=QS[:, 1024 + (sl % 2) * 512:1024 + (sl % 2) * 512 + 512]
                    .rearrange("p (a c) -> p a c", a=4)), 16)
            sp.wait(nc.tensor)

    sv.wait(nc.vector)
    lastz = None
    for h in range(NH):
        lastz = nc.vector.memset(QS[:, 512 * h:512 * (h + 1)], 0.0)
    sv.inc(lastz)
    eye32 = eyebf[0:32, 0:32]
    mskSL = KB[0:32, 3616:3648]
    mskU = KB[0:32, 3648:3680]
    sw.inc(nc.gpsimd.dma_start(out=KB[0:32, 3616:3680],
                               in_=ctx.dram["masks32"][:]), 16)
    sw.wait(nc.tensor)
    TBC = 3072
    tcols = {"A": 0, "B": 32, "A2": 64, "B2": 96, "A4": 128, "B4": 160,
             "A8": 192, "B8": 224, "B16": 256, "C1": 288, "D1": 320,
             "C2": 352, "D2": 384, "C3": 416, "D3": 448, "TT": 480}

    def tb(nm):
        return QS[0:32, TBC + tcols[nm]:TBC + tcols[nm] + 32]

    for h in range(NH):
        if h == 0:
            sp.wait(nc.sync)
        ctx.sems["mm"].wait(nc.sync)
        ctx.sems["ve"].wait(nc.sync)
        sld.inc(nc.sync.dma_start(
            out=A[:, 0:4096].rearrange("p (a c) -> p a c", a=2),
            in_=ctx.dram["scr_qT"].rearrange("(a p) c -> p a c", p=128)
            [:, 2 * h:2 * h + 2, :]), 16)
        ldt = sld.inc(nc.sync.dma_start(
            out=A[:, 4096:8192].rearrange("p (a c) -> p a c", a=2),
            in_=ctx.dram["scr_kT"].rearrange("(a p) c -> p a c", p=128)
            [:, 2 * h:2 * h + 2, :]), 16)
        sld.wait(nc.tensor, at=ldt)
        sv.wait(nc.tensor)
        ctx.sems["mm"].wait(nc.vector)
        gi = 0
        evtok = {}
        for tt in range(NTT):
            pt = (ptA, ptB)[gi % 2]
            if gi - 2 in evtok:
                ctx.sems["ve"].wait(nc.tensor, at=evtok[gi - 2])
            nc.tensor.transpose(
                pt[:, 0:128],
                A[:, 4096 + tt * 128:4096 + (tt + 1) * 128], eyebf[:])
            lastt = nc.tensor.transpose(
                pt[:, 128:256],
                A[:, 4096 + T + tt * 128:4096 + T + (tt + 1) * 128], eyebf[:])
            sm.inc(lastt)
            sm.wait(nc.vector)
            ev = nc.vector.tensor_copy(
                A[:, 8192 + tt * 256:8192 + (tt + 1) * 256], pt[:, 0:256])
            evtok[gi] = sv.inc(ev)
            gi += 1
        import os as _osA
        if _osA.environ.get("L2A_SPILL") == "ktokraw" and h == 0:
            ctx.barrier()
            sp.inc(nc.sync.dma_start(
                out=ctx.dram["scr_od"].rearrange("(a p) c -> p a c", p=128)
                [:, 0:4, :],
                in_=A[:, 8192:12288].rearrange("p (a c) -> p a c", a=4)), 16)
            sp.wait(nc.sync)
            ctx.barrier()
        import os as _osB
        _nonorm = _osB.environ.get("L2A_NONORM")
        sv.wait(nc.tensor)
        gi = 0
        evtok2 = {}
        for tt in range(NTT):
            pt = (ptA, ptB)[gi % 2]
            if gi - 2 in evtok2:
                ctx.sems["ve"].wait(nc.tensor, at=evtok2[gi - 2])
            nc.tensor.transpose(pt[:, 0:128],
                                A[:, tt * 128:(tt + 1) * 128], eyebf[:])
            lastt = nc.tensor.transpose(
                pt[:, 128:256], A[:, T + tt * 128:T + (tt + 1) * 128], eyebf[:])
            sm.inc(lastt)
            sm.wait(nc.vector)
            ev = nc.vector.tensor_copy(PS[:, tt * 256:(tt + 1) * 256],
                                       pt[:, 0:256])
            evtok2[gi] = sv.inc(ev)
            gi += 1
        sv.wait(nc.scalar)
        lastq1 = lastq2 = None
        for tt in range(NTT):
            lastq1 = nc.scalar.activation(QS[:, 2048 + (tt % 4) * 256:
                                             2048 + (tt % 4) * 256 + 256],
                                          PS[:, tt * 256:(tt + 1) * 256],
                                          AF.Square, accum_out=F128[:, tt:tt + 1])
        for tt in range(NTT):
            lastq2 = nc.scalar.activation(QS[:, 2048 + (tt % 4) * 256:
                                             2048 + (tt % 4) * 256 + 256],
                                          A[:, 8192 + tt * 256:8192 + (tt + 1) * 256],
                                          AF.Square, accum_out=F128[:, 32 + tt:33 + tt])
        ss.inc(lastq2)
        ss.wait(nc.scalar)
        ss.wait(nc.vector)
        sssync(ctx, nc.scalar.activation(F128[:, 16:32], F128[:, 0:16], AF.Ln,
                                         bias=V32[:, 241:242]))
        sssync(ctx, nc.scalar.activation(F128[:, 16:32], F128[:, 16:32], AF.Exp,
                                         scale=-0.5))
        sssync(ctx, nc.scalar.activation(F128[:, 48:64], F128[:, 32:48], AF.Ln,
                                         bias=V32[:, 241:242]))
        sssync(ctx, nc.scalar.activation(F128[:, 48:64], F128[:, 48:64], AF.Exp,
                                         scale=-0.5))
        ss.wait(nc.vector)
        if _osB.environ.get("L2A_RK1"):
            vv(ctx, nc.vector.memset(F128[:, 48:64], 1.0))
            vv(ctx, nc.vector.memset(F128[:, 16:32], 1.0))
        lastkn = None
        for tt in range(NTT):
            lastkn = nc.vector.tensor_scalar(
                A[:, 8192 + tt * 256:8192 + (tt + 1) * 256],
                A[:, 8192 + tt * 256:8192 + (tt + 1) * 256],
                F128[:, 48 + tt:49 + tt], None, ALU.mult)
        sv.inc(lastkn)
        import os as _os9
        if _os9.environ.get("L2A_SPILL") == "ktok" and h == 0:
            sv.wait(nc.sync)
            spk = sp.inc(nc.sync.dma_start(
                out=ctx.dram["scr_od"].rearrange("(a p) c -> p a c", p=128)
                [:, 0:4, :],
                in_=A[:, 8192:12288].rearrange("p (a c) -> p a c", a=4)), 16)
            sp.wait(nc.sync)
        for rid, fcol, base in ((0, 16, 0), (1, 48, 4096)):
            vv(ctx, nc.vector.tensor_copy(QS[0:128, 2048:2064],
                                          F128[:, fcol:fcol + 16]))
            sv.wait(nc.tensor)
            sm.inc(nc.tensor.transpose(ptA[0:16, 0:128], QS[:, 2048:2064], eyebf[:]))
            sm.wait(nc.vector)
            sv.inc(nc.vector.tensor_copy(QS[0:16, 2064:2192], ptA[0:16, 0:128]))
            sv.wait(nc.sync)
            spr = sp.inc(nc.sync.dma_start(out=ctx.dram["scr_row"][:],
                                           in_=QS[0:16, 2064:2192]), 16)
            nc.sync.wait_ge(sp.h, spr)
            sp.inc(nc.sync.dma_start(
                out=QS[0:1, 2192:4240] if False else QS[0:1, 2048:4096],
                in_=ctx.dram["scr_row"].rearrange("(o a) c -> o (a c)", o=1)), 16)
            sp.wait(nc.tensor)
            sp.wait(nc.vector)
            for sl in range(4):
                mm_acc(ctx, pk[:, 0:512],
                       [(ones[0:1, 0:128],
                         QS[0:1, 2048 + sl * 512:2048 + (sl + 1) * 512])], sem=sm)
                sm.wait(nc.vector)
                lastn2 = None
                for dt_ in range(2):
                    lastn2 = nc.vector.tensor_tensor(
                        A[:, base + dt_ * T + sl * 512:base + dt_ * T + (sl + 1) * 512],
                        A[:, base + dt_ * T + sl * 512:base + dt_ * T + (sl + 1) * 512],
                        pk[:, 0:512], ALU.mult)
                sv.inc(lastn2)
                sv.wait(nc.tensor)

        Spsum = ps_sc[h]
        last_osp = 0
        import os as _os7
        _nck = int(_os7.environ.get("L2A_NCK", str(NCK)))
        _c0 = int(_os7.environ.get("L2A_C0", "0"))
        for c in range(_c0, _nck):
            tt, pb32 = c // 4, (c % 4) * 32
            qoff = c * 32
            if c % 4 == 0:
                vbase = 1312 + ((c // 4) % 2) * 256
                sv.wait(nc.sync)
                vt_tok = sld.inc(nc.sync.dma_start(
                    out=PS[:, vbase:vbase + 256].rearrange("p (a c) -> p a c", a=1),
                    in_=ctx.dram["scr_vtok"].rearrange("(a p) c -> p a c", p=128)
                    [:, tt:tt + 1, 256 * h:256 * (h + 1)]), 16)
                sld.wait(nc.vector, at=vt_tok)
            ctx.sems["mm"].wait(nc.vector)
            nc.vector.tensor_copy(PS[0:32, 0:256],
                                  A[pb32:pb32 + 32,
                                    8192 + tt * 256:8192 + (tt + 1) * 256])
            vv(ctx, nc.vector.tensor_copy(
                F128[0:32, 121:122],
                V32[pb32:pb32 + 32, tt * 4 + h:tt * 4 + h + 1]))
            nc.vector.tensor_scalar(PS[0:32, 256:512], PS[0:32, 0:256],
                                    F128[0:32, 121:122], None, ALU.mult)
            nc.vector.tensor_scalar(
                PS[0:32, 512:768],
                PS[pb32:pb32 + 32,
                   1312 + ((c // 4) % 2) * 256:1568 + ((c // 4) % 2) * 256],
                F128[0:32, 121:122], None, ALU.mult)
            lastcp = nc.vector.tensor_scalar(
                F128[0:32, 120:121], F128[0:32, 121:122], -1.0, None, ALU.mult)
            sv.inc(lastcp)
            sv.wait(nc.tensor)
            import os as _os5
            if _os5.environ.get("L2A_DBG_CHUNK") == f"{h}:{c}":
                print(f"[DBG] emitting chunk dump at h={h} c={c}")
                ctx.barrier()
                sdb = ctx.sem("st")
                sdb.inc(nc.sync.dma_start(out=ctx.dram["l2a_dbgc"][0:32, 0:1312],
                                          in_=PS[0:32, 0:1312]), 16)
                sdb.inc(nc.sync.dma_start(out=ctx.dram["l2a_dbgc"][0:32, 1312:1824],
                                          in_=QS[0:32, 512 * h:512 * h + 512]), 16)
                sdb.wait(nc.sync)
                ctx.barrier()
            kTc = [A[:, 4096 + j * T + qoff:4096 + j * T + qoff + 32]
                   for j in range(2)]
            qTc = [A[:, j * T + qoff:j * T + qoff + 32] for j in range(2)]
            mm_acc(ctx, pk[0:32, 0:32], [(kTc[0], kTc[0]), (kTc[1], kTc[1])],
                   sem=sm)
            sm.wait(nc.vector)
            vv(ctx, nc.vector.tensor_scalar(PS[0:32, 1024:1056], pk[0:32, 0:32],
                                            F128[0:32, 120:121], None, ALU.mult))
            sv.inc(nc.vector.tensor_tensor(tb("A"), PS[0:32, 1024:1056],
                                           mskSL, ALU.mult))
            sv.wait(nc.tensor)
            sm.inc(nc.tensor.transpose(ptA[0:32, 0:32], tb("A"), eye32))
            sm.wait(nc.vector)
            sv.inc(nc.vector.tensor_copy(tb("B"), ptA[0:32, 0:32]))
            sv.wait(nc.tensor)

            def mm32(dst, lhs_nm, rhs_nm, adds=()):
                pairs = [(tb(lhs_nm), tb(rhs_nm))]
                for en in adds:
                    pairs.append((eye32, eye32 if en == "I" else tb(en)))
                mm_acc(ctx, pk[0:32, 32:64], pairs, sem=sm)
                sm.wait(nc.vector)
                sv.inc(nc.vector.tensor_copy(tb(dst), pk[0:32, 32:64]))
                sv.wait(nc.tensor)

            mm32("A2", "B", "A")
            mm32("B2", "A", "B")
            mm32("A4", "B2", "A2")
            mm32("B4", "A2", "B2")
            mm32("A8", "B4", "A4")
            mm32("B8", "A4", "B4")
            mm32("B16", "A8", "B8")
            mm32("C1", "A", "B2", ["B", "B2", "I"])
            mm32("D1", "B2", "A", ["A", "A2", "I"])
            mm32("C2", "D1", "B4", ["C1"])
            mm32("D2", "B4", "D1", ["D1"])
            mm32("C3", "D2", "B8", ["C2"])
            mm32("D3", "B8", "D2", ["D2"])
            mm32("TT", "D3", "B16", ["C3"])

            mm_acc(ctx, pk[0:32, 64:96], [(kTc[0], qTc[0]), (kTc[1], qTc[1])],
                   sem=sm)
            sm.wait(nc.vector)
            sv.inc(nc.vector.tensor_tensor(PS[0:32, 3072:3104], pk[0:32, 64:96],
                                           mskU, ALU.mult))
            sv.wait(nc.tensor)
            lastw = None
            for dh in range(2):
                lastw = nc.tensor.matmul(
                    pk2[:, 256 + 32 * dh:256 + 32 * (dh + 1)],
                    PS[0:32, 256 + 128 * dh:256 + 128 * (dh + 1)],
                    tb("TT"), start=True, stop=True)
            sm.inc(lastw)
            sm.wait(nc.scalar)
            nc.scalar.activation(PS[:, 3584:3616], pk2[:, 256:288],
                                 AF.Identity, scale=-1.0)
            ss.inc(nc.scalar.activation(PS[:, 3616:3648], pk2[:, 288:320],
                                        AF.Identity, scale=-1.0))
            ss.wait(nc.tensor)
            import os as _os6
            _nos = _os6.environ.get("L2A_NOS")
            pairs = [(tb("TT"), PS[0:32, 512:768])]
            if not _nos:
                pairs += [(PS[:, 3584:3616], QS[:, 512 * h:512 * h + 256]),
                          (PS[:, 3616:3648], QS[:, 512 * h + 256:512 * h + 512])]
            mm_acc(ctx, pk[0:32, 128:384], pairs, sem=sm)
            sm.wait(nc.vector)
            sv.inc(nc.vector.tensor_copy(PS[0:32, 3328:3584], pk[0:32, 128:384]))
            sv.wait(nc.tensor)
            pairs = [(PS[0:32, 3072:3104], PS[0:32, 3328:3584])]
            if not _nos:
                pairs = [(qTc[0], QS[:, 512 * h:512 * h + 256]),
                         (qTc[1], QS[:, 512 * h + 256:512 * h + 512])] + pairs
            mm_acc(ctx, pk2[0:32, 0:256], pairs, sem=sm)
            lastS = None
            for dh in range(2):
                lastS = nc.tensor.matmul(
                    Spsum[:, 256 * dh:256 * (dh + 1)],
                    PS[0:32, 128 * dh:128 * (dh + 1)],
                    PS[0:32, 3328:3584], start=True, stop=True,
                    skip_group_check=True)
            sm.inc(lastS)
            sm.wait(nc.vector)
            import os as _os8
            nc.vector.wait_ge(sp.h, last_osp)
            _src_sp = {"u": PS[0:32, 3328:3584], "k": PS[0:32, 0:256],
                       "kb": PS[0:32, 256:512], "vb": PS[0:32, 512:768],
                       "att": None}.get(_os8.environ.get("L2A_SPILL", "o"))
            if _src_sp is None and _os8.environ.get("L2A_SPILL") == "att":
                nc.vector.tensor_copy(PS[0:32, 1056:1312][:, 0:32],
                                      PS[0:32, 3072:3104])
                nc.vector.tensor_copy(PS[0:32, 1088:1312],
                                      pk2[0:32, 32:256])
            elif _src_sp is not None and _os8.environ.get("L2A_SPILL") != "o":
                nc.vector.tensor_copy(PS[0:32, 1056:1312], _src_sp)
            else:
                nc.vector.tensor_copy(PS[0:32, 1056:1312], pk2[0:32, 0:256])
            if c == 0:
                nc.vector.tensor_copy(K32[:, 0:512], Spsum[:, 0:512])
            else:
                nc.vector.tensor_tensor(K32[:, 0:512], K32[:, 0:512],
                                        Spsum[:, 0:512], ALU.add)
            vspacer(ctx)
            lasts = nc.vector.tensor_copy(QS[:, 512 * h:512 * (h + 1)],
                                          K32[:, 0:512])
            sv.inc(lasts)
            sv.wait(nc.sync)
            last_osp = sp.inc(nc.sync.dma_start(
                out=ctx.dram["scr_od"].rearrange("(a p) c -> p a c", p=128)
                [pb32:pb32 + 32, tt:tt + 1, 256 * h:256 * (h + 1)],
                in_=PS[0:32, 1056:1312].rearrange("p (a c) -> p a c", a=1)), 16)
            sv.wait(nc.tensor)

    sv.wait(nc.tensor)
    ss.wait(nc.tensor)
    ctx.sems["ve"].wait(nc.gpsimd)
    ctx.sems["mm"].wait(nc.gpsimd)
    sw.inc(nc.gpsimd.dma_start(
        out=KB[:, 3584:3616].rearrange("p (a c) -> p a c", a=8),
        in_=hbm_slab(ctx, "l2a_w2t", 0, 4)), 16)
    sw.inc(nc.gpsimd.dma_start(
        out=KB[0:16, 2560:3584].rearrange("p (a c) -> p a c", a=1),
        in_=ctx.dram["l2a_w1st"].rearrange("p (a c) -> p a c", a=1)), 16)
    sldc = ctx.sem("ldx")
    sldc.inc(nc.sync.dma_start(out=K32[:, 960:968], in_=ctx.dram["l2a_b1pc"][:]), 16)
    sldc.inc(nc.sync.dma_start(out=F128[0:4, 64:68], in_=ctx.dram["l2a_ltemp4"][:]), 16)
    sldc.inc(nc.sync.dma_start(out=F128[0:4, 68:72], in_=ctx.dram["l2a_cb4"][:]), 16)
    sldc.inc(nc.sync.dma_start(out=F128[0:4, 72:73], in_=ctx.dram["l2a_b24"][:]), 16)
    sldc.inc(nc.sync.dma_start(out=F128[0:4, 73:74], in_=ctx.dram["maskD41"][:]), 16)
    sw.inc(nc.gpsimd.dma_start(out=QS[:, 3072:4096], in_=ctx.dram["l2a_onw"][:]), 16)
    sldc.wait(nc.vector)
    sldc.wait(nc.scalar)
    sw.wait(nc.vector)
    sw.wait(nc.scalar)
    sw.wait(nc.tensor)

    sv.wait(nc.vector)
    sv.inc(nc.vector.memset(K32[:, 64:192], 0.0))
    for tt in range(NTT):
        sp.wait(nc.sync)
        ldt = None
        for si, nm in enumerate(("scr_fs", "scr_fl", "scr_od", "scr_vtok")):
            ldt = sld.inc(nc.sync.dma_start(
                out=PS[:, si * 1024:(si + 1) * 1024].rearrange(
                    "p (a c) -> p a c", a=1),
                in_=ctx.dram[nm].rearrange("(a p) c -> p a c", p=128)
                [:, tt:tt + 1, :]), 16)
        sld.wait(nc.vector, at=ldt)
        sld.wait(nc.scalar, at=ldt)
        for si in range(4):
            s3 = PS[:, si * 1024:(si + 1) * 1024].rearrange(
                "p (h d) -> p h d", h=4)
            vv(ctx, nc.vector.tensor_reduce(K32[:, si * 4:si * 4 + 4], s3,
                                            AX.X, ALU.add))
            vv(ctx, nc.vector.tensor_reduce(K32[:, 16 + si * 4:16 + si * 4 + 4],
                                            s3, AX.X, ALU.add,
                                            apply_absolute_value=True))
            sssync(ctx, nc.scalar.activation(QS[:, 0:1024],
                                             PS[:, si * 1024:(si + 1) * 1024],
                                             AF.Square))
            ss.wait(nc.vector)
            vv(ctx, nc.vector.tensor_reduce(
                K32[:, 32 + si * 4:32 + si * 4 + 4],
                QS[:, 0:1024].rearrange("p (h d) -> p h d", h=4), AX.X, ALU.add))
        for si in range(4):
            for h in range(NH):
                s_ = K32[:, si * 4 + h:si * 4 + h + 1]
                a_ = K32[:, 16 + si * 4 + h:16 + si * 4 + h + 1]
                q_ = K32[:, 32 + si * 4 + h:32 + si * 4 + h + 1]
                base = 64 + h * 32 + si * 4
                nc.vector.tensor_scalar(K32[:, base:base + 1], s_, 1.0 / 256,
                                        None, ALU.mult)
                nc.vector.tensor_scalar(K32[:, base + 2:base + 3], a_, 1.0 / 256,
                                        None, ALU.mult)
        sv.inc(vspacer(ctx))
        sv.wait(nc.vector)
        for si in range(4):
            for h in range(NH):
                q_ = K32[:, 32 + si * 4 + h:32 + si * 4 + h + 1]
                base = 64 + h * 32 + si * 4
                nc.vector.tensor_tensor(K32[:, base + 1:base + 2],
                                        K32[:, base:base + 1],
                                        K32[:, base:base + 1], ALU.mult)
        sv.inc(vspacer(ctx))
        sv.wait(nc.vector)
        for si in range(4):
            for h in range(NH):
                q_ = K32[:, 32 + si * 4 + h:32 + si * 4 + h + 1]
                base = 64 + h * 32 + si * 4
                nc.vector.scalar_tensor_tensor(
                    K32[:, base + 1:base + 2], q_, 1.0 / 256,
                    K32[:, base + 1:base + 2], ALU.mult, ALU.subtract)
        sv.inc(vspacer(ctx))
        sv.wait(nc.scalar)
        for si in range(4):
            for h in range(NH):
                q_ = K32[:, 32 + si * 4 + h:32 + si * 4 + h + 1]
                base = 64 + h * 32 + si * 4
                nc.scalar.activation(K32[:, base + 3:base + 4], q_, AF.Sqrt)
        ss.inc(sspacer(ctx))
        ss.wait(nc.vector)
        sv.inc(nc.vector.tensor_copy(QS[0:128, 1024:1152], K32[:, 64:192]))
        sv.wait(nc.tensor)
        sm.inc(nc.tensor.transpose(ptA[:, 0:128], QS[:, 1024:1152], eyebf[:]))
        sm.wait(nc.vector)
        lastst = None
        for h in range(NH):
            lastst = nc.vector.tensor_copy(
                C[0:16, h * 2048 + tt * 128:h * 2048 + (tt + 1) * 128],
                ptA[32 * h:32 * h + 16, 0:128])
        sv.inc(lastst)
        sv.wait(nc.tensor)
        sv.wait(nc.sync)

    for h in range(NH):
        sssync(ctx, nc.scalar.activation(F128[0:4, 74:75], F128[0:4, 64 + h:65 + h],
                                         AF.Exp))
        ss.wait(nc.vector)
        vv(ctx, nc.vector.reciprocal(F128[0:4, 75:76], F128[0:4, 74:75]))
        vv(ctx, nc.vector.scalar_tensor_tensor(
            F128[0:4, 76:77], F128[0:4, 73:74], F128[0:4, 68 + h:69 + h],
            F128[0:4, 72:73], ALU.mult, ALU.add))
        sv.inc(nc.vector.tensor_tensor(F128[0:4, 77:78], F128[0:4, 76:77],
                                       F128[0:4, 75:76], ALU.mult))
        sv.wait(nc.scalar)
        sv.wait(nc.tensor)
        htoks = []
        for o in range(8):
            vw, tw, sw_ = ring.load(hbm_slab(ctx, "l2a_w1hs", o * 128,
                                             (o + 1) * 128))
            for sl in range(4):
                pb, pid = (pk, 0) if len(htoks) % 2 == 0 else (pk2, 1)
                nc.tensor.wait_ge(sw.h, tw)
                for q, tk in htoks[-2:]:
                    if q == pid:
                        ctx.sems["se"].wait(nc.tensor, at=tk)
                pairs = [(vw(k, 0, 128),
                          B[:, k * T + sl * 512:k * T + (sl + 1) * 512])
                         for k in range(8)]
                pairs.append((KB[0:16, 2560 + o * 128:2560 + (o + 1) * 128],
                              C[0:16, h * 2048 + sl * 512:h * 2048 + (sl + 1) * 512]))
                mm_acc(ctx, pb[:, 0:512], pairs, sem=sm)
                sm.wait(nc.scalar)
                ev = nc.scalar.activation(
                    A[:, o * T + sl * 512:o * T + (sl + 1) * 512], pb[:, 0:512],
                    AF.Gelu_apprx_tanh, bias=K32[:, 960 + o:961 + o])
                htoks.append((pid, ss.inc(ev)))
            ring.mark_free(sw_, ss, ss.count)
        ss.wait(nc.tensor)
        for sl in range(4):
            pb = pk if sl % 2 == 0 else pk2
            ctx.sems["se"].wait(nc.tensor)
            mm_acc(ctx, pb[0:4, 0:512],
                   [(KB[:, 3584 + k * 4:3584 + (k + 1) * 4],
                     A[:, k * T + sl * 512:k * T + (sl + 1) * 512])
                    for k in range(8)], sem=sm)
            sm.wait(nc.scalar)
            ss.inc(nc.scalar.activation(QS[0:4, sl * 512:(sl + 1) * 512],
                                        pb[0:4, 0:512], AF.Identity,
                                        scale=F128[0:4, 75:76],
                                        bias=F128[0:4, 77:78]))
        ss.wait(nc.tensor)
        for tt in range(NTT):
            sm.inc(nc.tensor.transpose(ptB[:, 0:4],
                                       QS[0:4, tt * 128:(tt + 1) * 128],
                                       eyebf[0:4, 0:4]))
            sm.wait(nc.vector)
            vv(ctx, nc.vector.tensor_copy(F128[:, 80:84], ptB[:, 0:4]))
            vv(ctx, nc.vector.tensor_reduce(F128[:, 84:85], F128[:, 80:84],
                                            AX.X, ALU.max, negate=True))
            sv.wait(nc.scalar)
            ss.inc(nc.scalar.activation(F128[:, 85:89], F128[:, 80:84], AF.Exp,
                                        bias=F128[:, 84:85],
                                        accum_out=F128[:, 89:90]))
            ss.wait(nc.vector)
            vv(ctx, nc.vector.reciprocal(F128[:, 90:91], F128[:, 89:90]))
            vv(ctx, nc.vector.tensor_scalar(F128[:, 91:95], F128[:, 85:89],
                                            F128[:, 90:91], None, ALU.mult))
            base = 256 + tt * 16 + h * 4
            lastw = nc.vector.tensor_scalar(K32[:, base:base + 4],
                                            F128[:, 91:95], 1.0 - 4.0 * EPSF,
                                            EPSF, ALU.mult, ALU.add)
            sv.inc(lastw)
            sv.wait(nc.tensor)

    for tt in range(NTT):
        sp.wait(nc.sync)
        ldt = None
        for si, nm in enumerate(("scr_fs", "scr_fl", "scr_od", "scr_vtok")):
            ldt = sld.inc(nc.sync.dma_start(
                out=PS[:, si * 1024:(si + 1) * 1024].rearrange(
                    "p (a c) -> p a c", a=1),
                in_=ctx.dram[nm].rearrange("(a p) c -> p a c", p=128)
                [:, tt:tt + 1, :]), 16)
        sld.wait(nc.vector, at=ldt)
        for h in range(NH):
            wb_ = 256 + tt * 16 + h * 4
            acc = QS[:, 2048 + h * 256:2048 + (h + 1) * 256]
            nc.vector.tensor_scalar(acc, PS[:, h * 256:(h + 1) * 256],
                                    K32[:, wb_:wb_ + 1], None, ALU.mult)
            nc.vector.scalar_tensor_tensor(
                acc, PS[:, 1024 + h * 256:1024 + (h + 1) * 256],
                K32[:, wb_ + 1:wb_ + 2], acc, ALU.mult, ALU.add)
            nc.vector.scalar_tensor_tensor(
                acc, PS[:, 2048 + h * 256:2048 + (h + 1) * 256],
                K32[:, wb_ + 2:wb_ + 3], acc, ALU.mult, ALU.add)
            lastm = nc.vector.scalar_tensor_tensor(
                acc, PS[:, 3072 + h * 256:3072 + (h + 1) * 256],
                K32[:, wb_ + 3:wb_ + 4], acc, ALU.mult, ALU.add)
            sv.inc(lastm)
        sv.wait(nc.scalar)
        sssync(ctx, nc.scalar.activation(QS[:, 0:1024], QS[:, 2048:3072],
                                         AF.Square))
        ss.wait(nc.vector)
        vv(ctx, nc.vector.tensor_reduce(F128[:, 96:100],
                                        QS[:, 0:1024].rearrange(
                                            "p (h d) -> p h d", h=4),
                                        AX.X, ALU.add))
        vv(ctx, nc.vector.tensor_scalar(F128[:, 100:104], F128[:, 96:100],
                                        1.0 / 256, None, ALU.mult))
        sv.wait(nc.scalar)
        sssync(ctx, nc.scalar.activation(F128[:, 104:108], F128[:, 100:104],
                                         AF.Ln, bias=V32[:, 240:241]))
        sssync(ctx, nc.scalar.activation(F128[:, 108:112], F128[:, 104:108],
                                         AF.Exp, scale=-0.5))
        ss.wait(nc.vector)
        for h in range(NH):
            lastn = nc.vector.scalar_tensor_tensor(
                QS[:, 2048 + h * 256:2048 + (h + 1) * 256],
                QS[:, 2048 + h * 256:2048 + (h + 1) * 256],
                F128[:, 108 + h:109 + h],
                QS[:, 3072 + h * 256:3072 + (h + 1) * 256], ALU.mult, ALU.mult)
            sv.inc(lastn)
        sv.wait(nc.tensor)
        for g2 in range(2):
            pt = (ptA, ptB)[g2]
            lastt = None
            for j in range(4):
                lastt = nc.tensor.transpose(
                    pt[:, j * 128:(j + 1) * 128],
                    QS[:, 2048 + g2 * 512 + j * 128:2048 + g2 * 512 + (j + 1) * 128],
                    eyebf[:])
            sm.inc(lastt)
            sm.wait(nc.vector)
            lastc = None
            for j in range(4):
                ft = g2 * 4 + j
                lastc = nc.vector.tensor_copy(
                    C[:, ft * T + tt * 128:ft * T + (tt + 1) * 128],
                    pt[:, j * 128:(j + 1) * 128])
            sv.inc(lastc)
        sv.wait(nc.sync)

    evt2 = []
    it = 0
    for qtr in range(4):
        vw, tw, sw_ = ring.load(hbm_slab(ctx, "l2a_wo", qtr * 256, (qtr + 1) * 256))
        for t in range(NTT):
            pb, pid = (pk2, 1) if it % 2 == 0 else (pk, 0)
            it += 1
            nc.tensor.wait_ge(sw.h, tw)
            if it == 1:
                sv.wait(nc.tensor)
                ctx.sems["se"].wait(nc.tensor)
            for q, tk in evt2[-2:]:
                if q == pid:
                    ctx.sems["ve"].wait(nc.tensor, at=tk)
            mm_acc(ctx, pb[:, 0:256],
                   [(C[:, k * T + t * 128:k * T + (t + 1) * 128], vw(k, 0, 256))
                    for k in range(8)], sem=sm)
            sm.wait(nc.vector)
            xsl = X[:, t * D + qtr * 256:t * D + (qtr + 1) * 256]
            ev = nc.vector.tensor_tensor(xsl, xsl, pb[:, 0:256], ALU.add)
            evt2.append((pid, sv.inc(ev)))
        ring.mark_free(sw_, sv, sv.count)
    sv.wait(nc.tensor)



BLOCK_WEIGHTS = {
    "l1a": {"wq": "l1_wq", "wk": "l1_wk", "wv": "l1_wv", "wo": "l1_wo"},
}


def declare_common(ctx, consts=True):
    nc = ctx.nc
    ctx.sbuf("X", [128, NTT * D], F32)
    ctx.sbuf("A", [128, NTT * D], BF16)
    ctx.sbuf("B", [128, NFT * T], BF16)
    ctx.sbuf("C", [128, NFT * T], BF16)
    ctx.sbuf("QS", [128, 4096], BF16)
    ctx.sbuf("PS", [128, 4224], BF16)
    ctx.sbuf("W", [128, 6144], BF16)
    ctx.sbuf("S32", [128, 256], F32)
    ctx.sbuf("V32", [128, 256], F32)
    for nm in ("lnsum", "lnssq", "lnscl", "lnbia", "lntmpA", "lntmpB"):
        ctx.sbuf(nm, [128, 16], F32)
    ctx.sbuf("JNK", [128, 4], F32)
    ctx.sbuf("F128", [128, 128], F32)
    ctx.sbuf("K32", [128, 1024], F32)
    ctx.sbuf("KB", [128, 4608], BF16)
    ctx.sbuf("eye_bf", [128, 128], BF16)
    ctx.sbuf("ones_bf", [1, 512], BF16)
    ctx.sbuf("maskA", [128, 128], F32)
    ctx.sbuf("maskB", [128, 128], F32)
    ctx.sbuf("maskWA", [128, 128], F32)
    ctx.sbuf("maskWB", [128, 128], F32)
    ctx.psum_t("pk", [128, 512], F32)
    ctx.psum_t("pk2", [128, 512], F32)
    ctx.psum_t("ps0", [128, 512], F32)
    ctx.psum_t("ps1", [128, 512], F32)
    ctx.psum_t("ps2", [128, 512], F32)
    ctx.psum_t("ps3", [128, 512], F32)
    ctx.psum_t("ptA", [128, 1024], BF16)
    ctx.psum_t("ptB", [128, 1024], BF16)


def emit_prologue(ctx):
    nc = ctx.nc
    s = ctx.sem("ldx")
    s.inc(nc.sync.dma_start(
        out=ctx.ar["X"][:].rearrange("p (n d) -> p n d", n=NTT),
        in_=ctx.dram["x"].rearrange("(n p) d -> p n d", p=128)), 16)
    for nm in ("maskA", "maskB", "maskWA", "maskWB"):
        s.inc(nc.sync.dma_start(out=ctx.ar[nm][:], in_=ctx.dram[nm][:]), 16)

    sg = ctx.sem("wld")
    sg.inc(nc.gpsimd.dma_start(out=ctx.ar["eye_bf"][:], in_=ctx.dram["eye"][:]), 16)
    sg.inc(nc.gpsimd.dma_start(out=ctx.ar["ones_bf"][:],
                               in_=ctx.dram["ones_row"][:, 0:512]), 16)
    nc.vector.memset(ctx.ar["V32"][:, 240:241], 1e-5)
    nc.vector.memset(ctx.ar["V32"][:, 241:242], 1e-6)
    nc.vector.memset(ctx.ar["JNK"][:], 0.0)
    for eng in (nc.tensor, nc.vector, nc.scalar, nc.gpsimd, nc.sync):
        s.wait(eng)
        sg.wait(eng)


def emit_epilogue(ctx):
    nc = ctx.nc
    st = ctx.sem("st")
    sv = ctx.sem("ve")
    ctx.barrier()
    st.inc(nc.sync.dma_start(
        out=ctx.dram["y"].rearrange("(n p) d -> p n d", p=128),
        in_=ctx.ar["X"][:].rearrange("p (n d) -> p n d", n=NTT)), 16)
    st.wait(nc.sync)


def build_program(block_list, dram_specs):
    nc = bass.Bass("TRN2", target_bir_lowering=False,
                   detect_race_conditions=False)
    ctx = Ctx(nc)
    for name, shape, dt, kind in dram_specs:
        ctx.dram[name] = nc.dram_tensor(name, list(shape), dt, kind=kind).ap()
    if "l1f" in block_list:
        ctx.dram["scr_h"] = nc.dram_tensor("scr_h", [4096, T], BF16).ap()
    if "l0f" in block_list:
        ctx.dram["scr_x"] = nc.dram_tensor("scr_x", [T, D], F32).ap()
    if "l2a" in block_list:
        for nm in ("scr_qT", "scr_kT"):
            ctx.dram[nm] = nc.dram_tensor(nm, [1024, T], BF16).ap()
        for nm in ("scr_vtok", "scr_fs", "scr_fl", "scr_od"):
            ctx.dram[nm] = nc.dram_tensor(nm, [T, D], BF16).ap()
        ctx.dram["scr_row"] = nc.dram_tensor("scr_row", [16, 128], BF16).ap()
        import os as _os4
        if _os4.environ.get("L2A_DBG_CHUNK"):
            ctx.dram["l2a_dbgc"] = nc.dram_tensor("l2a_dbgc", [32, 1824], BF16,
                                                  kind="ExternalOutput").ap()
            ctx.dram["l2a_dbgc2"] = nc.dram_tensor("l2a_dbgc2", [128, 512], BF16,
                                                   kind="ExternalOutput").ap()
    import os as _os
    if "l3f" in block_list and _os.environ.get("L3F_DEBUG"):
        ctx.dram["l3f_dbg"] = nc.dram_tensor("l3f_dbg", [128, 224], F32,
                                             kind="ExternalOutput").ap()
    if "l3f" in block_list and _os.environ.get("L3F_DEBUG") == "3":
        ctx.dram["l3f_dbg3"] = nc.dram_tensor("l3f_dbg3", [128, 1280], F32,
                                              kind="ExternalOutput").ap()
    if "l3f" in block_list and _os.environ.get("L3F_DEBUG") == "2":
        ctx.dram["l3f_dbg2"] = nc.dram_tensor("l3f_dbg2", [128, 2048], BF16,
                                              kind="ExternalOutput").ap()
    declare_common(ctx)
    emit_prologue(ctx)
    for b in block_list:
        _scope = nc.named_scope(b)
        _scope.__enter__()
        import os as _os
        _sa = _os.environ.get("MHA_STOP") or None
        if b == "l0a":
            emit_mha(ctx, {"wq": "l0a_wq", "wk": "l0a_wk", "wv": "l0a_wv",
                           "wo": "l0a_wo", "causal": True, "scale": 0.125})
        elif b == "l1a":
            emit_mha(ctx, {"wq": "l1_wq", "wk": "l1_wk", "wv": "l1_wv",
                           "wo": "l1_wo", "causal": True, "scale": 0.125,
                           "stop_after": _sa})
        elif b == "l3a":
            emit_mha(ctx, {"wq": "l3_wq", "wk": "l3_wk", "wv": "l3_wv",
                           "wo": "l3_wo", "causal": False,
                           "scale": 0.125, "bq": "l3_bq", "bk": "l3_bk",
                           "bv": "l3_bv", "bo": "l3_bo"})
        elif b == "l2f":
            emit_l2f(ctx)
        elif b == "l1f":
            emit_l1f(ctx)
        elif b == "l3f":
            emit_l3f(ctx)
        elif b == "l0f":
            emit_l0f(ctx)
        elif b == "l2a":
            emit_l2a(ctx)
        else:
            raise NotImplementedError(b)
        ctx.barrier()
        _scope.__exit__(None, None, None)
    emit_epilogue(ctx)
    return nc, ctx



def prep_weights(inp):
    d = {}
    wqkv = np.asarray(inp["L0_attn_w_qkv"])
    _q0 = wqkv.reshape(1024, 16, 3, 64).transpose(0, 2, 1, 3).reshape(1024, 3072)
    d["l0a_wq"] = _q0[:, 0:1024]
    d["l0a_wk"] = _q0[:, 1024:2048]
    d["l0a_wv"] = _q0[:, 2048:3072]
    d["l0a_wo"] = np.ascontiguousarray(
        np.asarray(inp["L0_attn_w_out"]).reshape(1024, 1024))
    d["l1_wq"] = _t(inp["L1_attn_q_heads"])
    d["l1_wk"] = _t(inp["L1_attn_k_heads"])
    d["l1_wv"] = _t(inp["L1_attn_v_heads"])
    d["l1_wo"] = _t(inp["L1_attn_output_proj"])
    wq0 = np.asarray(inp["L0_ffn_wqkv"], dtype=np.float32)
    d["l0f_wqkv"] = np.ascontiguousarray(wq0)
    d["l0f_wo"] = np.ascontiguousarray(np.asarray(inp["L0_ffn_wo"], np.float32))
    w1 = np.asarray(inp["L0_ffn_w1"], np.float32)
    w1a = np.zeros((2, 1024, 2816), np.float32)
    w1g = np.zeros((2, 1024, 2816), np.float32)
    w1a[:, :, :2730] = w1[:, :, :2730]
    w1g[:, :, :2730] = w1[:, :, 2730:]
    d["l0f_w1a"] = w1a
    d["l0f_w1g"] = w1g
    w2 = np.asarray(inp["L0_ffn_w2"], np.float32)
    w2p = np.zeros((2, 3072, 1024), np.float32)
    w2p[:, :2730, :] = w2
    d["l0f_w2"] = w2p
    b1 = np.asarray(inp["L0_ffn_b1"], np.float32)
    b1a = np.zeros((2, 2816), np.float32); b1a[:, :2730] = b1[:, :2730]
    b1g = np.zeros((2, 2816), np.float32); b1g[:, :2730] = b1[:, 2730:]
    d["l0f_b1a"] = np.concatenate(
        [_pcols(b1a[0]), _pcols(b1a[1])], axis=1)
    d["l0f_b1g"] = np.concatenate(
        [_pcols(b1g[0]), _pcols(b1g[1])], axis=1)
    d["l0f_b2"] = np.asarray(inp["L0_ffn_b2"], np.float32).reshape(1, 2048)
    lg = np.asarray(inp["L0_ffn_ln_g"], np.float32)
    lb = np.asarray(inp["L0_ffn_ln_b"], np.float32)
    d["l0f_lgb"] = np.concatenate(
        [_pcols(lg[0]), _pcols(lg[1]), _pcols(lb[0]), _pcols(lb[1])], axis=1)
    qs = np.asarray(inp["L0_ffn_q_scale"], np.float32)
    ks = np.asarray(inp["L0_ffn_k_scale"], np.float32)
    d["l0f_qks"] = np.ascontiguousarray(np.stack(
        [np.tile(ks[0], 2), np.tile(ks[1], 2),
         np.tile(qs[0], 2), np.tile(qs[1], 2)], axis=1))
    DH, WIN, NSEQ = 64, 512, 2048
    half = np.arange(0, DH, 2, dtype=np.float64)
    inv_freq = 1.0 / 10000.0 ** (half / DH)
    tpos = np.arange(NSEQ, dtype=np.float64)
    fr = tpos[:, None] * inv_freq[None, :]
    fr = np.concatenate([fr, fr], axis=-1)
    base = (half + 0.4 * DH) / (1.4 * DH)
    power = (tpos - NSEQ // 2) / float(WIN // 2)
    scale = base[None, :] ** power[:, None]
    scale = np.concatenate([scale, scale], axis=-1)
    cos, sin = np.cos(fr), np.sin(fr)
    qc = (cos * scale).astype(np.float32).T
    qsn = (sin * scale).astype(np.float32).T
    kc = (cos / scale).astype(np.float32).T
    ksn = (sin / scale).astype(np.float32).T
    d["rotqc"] = np.ascontiguousarray(np.concatenate([qc, qc], axis=0))
    d["rotqs"] = np.ascontiguousarray(np.concatenate([qsn, qsn], axis=0))
    d["rotkc"] = np.ascontiguousarray(np.concatenate([kc, kc], axis=0))
    d["rotks"] = np.ascontiguousarray(np.concatenate([ksn, ksn], axis=0))
    def _chpc(w, taps):
        w = np.asarray(w, np.float32).reshape(1024, taps)
        return np.ascontiguousarray(
            w.reshape(8, 128, taps).transpose(1, 0, 2).reshape(128, 8 * taps))
    d["l2a_wq"] = _t(inp["L2_attn_Wq"])
    d["l2a_wk"] = _t(inp["L2_attn_Wk"])
    d["l2a_wv"] = _t(inp["L2_attn_Wv"])
    d["l2a_wb"] = _t(inp["L2_attn_Wb"])
    d["l2a_cq"] = _chpc(inp["L2_attn_conv_q_w"], 4)
    d["l2a_ck"] = _chpc(inp["L2_attn_conv_k_w"], 4)
    d["l2a_cv"] = _chpc(inp["L2_attn_conv_v_w"], 4)
    d["l2a_firs"] = _chpc(inp["L2_attn_fir_short_filt"], 5)
    d["l2a_firl"] = _chpc(inp["L2_attn_fir_long_filt"], 64)
    w1t = _t(inp["L2_attn_gate_W1"])
    d["l2a_w1hs"] = np.ascontiguousarray(w1t[0:1024])
    d["l2a_w1st"] = np.ascontiguousarray(w1t[1024:1040])
    d["l2a_w2t"] = _t(inp["L2_attn_gate_W2"])
    d["l2a_b1pc"] = _pcols(inp["L2_attn_gate_b1"])
    d["l2a_ltemp4"] = np.ascontiguousarray(np.broadcast_to(
        np.asarray(inp["L2_attn_gate_log_temp"], np.float32)[None, :], (4, 4)))
    d["l2a_cb4"] = np.ascontiguousarray(np.broadcast_to(
        np.asarray(inp["L2_attn_gate_copy_bias"], np.float32)[None, :], (4, 4)))
    d["l2a_b24"] = np.asarray(inp["L2_attn_gate_b2"], np.float32).reshape(4, 1)
    d["l2a_onw"] = np.ascontiguousarray(np.broadcast_to(
        np.tile(np.asarray(inp["L2_attn_o_norm_w"], np.float32), 4)[None, :],
        (128, 1024)))
    d["l2a_wo"] = _t(inp["L2_attn_Wo"])
    d["l1f_w1"] = _t(inp["L1_ffn_w1_w"])
    d["l1f_w2"] = _t(inp["L1_ffn_w2_w"])
    d["l1f_h1"] = _t(inp["L1_ffn_h1_w"])
    d["l1f_h2"] = _t(inp["L1_ffn_h2_w"])
    d["l1f_w3"] = _t(inp["L1_ffn_w3_w"])
    d["l1f_s1"] = _pcols(inp["L1_ffn_w1_s"])
    d["l1f_s2"] = _pcols(inp["L1_ffn_w2_s"])
    d["l1f_sh1"] = _pcols(inp["L1_ffn_h1_s"])
    d["l1f_sh2"] = _pcols(inp["L1_ffn_h2_s"])
    d["l1f_s3b"] = np.ascontiguousarray(np.broadcast_to(
        np.asarray(inp["L1_ffn_w3_s"], dtype=np.float32)[None, :], (128, 1024)))
    d["l3f_proj"] = _t(inp["L3_ffn_proj_w"])
    d["l3f_anchT"] = _t(inp["L3_ffn_anchors"])
    d["l3f_temp"] = np.ascontiguousarray(np.broadcast_to(
        np.asarray(inp["L3_ffn_routing_temp"], dtype=np.float32).reshape(1, 1),
        (128, 1)))
    d["l3f_w1"] = np.ascontiguousarray(np.asarray(inp["L3_ffn_w1"]))
    d["l3f_w2"] = np.ascontiguousarray(np.asarray(inp["L3_ffn_w2"]))
    d["l2f_wg"] = _t(inp["L2_ffn_w_gate"])
    d["l2f_wu"] = _t(inp["L2_ffn_w_up"])
    d["l2f_wd"] = _t(inp["L2_ffn_w_down"])
    d["l3_wq"] = np.asarray(inp["L3_attn_Wq"])
    d["l3_wk"] = np.asarray(inp["L3_attn_Wk"])
    d["l3_wv"] = np.asarray(inp["L3_attn_Wv"])
    d["l3_wo"] = np.ascontiguousarray(np.asarray(inp["L3_attn_Wo"]))
    for bn, k in (("l3_bq", "L3_attn_bq"), ("l3_bk", "L3_attn_bk"),
                  ("l3_bv", "L3_attn_bv"), ("l3_bo", "L3_attn_bo")):
        d[bn] = np.asarray(inp[k]).reshape(1, 1024)
    for nm, (kt, cg) in _RP_GRIDS.items():
        w = d[nm]
        if w.ndim == 2:
            d[nm] = _rp(w, kt, cg)
        else:
            d[nm] = np.ascontiguousarray(
                np.stack([_rp(w[e], kt, cg) for e in range(w.shape[0])]))
    return d


_RP_GRIDS = {
    "l0a_wq": (8, 128), "l0a_wk": (8, 128), "l0a_wv": (8, 256),
    "l0a_wo": (8, 256),
    "l1_wq": (8, 128), "l1_wk": (8, 128), "l1_wv": (8, 256), "l1_wo": (8, 256),
    "l3_wq": (8, 128), "l3_wk": (8, 128), "l3_wv": (8, 256), "l3_wo": (8, 256),
    "l2f_wg": (8, 256), "l2f_wu": (8, 256), "l2f_wd": (8, 256),
    "l1f_w1": (8, 256), "l1f_w2": (8, 256),
    "l1f_h1": (16, 128), "l1f_h2": (16, 128), "l1f_w3": (16, 128),
    "l3f_w1": (8, 128), "l3f_w2": (4, 256), "l3f_proj": (8, 6),
    "l0f_wqkv": (8, 256), "l0f_wo": (4, 256),
    "l0f_w1a": (8, 128), "l0f_w1g": (8, 128), "l0f_w2": (8, 256),
    "l2a_wq": (8, 256), "l2a_wk": (8, 256), "l2a_wv": (8, 256),
    "l2a_wb": (8, 4), "l2a_w1hs": (8, 128), "l2a_wo": (8, 256),
    "l2a_w2t": (8, 4),
}


def _rp(w, kt, cg):
    K, N = w.shape
    nA, nC = K // (128 * kt), N // cg
    assert nA * kt * 128 == K and nC * cg == N, (w.shape, kt, cg)
    r = w.reshape(nA, kt, 128, nC, cg).transpose(0, 3, 2, 1, 4)
    return np.ascontiguousarray(r)


def run_blocks(block_list, x, weights, consts):
    dram_specs = [("x", (T, D), F32, "ExternalInput"),
                  ("y", (T, D), F32, "ExternalOutput")]
    need = set()
    for b in block_list:
        import os as _os
        _sa = _os.environ.get("MHA_STOP") or None
        if b == "l0a":
            need |= {"l0a_wq", "l0a_wk", "l0a_wv", "l0a_wo"}
        elif b == "l1a":
            need |= {"l1_wq", "l1_wk", "l1_wv", "l1_wo"}
        elif b == "l3a":
            need |= {"l3_wq", "l3_wk", "l3_wv", "l3_wo",
                     "l3_bq", "l3_bk", "l3_bv", "l3_bo"}
        elif b == "l2f":
            need |= {"l2f_wg", "l2f_wu", "l2f_wd"}
        elif b == "l1f":
            need |= {"l1f_w1", "l1f_w2", "l1f_h1", "l1f_h2", "l1f_w3",
                     "l1f_s1", "l1f_s2", "l1f_sh1", "l1f_sh2", "l1f_s3b"}
        elif b == "l3f":
            need |= {"l3f_proj", "l3f_anchT", "l3f_temp", "l3f_w1", "l3f_w2"}
        elif b == "l0f":
            need |= {"l0f_wqkv", "l0f_wo", "l0f_w1a", "l0f_w1g", "l0f_w2",
                     "l0f_b1a", "l0f_b1g", "l0f_b2", "l0f_lgb", "l0f_qks",
                     "rotqc", "rotqs", "rotkc", "rotks"}
        elif b == "l2a":
            need |= {"l2a_wq", "l2a_wk", "l2a_wv", "l2a_wb", "l2a_cq", "l2a_ck",
                     "l2a_cv", "l2a_firs", "l2a_firl", "l2a_w1hs", "l2a_w1st",
                     "l2a_w2t", "l2a_b1pc", "l2a_ltemp4", "l2a_cb4", "l2a_b24",
                     "l2a_onw", "l2a_wo"}
    for nm in sorted(need):
        dt = I32 if weights[nm].dtype == np.int32 else F32
        dram_specs.append((nm, weights[nm].shape, dt, "ExternalInput"))
    for nm in ("eye", "ones_row", "maskA", "maskB", "maskWA", "maskWB",
               "hsel", "hselT", "onescol", "masks32", "maskD41"):
        dram_specs.append((nm, consts[nm].shape, F32, "ExternalInput"))
    nc, ctx = build_program(block_list, dram_specs)
    in_maps = []
    for b in range(x.shape[0]):
        m = {"x": np.ascontiguousarray(x[b])}
        for nm in sorted(need):
            m[nm] = weights[nm]
        for nm in ("eye", "ones_row", "maskA", "maskB", "maskWA", "maskWB",
                   "hsel", "hselT", "onescol", "masks32", "maskD41"):
            m[nm] = consts[nm]
        in_maps.append(m)
    res = run_bass_kernel_spmd(nc, in_maps, core_ids=list(range(x.shape[0])))
    ctx.close()
    import os as _os
    if _os.environ.get("L3F_DEBUG") and "l3f_dbg" in res.results[0]:
        np.save("/tmp/l3f_dbg.npy", res.results[0]["l3f_dbg"])
    if "l2a_dbgc" in res.results[0]:
        np.save("/tmp/l2a_dbgc.npy", res.results[0]["l2a_dbgc"].astype(np.float32))
    if "l2a_dbgc2" in res.results[0]:
        np.save("/tmp/l2a_dbgc2.npy", res.results[0]["l2a_dbgc2"].astype(np.float32))
    if "l3f_dbg3" in res.results[0]:
        np.save("/tmp/l3f_dbg3.npy", res.results[0]["l3f_dbg3"])
    if "l3f_dbg2" in res.results[0]:
        np.save("/tmp/l3f_dbg2.npy", res.results[0]["l3f_dbg2"].astype(np.float32))
    return np.stack([r["y"] for r in res.results], axis=0)


ALL_BLOCKS = ["l0a", "l0f", "l1a", "l1f", "l2a", "l2f", "l3a", "l3f"]


def kernel(**inputs):
    x = np.ascontiguousarray(np.asarray(inputs["L0_attn_x"], dtype=np.float32))
    weights = prep_weights(inputs)
    consts = host_consts()
    groups = [["l0a", "l0f", "l1a", "l1f", "l2a", "l2f", "l3a", "l3f"]]
    for group in groups:
        x = run_blocks(group, x, weights, consts)
    return x
